# version 21
# baseline (speedup 1.0000x reference)
import math
import numpy as np
import ml_dtypes
import concourse.bass as bass
import concourse.mybir as mybir
from concourse.bass_utils import run_bass_kernel_spmd

F32 = mybir.dt.float32
BF16 = mybir.dt.bfloat16
FP8 = mybir.dt.float8e5
I32 = mybir.dt.int32
AF = mybir.ActivationFunctionType
ALU = mybir.AluOpType

D = 1024
T = 2048
NCORES = 8
FFTN = 4096
FT = 17
LT = 16

_DEBUG_TAPS = []
_PHASES = ("l0_attn", "l0_ffn", "l1_attn", "l1_ffn")
_NLAYERS = 4
_LAYER_LIST = None


_STREAMS = ("pe", "dve", "act", "pool", "sp")
_DMA_QUEUES = ("sp", "pool", "act")
_NSLOT = 8


class _Op:
    pass


class PB:
    serialize_all = False

    def __init__(self):
        self.ops = []
        self._prev = None
        self.last_writer = {}
        self.readers = {}
        self.qops = {q: [] for q in _DMA_QUEUES}

    def emit(self, stream, fn, reads=(), writes=(), dma=False, tag=""):
        op = _Op()
        op.stream, op.fn, op.is_dma, op.tag = stream, fn, dma, tag
        deps = []
        for k in reads:
            w = self.last_writer.get(k)
            if w is not None:
                deps.append(w)
        for k in writes:
            w = self.last_writer.get(k)
            if w is not None:
                deps.append(w)
            deps.extend(self.readers.get(k, ()))
        if dma:
            assert stream in _DMA_QUEUES, stream
            k = len(self.qops[stream])
            if k >= _NSLOT:
                deps.append(self.qops[stream][k - _NSLOT])
            self.qops[stream].append(op)
        if PB.serialize_all and self._prev is not None:
            deps.append(self._prev)
        self._prev = op
        op.deps = [d for d in dict.fromkeys(deps) if d is not op]
        for k in reads:
            self.readers.setdefault(k, []).append(op)
        for k in writes:
            self.last_writer[k] = op
            self.readers[k] = []
        self.ops.append(op)
        return op

    def finalize(self):
        ctrs = list(_STREAMS) + [("q", q, s) for q in _DMA_QUEUES
                                 for s in range(_NSLOT)]
        self.ctrs = ctrs
        self.cidx = {c: i for i, c in enumerate(ctrs)}
        NC = len(ctrs)
        qseq = {q: 0 for q in _DMA_QUEUES}
        seqs = {s: 0 for s in _STREAMS}
        for op in self.ops:
            if op.is_dma:
                k = qseq[op.stream]
                qseq[op.stream] = k + 1
                op.ctr = self.cidx[("q", op.stream, k % _NSLOT)]
                op.cval = (k // _NSLOT + 1) * 16
            else:
                seqs[op.stream] += 1
                op.ctr = self.cidx[op.stream]
                op.cval = seqs[op.stream]
        cur_done = {s: [0] * NC for s in _STREAMS}
        stream_retire = {s: [0] * NC for s in _STREAMS}
        waited = [set() for _ in range(NC)]
        for op in self.ops:
            s = op.stream
            vc = list(cur_done[s])
            op.waits = []
            for d in op.deps:
                if vc[d.ctr] < d.cval:
                    op.waits.append(d)
                    waited[d.ctr].add(d.cval)
                    dc = d.inherit
                    for j in range(NC):
                        if dc[j] > vc[j]:
                            vc[j] = dc[j]
                    if d.cval > vc[d.ctr]:
                        vc[d.ctr] = d.cval
            cur_done[s] = vc
            if op.is_dma:
                inh = list(vc)
                if op.cval > inh[op.ctr]:
                    inh[op.ctr] = op.cval
                op.inherit = inh
            else:
                inh = list(stream_retire[s])
                for j in range(NC):
                    if vc[j] > inh[j]:
                        inh[j] = vc[j]
                if op.cval > inh[op.ctr]:
                    inh[op.ctr] = op.cval
                op.inherit = inh
                stream_retire[s] = inh
        self.rank = {}
        for s in _STREAMS:
            j = self.cidx[s]
            self.rank[j] = {v: k + 1 for k, v in enumerate(sorted(waited[j]))}
        for op in self.ops:
            op.inc = op.is_dma or (op.cval in waited[op.ctr])

    def wait_val(self, d):
        return d.cval if d.is_dma else self.rank[d.ctr][d.cval]

    def inc_val(self, op):
        return 16 if op.is_dma else 1


def _keep(nc, cm):
    obj = cm.__enter__()
    nc._open_cms = getattr(nc, "_open_cms", [])
    nc._open_cms.append(cm)
    return obj



def _host_constants():
    c = {}
    l = np.arange(T, dtype=np.float64)
    f = np.arange(FT * 128, dtype=np.float64)
    ang = np.outer(l, f) * (2 * np.pi / FFTN)
    c["dftc"] = np.ascontiguousarray(
        np.cos(ang).reshape(LT, 128, FT, 128).transpose(2, 1, 0, 3)).astype(
        ml_dtypes.bfloat16)
    c["dfts"] = np.ascontiguousarray(
        (-np.sin(ang)).reshape(LT, 128, FT, 128).transpose(2, 1, 0, 3)).astype(
        ml_dtypes.bfloat16)
    wf = np.zeros(FT * 128)
    wf[0] = 1.0
    wf[1:2048] = 2.0
    wf[2048] = 1.0
    ang2 = np.outer(f, l) * (2 * np.pi / FFTN)
    c["idftc"] = np.ascontiguousarray(
        ((wf[:, None] * np.cos(ang2)) / FFTN).reshape(FT, 128, LT, 128)
        .transpose(2, 1, 0, 3)).astype(ml_dtypes.bfloat16)
    c["idfts"] = np.ascontiguousarray(
        ((-wf[:, None] * np.sin(ang2)) / FFTN).reshape(FT, 128, LT, 128)
        .transpose(2, 1, 0, 3)).astype(ml_dtypes.bfloat16)
    c["ident_bf"] = np.eye(128, dtype=ml_dtypes.bfloat16)
    c["ident_f32"] = np.eye(128, dtype=np.float32)
    tri = (np.arange(128)[None, :] >= np.arange(128)[:, None])
    c["triu_bf"] = tri.astype(ml_dtypes.bfloat16)
    c["ones_col_bf"] = np.ones((128, 1), dtype=ml_dtypes.bfloat16)
    tri = (np.arange(128)[None, :] >= np.arange(128)[:, None]).astype(np.float32)
    sh = np.zeros((64, 128), dtype=np.float32)
    sh[np.arange(64), 64 + np.arange(64)] = 1.0
    c["shift64_bf"] = sh.astype(ml_dtypes.bfloat16)
    sm_ = np.zeros((4, 128, 512), dtype=np.float32)
    for v in range(4):
        sm_[v, :, v * 128:(v + 1) * 128] = tri
        sm_[v, :, (v + 1) * 128:] = 1.0
    c["spanmask"] = np.ascontiguousarray(sm_.transpose(1, 0, 2)).astype(
        ml_dtypes.bfloat16)
    c["ones_col_f32"] = np.ones((128, 1), dtype=np.float32)
    c["ones_row_f32"] = np.ones((1, 128), dtype=np.float32)
    return c


def _host_prep(inputs):
    x = np.asarray(inputs["L0_attn_x"])
    outd = {}

    def cg(a):
        a = np.asarray(a)
        return np.ascontiguousarray(
            a.reshape(4, 4, 256, 32).transpose(0, 1, 3, 2).reshape(4, 128, 256)
        ).astype(np.float32)

    outd["are_cg"] = cg(inputs["L0_attn_A_re_log"])
    outd["aim_cg"] = cg(inputs["L0_attn_A_im"])
    outd["cre_cg"] = cg(inputs["L0_attn_C_re"])
    outd["cim_cg"] = cg(inputs["L0_attn_C_im"])
    logdt = np.asarray(inputs["L0_attn_log_dt"])
    outd["logdt_cg"] = np.ascontiguousarray(
        np.repeat(logdt.reshape(4, 4, 1, 256), 32, axis=2).reshape(4, 128, 256)
    ).astype(np.float32)
    outd["dskip_cg"] = np.ascontiguousarray(
        np.asarray(inputs["L0_attn_D_skip"]).reshape(4, 4, 256)).astype(np.float32)
    outd["ln_w"] = np.ascontiguousarray(
        np.asarray(inputs["L0_attn_ln_w"]).reshape(4, 8, 128)
        .transpose(0, 2, 1)).astype(np.float32)
    outd["ln_b"] = np.ascontiguousarray(
        np.asarray(inputs["L0_attn_ln_b"]).reshape(4, 8, 128)
        .transpose(0, 2, 1)).astype(np.float32)
    wot = np.asarray(inputs["L0_attn_W_out"]).transpose(0, 2, 1)
    outd["w_out_t"] = np.ascontiguousarray(
        wot.reshape(4, 8, 128, 16, 128).transpose(0, 3, 2, 1, 4)).astype(
        np.float32)
    outd["b_out"] = np.ascontiguousarray(
        np.asarray(inputs["L0_attn_b_out"]).reshape(4, 16, 128)
        .transpose(0, 2, 1)).astype(np.float32)
    w1t = np.asarray(inputs["L0_ffn_W1"]).T
    outd["w1_t"] = np.ascontiguousarray(
        w1t.reshape(8, 128, 32, 128).transpose(2, 1, 0, 3)).astype(np.float32)
    outd["b1"] = np.ascontiguousarray(
        np.asarray(inputs["L0_ffn_b1"]).reshape(32, 128).T).astype(np.float32)
    w2t = np.asarray(inputs["L0_ffn_W2"]).T
    outd["w2_t"] = np.ascontiguousarray(
        w2t.reshape(2, 16, 128, 8, 128).transpose(3, 0, 2, 1, 4)).astype(
        np.float32)
    outd["b2"] = np.ascontiguousarray(
        np.asarray(inputs["L0_ffn_b2"]).reshape(8, 128).T).astype(np.float32)
    for k in ("L1_attn_Wq", "L1_attn_Wk", "L1_attn_Wv",
              "L1_ffn_Wq", "L1_ffn_Wk", "L1_ffn_Wv"):
        w = np.asarray(inputs[k])
        outd[k.lower()] = np.ascontiguousarray(
            w.reshape(8, 128, 8, 128).transpose(2, 1, 0, 3)).astype(np.float32)
    for k in ("L1_attn_Wv", "L1_attn_Wk", "L1_ffn_Wv"):
        w = np.asarray(inputs[k])
        outd[k.lower() + "2"] = np.ascontiguousarray(
            w.reshape(8, 128, 2, 512).transpose(2, 1, 0, 3)).astype(
            np.float32)
    for k in ("L1_attn_Wo", "L1_ffn_Wo"):
        w = np.asarray(inputs[k])
        outd[k.lower()] = np.ascontiguousarray(
            w.reshape(2, 4, 128, 8, 128).transpose(0, 3, 2, 1, 4)).astype(
            np.float32)
    outd["l1_attn_bo"] = np.ascontiguousarray(
        np.asarray(inputs["L1_attn_bo"]).reshape(8, 128).T).astype(np.float32)
    outd["l1_ffn_wg"] = np.ascontiguousarray(
        np.asarray(inputs["L1_ffn_Wg"]).reshape(8, 128).T).astype(np.float32)
    import jax
    with jax.default_device(jax.devices("cpu")[0]):
        keep = np.asarray(jax.random.bernoulli(
            jax.random.key(42, impl="threefry2x32"), 0.8, (8, T, D)))
    maskv = keep.astype(np.float32) * 1.25
    consts = _host_constants()
    per_core = []
    for ci in range(NCORES):
        d = dict(outd)
        d.update(consts)
        d["x_fm"] = np.ascontiguousarray(x[ci].T).astype(np.float32)
        d["dropmask_fm"] = np.ascontiguousarray(maskv[ci].T).astype(
            ml_dtypes.float8_e5m2)
        per_core.append(d)
    return per_core



class Ctx:
    pass


def build_program():
    nc = bass.Bass("TRN2", target_bir_lowering=False, debug=False)
    pb = PB()
    cx = Ctx()
    cx.nc, cx.pb = nc, pb

    dr = {}

    def din(name, shape, dtype=F32):
        dr[name] = nc.dram_tensor(name, list(shape), dtype,
                                  kind="ExternalInput").ap()

    din("x_fm", (D, T))
    for nm, shp in [("are_cg", (4, 128, 256)), ("aim_cg", (4, 128, 256)),
                    ("cre_cg", (4, 128, 256)), ("cim_cg", (4, 128, 256)),
                    ("logdt_cg", (4, 128, 256)), ("dskip_cg", (4, 4, 256)),
                    ("ln_w", (4, 128, 8)), ("ln_b", (4, 128, 8)),
                    ("w_out_t", (4, 16, 128, 8, 128)), ("b_out", (4, 128, 16)),
                    ("w1_t", (32, 128, 8, 128)), ("b1", (128, 32)),
                    ("w2_t", (8, 2, 128, 16, 128)), ("b2", (128, 8)),
                    ("l1_attn_wq", (8, 128, 8, 128)),
                    ("l1_attn_wk", (8, 128, 8, 128)),
                    ("l1_attn_wv", (8, 128, 8, 128)),
                    ("l1_attn_wo", (2, 8, 128, 4, 128)),
                    ("l1_attn_bo", (128, 8)), ("l1_ffn_wg", (128, 8)),
                    ("l1_ffn_wq", (8, 128, 8, 128)),
                    ("l1_ffn_wk", (8, 128, 8, 128)),
                    ("l1_ffn_wv", (8, 128, 8, 128)),
                    ("l1_ffn_wo", (2, 8, 128, 4, 128)),
                    ("l1_attn_wv2", (2, 128, 8, 512)),
                    ("l1_attn_wk2", (2, 128, 8, 512)),
                    ("l1_ffn_wv2", (2, 128, 8, 512))]:
        din(nm, shp)
    for nm, shp in [("dftc", (FT, 128, LT, 128)), ("dfts", (FT, 128, LT, 128)),
                    ("idftc", (LT, 128, FT, 128)), ("idfts", (LT, 128, FT, 128)),
                    ("ident_bf", (128, 128)), ("triu_bf", (128, 128)),
                    ("ones_col_bf", (128, 1)),
                    ("spanmask", (128, 4, 512)),
                    ("shift64_bf", (64, 128))]:
        din(nm, shp, BF16)
    din("ident_f32", (128, 128))
    din("ones_col_f32", (128, 1))
    din("ones_row_f32", (1, 128))
    din("dropmask_fm", (D, T), FP8)
    cx.dr = dr
    cx.out = nc.dram_tensor("out_fm", [D, T], F32, kind="ExternalOutput").ap()
    cx.h_dram = nc.dram_tensor("h_scr", [D, T], F32, kind="Internal").ap()
    cx.x_scr = nc.dram_tensor("x_scr", [D, T], F32, kind="Internal").ap()
    cx.k_scr = nc.dram_tensor("k_scr", [T, D], BF16, kind="Internal").ap()

    cx.AR = _keep(nc, nc.sbuf_tensor("AR", [128, 97792], BF16))
    cx.rw = _keep(nc, nc.sbuf_tensor("rw", [128, 2048], F32))
    cx.cols = _keep(nc, nc.sbuf_tensor("cols", [128, 96], F32))
    cx.cbf = _keep(nc, nc.sbuf_tensor("cbf", [128, 512], BF16))
    cx.cf32 = _keep(nc, nc.sbuf_tensor("cf32", [128, 288], F32))
    cx.ps = [_keep(nc, nc.psum_tensor("ps%d" % i, [128, 512], F32))
             for i in range(8)]

    def av(byte_off, cols, dtype=BF16, npart=128, p0=0):
        esz = mybir.dt.size(dtype)
        assert byte_off % 2 == 0 and (byte_off + cols * esz) <= 97792 * 2
        v = cx.AR[p0:p0 + npart, byte_off // 2: (byte_off + cols * esz) // 2]
        if dtype != BF16:
            v = v.bitcast(dtype)
        return v

    cx.av = av
    cx.w0 = av(135168, 2048, F32)
    cx.w1 = av(143360, 2048, F32)
    cx.w2 = av(151552, 2048, F32)
    cx.sm = av(159744, 2560, F32)
    cx.wst = av(169984, 4096, BF16)
    cx.dst = av(178176, 8704, BF16)
    cx.w0b = av(135168, 4096, BF16)
    cx.w1b = av(143360, 4096, BF16)
    cx.w2b = av(151552, 4096, BF16)

    cx.shift64_bf = cx.cbf[0:64, 260:388]
    cx.ident_bf = cx.cbf[:, 0:128]
    cx.triu_bf = cx.cbf[:, 128:256]
    cx.ones_col_bf = cx.cbf[:, 256:257]
    cx.ident_f32 = cx.cf32[:, 0:128]
    cx.ones_col_f32 = cx.cf32[:, 128:129]
    cx.ones_row_f32 = cx.cf32[0:1, 130:258]

    _build_body(cx)
    pb.finalize()

    sems = {}
    for i, c in enumerate(pb.ctrs):
        name = c if isinstance(c, str) else "%s_%d" % (c[1], c[2])
        sems[i] = _keep(nc, nc.semaphore("S_" + name))

    with nc.Block() as block:
        def runner(s):
            def go(eng):
                for op in pb.ops:
                    if op.stream != s:
                        continue
                    for d in op.waits:
                        eng.wait_ge(sems[d.ctr], pb.wait_val(d))
                    inst = op.fn(eng)
                    if op.inc:
                        assert inst is not None, "inc on None op: " + op.tag
                        inst.then_inc(sems[op.ctr], pb.inc_val(op))
            return go
        block.tensor(runner("pe"))
        block.vector(runner("dve"))
        block.scalar(runner("act"))
        block.gpsimd(runner("pool"))
        block.sync(runner("sp"))
    return nc, cx


def _tap(cx, name, src_ap, shape, dtype, read_keys):
    if name not in _DEBUG_TAPS:
        return
    t = cx.nc.dram_tensor("tap_" + name, list(shape), dtype,
                          kind="ExternalOutput").ap()
    cx.pb.emit("pool", lambda e: e.dma_start(out=t, in_=src_ap),
               reads=list(read_keys), writes=[("tap", name)], dma=True,
               tag="tap_" + name)



def _build_body(cx):
    pb, dr = cx.pb, cx.dr
    pb.emit("sp", lambda e: e.dma_start(out=cx.shift64_bf,
                                         in_=dr["shift64_bf"]),
            writes=[("c", "sh64")], dma=True, tag="ldc_sh64")
    for src, dst, key in [("ident_bf", cx.ident_bf, "idb"),
                          ("triu_bf", cx.triu_bf, "tri"),
                          ("ones_col_bf", cx.ones_col_bf, "ocb"),
                          ("ident_f32", cx.ident_f32, "idf"),
                          ("ones_col_f32", cx.ones_col_f32, "ocf"),
                          ("ones_row_f32", cx.ones_row_f32, "orf")]:
        pb.emit("sp", lambda e, s=src, d=dst: e.dma_start(out=d, in_=dr[s]),
                writes=[("c", key)], dma=True, tag="ldc_" + key)
    for t in range(8):
        pb.emit("pool", lambda e, t=t: e.dma_start(
            out=cx.x_scr[t * 128:(t + 1) * 128, :],
            in_=dr["x_fm"][t * 128:(t + 1) * 128, :]),
            writes=[("xscr", t)], dma=True, tag="xinit")

    if "l0_attn" in _PHASES:
        _phase_l0_attn(cx)
    if "l0_ffn" in _PHASES:
        _phase_l0_ffn(cx)
    if "l1_attn" in _PHASES:
        _phase_l1_attn(cx)
    if "l1_ffn" in _PHASES:
        _phase_l1_ffn(cx)

    for t in range(8):
        pb.emit("sp", lambda e, t=t: e.dma_start(
            out=cx.out[t * 128:(t + 1) * 128, :],
            in_=cx.x_scr[t * 128:(t + 1) * 128, :]),
            reads=[("xscr", t)], writes=[("outhbm", t)], dma=True, tag="st_out")
    pb.emit("dve", lambda e: e.engine_nop(),
            reads=[("outhbm", t) for t in range(8)], tag="final")



def _ln_full(cx, src_dram, src_keys, tag, mb, rb, mbk, rbk):
    pb, ps = cx.pb, cx.ps
    sq = cx.av(178176, 2048, F32)
    for t in range(8):
        buf = cx.w0 if t % 2 == 0 else cx.w1
        rkey = ("w0",) if t % 2 == 0 else ("w1",)
        bkey = ("lnbuf", tag, t % 2)
        pb.emit("sp" if t % 2 == 0 else "pool",
                lambda e, t=t, buf=buf: e.dma_start(
                    out=buf, in_=src_dram[t * 128:(t + 1) * 128, :]),
                reads=[src_keys(t)], writes=[bkey, rkey], dma=True,
                tag="lnld" + tag)
        pb.emit("act", lambda e, buf=buf: e.activation(sq, buf, AF.Square),
                reads=[bkey, rkey], writes=[("lnsq", tag), ("dstream", 0)],
                tag="lnsq")
        for s in range(4):
            pb.emit("pe", lambda e, t=t, s=s, buf=buf: e.matmul(
                ps[s][0:1, 0:512], cx.ones_col_f32,
                buf[:, s * 512:(s + 1) * 512], start=(t == 0), stop=(t == 7)),
                reads=[bkey, rkey, ("c", "ocf")], writes=[("ps", s)],
                tag="lns")
            pb.emit("pe", lambda e, t=t, s=s: e.matmul(
                ps[4 + s][0:1, 0:512], cx.ones_col_f32,
                sq[:, s * 512:(s + 1) * 512], start=(t == 0), stop=(t == 7)),
                reads=[("lnsq", tag), ("dstream", 0), ("c", "ocf")],
                writes=[("ps", 4 + s)], tag="lns2")
    _ln_finish(cx, tag, mb, rb, mbk, rbk)


def _ln_finish(cx, tag, mb, rb, mbk, rbk):
    pb, ps = cx.pb, cx.ps
    mrow = mb[0:1, :]
    vrow = rb[0:1, :]
    for s in range(4):
        sl = slice(s * 512, (s + 1) * 512)
        pb.emit("dve", lambda e, sl=sl, s=s: e.tensor_scalar_mul(
            mrow[:, sl], ps[s][0:1, 0:512], 1.0 / D),
            reads=[("ps", s)], writes=[("r_mean", tag)] + mbk, tag="mean")
        pb.emit("dve", lambda e, sl=sl, s=s: e.tensor_scalar_mul(
            vrow[:, sl], ps[4 + s][0:1, 0:512], 1.0 / D),
            reads=[("ps", 4 + s)], writes=[("r_var", tag)] + rbk, tag="ex2")
        pb.emit("dve", lambda e, sl=sl, s=s: e.tensor_tensor(
            ps[s][0:1, 0:512], mrow[:, sl], mrow[:, sl], ALU.mult),
            reads=[("r_mean", tag)] + mbk, writes=[("ps", s)], tag="msq")
        pb.emit("dve", lambda e, sl=sl, s=s: e.tensor_tensor(
            vrow[:, sl], vrow[:, sl], ps[s][0:1, 0:512], ALU.subtract),
            reads=[("r_var", tag), ("ps", s)] + rbk,
            writes=[("r_var", tag)] + rbk, tag="vsub")
    pb.emit("dve", lambda e: e.tensor_scalar_add(vrow, vrow, 1e-5),
            reads=[("r_var", tag)] + rbk, writes=[("r_var", tag)] + rbk,
            tag="eps")
    pb.emit("act", lambda e: e.activation(vrow, vrow, AF.Ln),
            reads=[("r_var", tag)] + rbk, writes=[("r_var", tag)] + rbk,
            tag="lnv")
    pb.emit("act", lambda e: e.activation(vrow, vrow, AF.Exp, scale=-0.5),
            reads=[("r_var", tag)] + rbk, writes=[("r_rstd", tag)] + rbk,
            tag="rstd")
    for s in range(4):
        sl = slice(s * 512, (s + 1) * 512)
        pb.emit("pe", lambda e, sl=sl: e.matmul(
            ps[6][:, 0:512], cx.ones_row_f32, mrow[:, sl],
            start=True, stop=True),
            reads=[("r_mean", tag), ("c", "orf")] + mbk, writes=[("ps", 6)],
            tag="bcm")
        pb.emit("dve", lambda e, sl=sl: e.tensor_copy(mb[:, sl],
                                                      ps[6][:, 0:512]),
            reads=[("ps", 6)], writes=[("mb", tag)] + mbk, tag="bcmc")
        pb.emit("pe", lambda e, sl=sl: e.matmul(
            ps[7][:, 0:512], cx.ones_row_f32, vrow[:, sl],
            start=True, stop=True),
            reads=[("r_rstd", tag), ("c", "orf")] + rbk, writes=[("ps", 7)],
            tag="bcr")
        pb.emit("dve", lambda e, sl=sl: e.tensor_copy(rb[:, sl],
                                                      ps[7][:, 0:512]),
            reads=[("ps", 7)], writes=[("rb", tag)] + rbk, tag="bcrc")


def _l0_module_ln(cx):
    pb = cx.pb
    mb = cx.sm[:, 0:2048]
    rb = cx.w2
    _ln_full(cx, cx.x_scr, lambda t: ("xscr", t), "mln0", mb, rb,
             [("sm",)], [("w2",)])
    for t in range(8):
        buf = cx.w0 if t % 2 == 0 else cx.w1
        rkey = ("w0",) if t % 2 == 0 else ("w1",)
        bkey = ("mlnb", t % 2)
        pb.emit("sp", lambda e, t=t, buf=buf: e.dma_start(
            out=buf, in_=cx.x_scr[t * 128:(t + 1) * 128, :]),
            reads=[("xscr", t)], writes=[bkey, rkey], dma=True, tag="mlnld")
        eng = "dve" if t % 2 == 0 else "pool"
        pb.emit(eng, lambda e, buf=buf: e.tensor_tensor(
            buf, buf, mb, ALU.subtract),
            reads=[bkey, ("mb", "mln0"), ("sm",)], writes=[bkey, rkey],
            tag="mlnsub")
        pb.emit(eng, lambda e, buf=buf: e.tensor_tensor(
            buf, buf, rb, ALU.mult),
            reads=[bkey, ("rb", "mln0"), ("w2",)], writes=[bkey, rkey],
            tag="mlnmul")
        pb.emit("pool", lambda e, t=t, buf=buf: e.dma_start(
            out=cx.h_dram[t * 128:(t + 1) * 128, :], in_=buf),
            reads=[bkey, rkey], writes=[("hdram", t)], dma=True, tag="mlnst")


def _l0_zT(cx, i):
    pb, dr = cx.pb, cx.dr
    tag = "ch%d" % i
    mb = cx.sm[:, 0:2048]
    rb = cx.w2
    _ln_full(cx, cx.h_dram, lambda t: ("hdram", t), tag, mb, rb,
             [("sm",)], [("w2",)])
    pb.emit("sp", lambda e: e.dma_start(
        out=cx.cols[:, 0:8], in_=dr["ln_w"][i]),
        writes=[("lncol", "w"), ("cols",)], dma=True, tag="lnw")
    pb.emit("sp", lambda e: e.dma_start(
        out=cx.cols[:, 8:16], in_=dr["ln_b"][i]),
        writes=[("lncol", "b"), ("cols",)], dma=True, tag="lnb")
    zT = cx.av(0, 16384, BF16)
    for t in range(8):
        buf = cx.w0 if t % 2 == 0 else cx.w1
        rkey = ("w0",) if t % 2 == 0 else ("w1",)
        eng = "dve" if t % 2 == 0 else "pool"
        bkey = ("zbuf", i, t % 2)
        pb.emit("sp", lambda e, t=t, buf=buf: e.dma_start(
            out=buf, in_=cx.h_dram[t * 128:(t + 1) * 128, :]),
            reads=[("hdram", t)], writes=[bkey, rkey], dma=True, tag="zld")
        pb.emit(eng, lambda e, buf=buf: e.tensor_tensor(
            buf, buf, mb, ALU.subtract),
            reads=[bkey, ("mb", tag), ("sm",)], writes=[bkey, rkey],
            tag="zsub")
        pb.emit(eng, lambda e, buf=buf: e.tensor_tensor(
            buf, buf, rb, ALU.mult),
            reads=[bkey, ("rb", tag), ("w2",)], writes=[bkey, rkey],
            tag="zmul")
        pb.emit(eng, lambda e, t=t, buf=buf: e.tensor_scalar(
            buf, buf, cx.cols[:, t:t + 1], cx.cols[:, 8 + t:9 + t],
            ALU.mult, ALU.add),
            reads=[bkey, ("lncol", "w"), ("lncol", "b"), ("cols",)],
            writes=[bkey, rkey], tag="zwb")
        for lb in range(16):
            pk = lb % 2
            pst = cx.ps[pk][:, 0:128]
            pb.emit("pe", lambda e, lb=lb, buf=buf, pst=pst: e.transpose(
                pst, buf[:, lb * 128:(lb + 1) * 128], cx.ident_f32),
                reads=[bkey, rkey, ("c", "idf")], writes=[("ps", pk)],
                tag="ztr")
            pb.emit("dve", lambda e, lb=lb, t=t, pst=pst: e.tensor_copy(
                zT[:, lb * 1024 + t * 128: lb * 1024 + (t + 1) * 128], pst),
                reads=[("ps", pk)], writes=[("zT", lb), ("A1",)], tag="zTev")


def _l0_kbuild_lambda(cx, i):
    pb, ps, dr = cx.pb, cx.ps, cx.dr
    sm = cx.sm
    S = lambda j: sm[:, j * 256:(j + 1) * 256]
    kk = lambda j: ("ksm", i, j)

    def op(stream, fn, rd, wr, tg):
        pb.emit(stream, fn, reads=[kk(j) for j in rd],
                writes=[kk(j) for j in wr] + [("sm",)], tag=tg)

    for nm, slot in [("logdt_cg", 0), ("are_cg", 7), ("aim_cg", 8),
                     ("cre_cg", 3), ("cim_cg", 4)]:
        pb.emit("sp", lambda e, nm=nm, slot=slot: e.dma_start(
            out=S(slot), in_=dr[nm][i]), writes=[kk(slot), ("sm",)], dma=True,
            tag="kld" + nm)
    op("act", lambda e: e.activation(S(0), S(0), AF.Exp), [0], [0], "dt")
    op("act", lambda e: e.activation(S(7), S(7), AF.Exp), [7], [7], "eare")
    op("dve", lambda e: e.tensor_scalar_mul(S(7), S(7), -1.0), [7], [7], "neg")
    op("dve", lambda e: e.tensor_tensor(S(9), S(7), S(0), ALU.mult),
       [7, 0], [9], "dtar")
    op("dve", lambda e: e.tensor_tensor(S(1), S(8), S(0), ALU.mult),
       [8, 0], [1], "dtai")
    op("act", lambda e: e.activation(S(0), S(9), AF.Exp), [9], [0], "er")
    C1 = 6.28125
    C2 = 2 * math.pi - C1
    op("dve", lambda e: e.tensor_scalar(S(2), S(1), 1.0 / (2 * math.pi), 64.0,
                                        ALU.mult, ALU.add), [1], [2], "rr1")
    itmp = cx.w1[:, 1024:1280].bitcast(I32)
    pb.emit("dve", lambda e: e.tensor_copy(itmp, S(2)),
            reads=[kk(2), ("sm",)], writes=[("kitmp",), ("w1",)], tag="rr2a")
    pb.emit("dve", lambda e: e.tensor_copy(S(2), itmp),
            reads=[("kitmp",), ("w1",)], writes=[kk(2), ("sm",)], tag="rr2b")
    op("dve", lambda e: e.tensor_scalar_add(S(2), S(2), -64.0), [2], [2], "rr4")
    op("dve", lambda e: e.tensor_single_scalar(S(9), S(2), C1, ALU.mult),
       [2], [9], "rr5")
    op("dve", lambda e: e.tensor_tensor(S(1), S(1), S(9), ALU.subtract),
       [1, 9], [1], "rr6")
    op("dve", lambda e: e.tensor_single_scalar(S(9), S(2), C2, ALU.mult),
       [2], [9], "rr7")
    op("dve", lambda e: e.tensor_tensor(S(1), S(1), S(9), ALU.subtract),
       [1, 9], [1], "rr8")
    op("dve", lambda e: e.tensor_single_scalar(S(9), S(1), math.pi, ALU.is_gt),
       [1], [9], "f1")
    op("dve", lambda e: e.tensor_single_scalar(S(9), S(9), C1 + C2, ALU.mult),
       [9], [9], "f2")
    op("dve", lambda e: e.tensor_tensor(S(1), S(1), S(9), ALU.subtract),
       [1, 9], [1], "f3")
    op("dve", lambda e: e.tensor_single_scalar(S(9), S(1), -math.pi, ALU.is_lt),
       [1], [9], "f4")
    op("dve", lambda e: e.tensor_single_scalar(S(9), S(9), C1 + C2, ALU.mult),
       [9], [9], "f5")
    op("dve", lambda e: e.tensor_tensor(S(1), S(1), S(9), ALU.add),
       [1, 9], [1], "f6")
    op("act", lambda e: e.activation(S(2), S(1), AF.Sin), [1], [2], "sin")
    op("act", lambda e: e.activation(S(9), S(1), AF.Abs), [1], [9], "abs")
    op("dve", lambda e: e.tensor_scalar(S(9), S(9), -1.0, math.pi / 2,
                                        ALU.mult, ALU.add), [9], [9], "carg")
    op("act", lambda e: e.activation(S(9), S(9), AF.Sin), [9], [9], "cos")
    op("dve", lambda e: e.tensor_tensor(S(1), S(0), S(9), ALU.mult),
       [0, 9], [1], "lr")
    op("dve", lambda e: e.tensor_tensor(S(2), S(0), S(2), ALU.mult),
       [0, 2], [2], "li")
    op("dve", lambda e: e.tensor_tensor(S(0), S(7), S(7), ALU.mult),
       [7], [0], "a2")
    op("dve", lambda e: e.tensor_tensor(S(9), S(8), S(8), ALU.mult),
       [8], [9], "b2")
    op("dve", lambda e: e.tensor_tensor(S(0), S(0), S(9), ALU.add),
       [0, 9], [0], "ab")
    op("dve", lambda e: e.reciprocal(S(0), S(0)), [0], [0], "rec")
    op("dve", lambda e: e.tensor_scalar_add(S(9), S(1), -1.0), [1], [9], "nr")
    op("dve", lambda e: e.tensor_tensor(S(5), S(9), S(7), ALU.mult),
       [9, 7], [5], "e1a")
    op("dve", lambda e: e.tensor_tensor(S(6), S(2), S(8), ALU.mult),
       [2, 8], [6], "e1b")
    op("dve", lambda e: e.tensor_tensor(S(5), S(5), S(6), ALU.add),
       [5, 6], [5], "e1c")
    op("dve", lambda e: e.tensor_tensor(S(5), S(5), S(0), ALU.mult),
       [5, 0], [5], "e1")
    op("dve", lambda e: e.tensor_tensor(S(6), S(2), S(7), ALU.mult),
       [2, 7], [6], "e2a")
    op("dve", lambda e: e.tensor_tensor(S(7), S(9), S(8), ALU.mult),
       [9, 8], [7], "e2b")
    op("dve", lambda e: e.tensor_tensor(S(6), S(6), S(7), ALU.subtract),
       [6, 7], [6], "e2c")
    op("dve", lambda e: e.tensor_tensor(S(6), S(6), S(0), ALU.mult),
       [6, 0], [6], "e2")
    op("dve", lambda e: e.tensor_tensor(S(0), S(3), S(5), ALU.mult),
       [3, 5], [0], "cp1")
    op("dve", lambda e: e.tensor_tensor(S(9), S(4), S(6), ALU.mult),
       [4, 6], [9], "cp2")
    op("dve", lambda e: e.tensor_tensor(S(7), S(0), S(9), ALU.subtract),
       [0, 9], [7], "cpr")
    op("dve", lambda e: e.tensor_tensor(S(0), S(3), S(6), ALU.mult),
       [3, 6], [0], "cp3")
    op("dve", lambda e: e.tensor_tensor(S(9), S(4), S(5), ALU.mult),
       [4, 5], [9], "cp4")
    op("dve", lambda e: e.tensor_tensor(S(8), S(0), S(9), ALU.add),
       [0, 9], [8], "cpi")
    op("dve", lambda e: e.tensor_copy(S(5), S(1)), [1], [5], "l6r")
    op("dve", lambda e: e.tensor_copy(S(6), S(2)), [2], [6], "l6i")
    for _ in range(6):
        op("dve", lambda e: e.tensor_tensor(S(0), S(5), S(5), ALU.mult),
           [5], [0], "sqa")
        op("dve", lambda e: e.tensor_tensor(S(9), S(6), S(6), ALU.mult),
           [6], [9], "sqb")
        op("dve", lambda e: e.tensor_tensor(S(6), S(5), S(6), ALU.mult),
           [5, 6], [6], "sqc")
        op("dve", lambda e: e.tensor_single_scalar(S(6), S(6), 2.0, ALU.mult),
           [6], [6], "sqd")
        op("dve", lambda e: e.tensor_tensor(S(5), S(0), S(9), ALU.subtract),
           [0, 9], [5], "sqe")
    if ("ksm%d" % i) in _DEBUG_TAPS:
        _tap(cx, "ksm%d" % i, cx.sm, (128, 2560), F32,
             [kk(j) for j in range(10)] + [("sm",)])

    op("dve", lambda e: e.memset(S(3), 0.0), [], [3], "dskz")
    pb.emit("pool", lambda e: e.dma_start(
        out=cx.sm[0:97:32, 768:1024], in_=dr["dskip_cg"][i]),
        reads=[kk(3)], writes=[("dskip", i), ("sm",)], dma=True, tag="dskld")


def _l0_kbuild_rest(cx, i, defer=None):
    pb, ps, dr = cx.pb, cx.ps, cx.dr
    _sec = []

    def em(*a, **k):
        _sec.append((a, k))
    sm = cx.sm
    S = lambda j: sm[:, j * 256:(j + 1) * 256]
    kk = lambda j: ("ksm", i, j)
    vre = cx.av(65536, 8192, BF16).rearrange("p (g q) -> p g q", q=64)
    vim = cx.av(81920, 8192, BF16).rearrange("p (g q) -> p g q", q=64)
    ure = cx.av(98304, 4096, BF16).rearrange("p (g r) -> p g r", r=32)
    uim = cx.av(106496, 4096, BF16).rearrange("p (g r) -> p g r", r=32)
    lhs = [cx.av(114688, 4096, BF16), cx.av(122880, 4096, BF16)]
    stg = cx.av(131072, 2048, BF16)
    mr = cx.rw[:, 0:64]
    mi = cx.rw[:, 64:128]
    tr_ = cx.rw[:, 128:192]
    ti_ = cx.rw[:, 192:256]
    mrb16 = cx.rw[:, 256:288].bitcast(BF16)
    mib16 = cx.rw[:, 288:320].bitcast(BF16)
    t1f = cx.rw[:, 320:832]
    t2f = cx.rw[:, 832:1344]

    def mul_chain_ops(gw, lam_r, lam_i, kread, mkey):
        em("dve", lambda e: e.tensor_copy(mr[:, 0:gw], lam_r),
           reads=kread + [("rw",)], writes=[mkey, ("rw",)], tag="vmr")
        em("dve", lambda e: e.tensor_copy(mi[:, 0:gw], lam_i),
           reads=kread + [("rw",)], writes=[mkey, ("rw",)], tag="vmi")

    def msq(gw, mkey):
        mrh, mih = mr[:, 0:gw], mi[:, 0:gw]
        trh, tih = tr_[:, 0:gw], ti_[:, 0:gw]
        em("dve", lambda e: e.tensor_tensor(trh, mrh, mrh, ALU.mult),
           reads=[mkey, ("rw",)], writes=[("mt",), ("rw",)], tag="ms1")
        em("dve", lambda e: e.tensor_tensor(tih, mih, mih, ALU.mult),
           reads=[mkey, ("rw",)], writes=[("mt",), ("rw",)], tag="ms2")
        em("dve", lambda e: e.tensor_tensor(mih, mrh, mih, ALU.mult),
           reads=[mkey, ("rw",)], writes=[mkey, ("rw",)], tag="ms3")
        em("dve", lambda e: e.tensor_single_scalar(mih, mih, 2.0, ALU.mult),
           reads=[mkey, ("rw",)], writes=[mkey, ("rw",)], tag="ms4")
        em("dve", lambda e: e.tensor_tensor(mrh, trh, tih, ALU.subtract),
           reads=[("mt",), mkey, ("rw",)], writes=[mkey, ("rw",)], tag="ms5")

    def dbl_step(R, I, gw, c, w, mkey, vkey, reg, off=0, cast=True):
        if cast:
            em("dve", lambda e: e.tensor_copy(mrb16[:, 0:gw], mr[:, 0:gw]),
               reads=[mkey, ("rw",)], writes=[("mb",), ("rw",)], tag="dc1")
            em("dve", lambda e: e.tensor_copy(mib16[:, 0:gw], mi[:, 0:gw]),
               reads=[mkey, ("rw",)], writes=[("mb",), ("rw",)], tag="dc2")
        mrb = mrb16[:, 0:gw].unsqueeze(2).broadcast_to((128, gw, w))
        mib = mib16[:, 0:gw].unsqueeze(2).broadcast_to((128, gw, w))
        a, b_ = R[:, :, off:off + w], I[:, :, off:off + w]
        o_r, o_i = R[:, :, c + off:c + off + w], I[:, :, c + off:c + off + w]
        t1 = t1f.bitcast(BF16)[:, 0:gw * w].rearrange(
            "p (g w) -> p g w", w=w)
        t2 = t2f.bitcast(BF16)[:, 0:gw * w].rearrange(
            "p (g w) -> p g w", w=w)
        em("dve", lambda e: e.tensor_tensor(t1, a, mrb, ALU.mult),
           reads=[vkey, ("mb",), ("rw",)], writes=[("t1",), ("rw",)], tag="d1")
        em("dve", lambda e: e.tensor_tensor(t2, b_, mib, ALU.mult),
           reads=[vkey, ("mb",), ("rw",)], writes=[("t2",), ("rw",)], tag="d2")
        em("dve", lambda e: e.tensor_tensor(o_r, t1, t2, ALU.subtract),
           reads=[("t1",), ("t2",), ("rw",)], writes=[vkey, reg], tag="d3")
        em("dve", lambda e: e.tensor_tensor(t1, a, mib, ALU.mult),
           reads=[vkey, ("mb",), ("rw",)], writes=[("t1",), ("rw",)], tag="d4")
        em("dve", lambda e: e.tensor_tensor(t2, b_, mrb, ALU.mult),
           reads=[vkey, ("mb",), ("rw",)], writes=[("t2",), ("rw",)], tag="d5")
        em("dve", lambda e: e.tensor_tensor(o_i, t1, t2, ALU.add),
           reads=[("t1",), ("t2",), ("rw",)], writes=[vkey, reg], tag="d6")

    vsecs = []
    for ch in range(4):
        _sec = []
        vsecs.append(_sec)
        g0 = ch * 64
        vsl = slice(g0 % 128, g0 % 128 + 64)
        vkey = ("vb", i, ch, "v")
        R = vre[:, vsl, :]
        I = vim[:, vsl, :]
        em("dve", lambda e, R=R: e.memset(R[:, :, 0:1], 1.0),
           writes=[vkey, ("B1",)], tag="v0r")
        em("dve", lambda e, I=I: e.memset(I[:, :, 0:1], 0.0),
           writes=[vkey, ("B1",)], tag="v0i")
        em("dve", lambda e, R=R, g0=g0: e.tensor_copy(
            R[:, :, 1:2], S(1)[:, g0:g0 + 64]),
            reads=[kk(1)], writes=[vkey, ("B1",)], tag="v1r")
        em("dve", lambda e, I=I, g0=g0: e.tensor_copy(
            I[:, :, 1:2], S(2)[:, g0:g0 + 64]),
            reads=[kk(2)], writes=[vkey, ("B1",)], tag="v1i")
        mkey = ("vm", i, ch)
        mul_chain_ops(64, S(1)[:, g0:g0 + 64], S(2)[:, g0:g0 + 64],
                      [kk(1), kk(2)], mkey)
        c = 2
        while c < 64:
            w = min(c, 64 - c)
            msq(64, mkey)
            for off in range(0, w, 16):
                dbl_step(R, I, 64, c, min(16, w - off), mkey, vkey,
                         ("B1",), off=off, cast=(off == 0))
            c += w
    udves, umps = [], []
    _sec = []
    em("dve", lambda e: e.memset(lhs[0], 0.0),
       writes=[("lhsb", 0), ("B2",)], tag="msA")
    em("dve", lambda e: e.memset(lhs[1], 0.0),
       writes=[("lhsb", 1), ("B2",)], tag="msB")
    pre_u = _sec
    l3r = lhs[0].rearrange("p (gl c) -> p gl c", c=128)
    l3i = lhs[1].rearrange("p (gl c) -> p gl c", c=128)
    for b in range(8):
        _sec = []
        udves.append(_sec)
        g0 = b * 32
        usl = slice(g0 % 128, g0 % 128 + 32)
        vsl0 = g0 % 128
        vkey = ("vb", i, g0 // 64, "v")
        ukey = ("ub", i, b)
        Ru = ure[:, usl, :]
        Iu = uim[:, usl, :]
        em("dve", lambda e, Ru=Ru, g0=g0: e.tensor_single_scalar(
            Ru[:, :, 0:1], S(7)[:, g0:g0 + 32], 2.0, ALU.mult),
            reads=[kk(7)], writes=[ukey, ("B2",)], tag="u0r")
        em("dve", lambda e, Iu=Iu, g0=g0: e.tensor_single_scalar(
            Iu[:, :, 0:1], S(8)[:, g0:g0 + 32], 2.0, ALU.mult),
            reads=[kk(8)], writes=[ukey, ("B2",)], tag="u0i")
        mkey = ("um", i, b)
        mul_chain_ops(32, S(5)[:, g0:g0 + 32], S(6)[:, g0:g0 + 32],
                      [kk(5), kk(6)], mkey)
        c = 1
        while c < 32:
            w = min(c, 32 - c)
            if c > 1:
                msq(32, mkey)
            dbl_step(Ru, Iu, 32, c, w, mkey, ukey, ("B2",))
            c += w
        del c
        for c4 in range(4):
            rows = slice(c4 * 32, (c4 + 1) * 32)
            csl = slice(c4 * 32, c4 * 32 + 32)
            em("dve", lambda e, rows=rows, usl=usl, csl=csl:
               e.tensor_copy(l3r[rows, :, csl], ure[rows, usl, :]),
               reads=[ukey, ("B2",)], writes=[("lhsb", 0), ("B2",)],
               tag="lscR")
            em("dve", lambda e, rows=rows, usl=usl, csl=csl:
               e.tensor_single_scalar(l3i[rows, :, csl],
                                      uim[rows, usl, :], -1.0, ALU.mult),
               reads=[ukey, ("B2",)], writes=[("lhsb", 1), ("B2",)],
               tag="lscI")
        _sec = []
        umps.append(_sec)
        for j in range(8):
            bank = 4 + j % 4
            pk = ("ps", bank)
            for k in range(4):
                gl = 4 * j + k
                csl_ = slice(k * 64, (k + 1) * 64)
                em("pe", lambda e, gl=gl, vsl0=vsl0, bank=bank, csl_=csl_:
                   e.matmul(cx.ps[bank][:, csl_], l3r[:, gl, :],
                            vre[:, vsl0 + gl, :], start=True, stop=False),
                   reads=[("lhsb", 0), vkey, ("B1",), ("B2",)],
                   writes=[pk], tag="kmmR")
                em("pe", lambda e, gl=gl, vsl0=vsl0, bank=bank, csl_=csl_:
                   e.matmul(cx.ps[bank][:, csl_], l3i[:, gl, :],
                            vim[:, vsl0 + gl, :], start=False, stop=True),
                   reads=[("lhsb", 1), vkey, ("B1",), ("B2",)],
                   writes=[pk], tag="kmmI")
            em("act", lambda e, j=j, bank=bank: e.activation(
                stg.rearrange("p (q gl) -> p q gl", gl=32)[
                    :, :, 4 * j:4 * j + 4],
                cx.ps[bank][:, 0:256].rearrange("p (g q) -> p q g", g=4),
                AF.Copy),
                reads=[pk], writes=[("stg",), ("B2",)], tag="kev")
        gsg = slice(768 + b * 32, 768 + (b + 1) * 32)
        em("dve", lambda e, gsg=gsg: e.tensor_tensor(
            stg[:, 0:32], stg[:, 0:32], cx.sm[:, gsg], ALU.add),
            reads=[("stg",), ("dskip", i), ("sm",)], writes=[("stg",)],
            tag="kdsk")
        for c4 in range(4):
            src_ = stg[c4 * 32:(c4 + 1) * 32, :].rearrange(
                "r (q gl) -> r q gl", gl=32)
            dst_ = cx.k_scr.rearrange(
                "(r q) (c4 bb gl) -> r q c4 bb gl", q=64, c4=4, bb=8)[
                0:32, :, c4, b, :]
            em("sp", lambda e, src_=src_, dst_=dst_: e.dma_start(
                out=dst_, in_=src_),
                reads=[("stg",)], writes=[("kscr", b, c4)], dma=True,
                tag="kdump")
    _sec = []
    em("dve", lambda e: e.engine_nop(),
       reads=[("kscr", b, c4) for b in range(8) for c4 in range(4)],
       writes=[("vb", i, 0, "v"), ("vb", i, 1, "v"), ("vb", i, 2, "v"),
               ("vb", i, 3, "v")] +
              [("ub", i, b) for b in range(8)] +
              [("lhsb", 0), ("lhsb", 1), ("stg",), ("Bfree", i)], tag="kbar")
    if defer is None and ("kscr%d" % i) in _DEBUG_TAPS:
        _tap(cx, "kscr%d" % i, cx.k_scr, (T, D), BF16,
             [("kscr", b2, c2) for b2 in range(8) for c2 in range(4)])
    kT = cx.av(65536, 34816, BF16)
    for t in range(16):
        em("pool" if t % 2 == 0 else "act", lambda e, t=t: e.dma_start(
            out=kT[:, t * 1024:(t + 1) * 1024],
            in_=cx.k_scr[t * 128:(t + 1) * 128, :]),
            reads=[("Bfree", i)] +
                  [("kscr", b2, c2) for b2 in range(8) for c2 in range(4)],
            writes=[("kT", t), ("B1",)], dma=True, tag="kTld")
    tail = _sec
    groups = [vsecs[0] + vsecs[1],
              pre_u + udves[0],
              umps[0] + udves[1],
              umps[1] + udves[2],
              vsecs[2] + umps[2] + udves[3],
              umps[3] + udves[4],
              vsecs[3] + umps[4] + udves[5],
              umps[5] + udves[6],
              umps[6] + udves[7],
              umps[7] + tail]
    if defer is None:
        for sec in groups:
            for a, k in sec:
                pb.emit(*a, **k)
        return None
    cuts = []
    for sec in groups:
        defer.extend(sec)
        cuts.append(len(defer))
    return cuts


def _l0_conv(cx, i, nxt=None, kq=None, vcut=0):
    pb, ps = cx.pb, cx.ps
    zT = cx.av(0, 16384, BF16)
    kT = cx.av(65536, 34816, BF16)
    A2 = cx.av(32768, 16384, BF16)
    KREG = cx.av(65536, 34816, BF16)

    def p_re(ft):
        if ft < 16:
            return A2[:, ft * 1024:(ft + 1) * 1024]
        return cx.wst[:, 0:1024]

    def p_im(ft):
        if ft < 16:
            return KREG[:, 16384 + ft * 1024:16384 + (ft + 1) * 1024]
        return KREG[:, 32768:33792]

    for ft in range(FT):
        half = ft % 2
        cosb = cx.dst[:, half * 4352: half * 4352 + 2048]
        sinb = cx.dst[:, half * 4352 + 2048: half * 4352 + 4096]
        ck = ("dstream", half)
        pb.emit("sp", lambda e, ft=ft, cosb=cosb: e.dma_start(
            out=cosb, in_=cx.dr["dftc"][ft].rearrange("p a c -> p (a c)")),
            writes=[ck], dma=True, tag="dftc")
        pb.emit("sp", lambda e, ft=ft, sinb=sinb: e.dma_start(
            out=sinb, in_=cx.dr["dfts"][ft].rearrange("p a c -> p (a c)")),
            writes=[ck], dma=True, tag="dfts")
        for lb in range(16):
            st, sp_ = (lb == 0), (lb == 15)
            for comp, buf in ((0, cosb), (1, sinb)):
                for s in range(2):
                    base = 4 * ((2 * ft + s) % 2)
                    for kind in range(2):
                        bank = base + comp + 2 * kind
                        rhs_t = zT if kind == 0 else kT
                        rkey = ("zT", lb) if kind == 0 else ("kT", lb)
                        extra = [("A1",)] if kind == 0 else [("B1",)]
                        pb.emit("pe", lambda e, lb=lb, s=s, buf=buf,
                                bank=bank, rhs_t=rhs_t, st=st, sp_=sp_:
                                e.matmul(ps[bank][:, :],
                                         buf[:, lb * 128:(lb + 1) * 128],
                                         rhs_t[:, lb * 1024 + s * 512:
                                               lb * 1024 + (s + 1) * 512],
                                         start=st, stop=sp_),
                                reads=[ck, rkey] + extra,
                                writes=[("ps", bank)], tag="cfmm")
        for s in range(2):
            base = 4 * ((2 * ft + s) % 2)
            zr = cx.w0b[:, 0:512]
            zi = cx.w0b[:, 512:1024]
            kr = cx.w0b[:, 1024:1536]
            ki = cx.w0b[:, 1536:2048]
            for j, dstv in ((base + 0, zr), (base + 1, zi), (base + 2, kr),
                            (base + 3, ki)):
                pb.emit("dve", lambda e, j=j, dstv=dstv: e.tensor_copy(
                    dstv, ps[j][:, :]),
                    reads=[("ps", j)], writes=[("pw", i), ("w0",)],
                    tag="spev")
            t1 = cx.w1b[:, 0:512]
            t2 = cx.w1b[:, 512:1024]
            pr = p_re(ft)[:, s * 512:(s + 1) * 512]
            pi = p_im(ft)[:, s * 512:(s + 1) * 512]
            pb.emit("dve", lambda e, t1=t1, zr=zr, kr=kr: e.tensor_tensor(
                t1, zr, kr, ALU.mult), reads=[("pw", i), ("w0",)],
                writes=[("pwt", i), ("w1",)], tag="pw1")
            pb.emit("dve", lambda e, t2=t2, zi=zi, ki=ki: e.tensor_tensor(
                t2, zi, ki, ALU.mult), reads=[("pw", i), ("w0",)],
                writes=[("pwt", i), ("w1",)], tag="pw2")
            pb.emit("dve", lambda e, pr=pr, t1=t1, t2=t2: e.tensor_tensor(
                pr, t1, t2, ALU.subtract), reads=[("pwt", i), ("w1",)],
                writes=[("P", ft, 0)] + ([("A2",)] if ft < 16 else [("wst",)]),
                tag="pw3")
            pb.emit("dve", lambda e, t1=t1, zr=zr, ki=ki: e.tensor_tensor(
                t1, zr, ki, ALU.mult), reads=[("pw", i), ("w0",)],
                writes=[("pwt", i), ("w1",)], tag="pw4")
            pb.emit("dve", lambda e, t2=t2, zi=zi, kr=kr: e.tensor_tensor(
                t2, zi, kr, ALU.mult), reads=[("pw", i), ("w0",)],
                writes=[("pwt", i), ("w1",)], tag="pw5")
            pb.emit("dve", lambda e, pi=pi, t1=t1, t2=t2: e.tensor_tensor(
                pi, t1, t2, ALU.add), reads=[("pwt", i), ("w1",)],
                writes=[("P", ft, 1), ("B2",)], tag="pw6")
    pb.emit("dve", lambda e: e.engine_nop(),
            reads=[("P", ft, c) for ft in range(FT) for c in range(2)],
            writes=[("zT", lb) for lb in range(16)] +
                   [("kT", t) for t in range(16)] + [("convfwd", i)],
            tag="fbar")
    if nxt is not None:
        _l0_kbuild_lambda(cx, nxt)
    vflushed = [0]

    def vflush(j):
        if not kq:
            return
        hi = min(vcut, (vcut * j) // 10)
        while vflushed[0] < hi:
            a, kw = kq[vflushed[0]]
            pb.emit(*a, **kw)
            vflushed[0] += 1
    ygT = cx.av(0, 16384, BF16)
    for lt in range(16):
        half = lt % 2
        icb = cx.dst[:, half * 4352: half * 4352 + 2176]
        isb = cx.dst[:, half * 4352 + 2176: half * 4352 + 4352]
        ik = ("dstream", half)
        pb.emit("sp", lambda e, lt=lt, icb=icb: e.dma_start(
            out=icb, in_=cx.dr["idftc"][lt].rearrange("p a c -> p (a c)")),
            writes=[ik], dma=True, tag="idftc")
        pb.emit("sp", lambda e, lt=lt, isb=isb: e.dma_start(
            out=isb, in_=cx.dr["idfts"][lt].rearrange("p a c -> p (a c)")),
            writes=[ik], dma=True, tag="idfts")
        bank0 = (lt % 2) * 2
        n = 0
        for ft in range(FT):
            for comp in range(2):
                buf = icb if comp == 0 else isb
                pv = p_re(ft) if comp == 0 else p_im(ft)
                st = (n == 0)
                sp_ = (n == 2 * FT - 1)
                for s in range(2):
                    pb.emit("pe", lambda e, ft=ft, buf=buf, pv=pv, s=s,
                            st=st, sp_=sp_, bank0=bank0: e.matmul(
                                ps[bank0 + s][:, :],
                                buf[:, ft * 128:(ft + 1) * 128],
                                pv[:, s * 512:(s + 1) * 512], start=st,
                                stop=sp_),
                            reads=[ik, ("P", ft, comp), ("convfwd", i),
                                   ("A2",), ("B2",), ("wst",)],
                            writes=[("ps", bank0 + s)], tag="idft")
                n += 1
        for s in range(2):
            pb.emit("act", lambda e, lt=lt, s=s, bank0=bank0: e.activation(
                ygT[:, lt * 1024 + s * 512: lt * 1024 + (s + 1) * 512],
                ps[bank0 + s][:, :], AF.Gelu_apprx_tanh),
                reads=[("ps", bank0 + s)], writes=[("ygT", lt), ("A1",)],
                tag="gelu")
        vflush(lt + 1)
    pb.emit("dve", lambda e: e.engine_nop(),
            reads=[("ygT", lt) for lt in range(16)],
            writes=[("P", ft, c) for ft in range(FT) for c in range(2)] +
                   [("idone", i)], tag="ibar")


def _l0_wout(cx, i, kq=None, cuts=None):
    pb, ps, dr = cx.pb, cx.ps, cx.dr
    kflushed = [cuts[0] if cuts else 0]

    def kflush(j):
        if not kq:
            return
        hi = cuts[min(j, len(cuts) - 1)]
        while kflushed[0] < hi:
            a, kw = kq[kflushed[0]]
            pb.emit(*a, **kw)
            kflushed[0] += 1
    ygT = cx.av(0, 16384, BF16)
    ygfm = cx.av(32768, 16384, BF16)
    for c in range(8):
        for lt in range(16):
            pk = lt % 2
            pst = ps[pk][:, 0:64].bitcast(BF16)
            pb.emit("pe", lambda e, c=c, lt=lt, pst=pst: e.transpose(
                pst, ygT[:, lt * 1024 + c * 128:lt * 1024 + (c + 1) * 128],
                cx.ident_bf),
                reads=[("ygT", lt), ("idone", i), ("A1",), ("c", "idb")],
                writes=[("ps", pk)], tag="ytr")
            pb.emit("dve", lambda e, c=c, lt=lt, pst=pst: e.tensor_copy(
                ygfm[:, c * 2048 + lt * 128:c * 2048 + (lt + 1) * 128], pst),
                reads=[("ps", pk)], writes=[("ygfm", c), ("A2",)],
                tag="ytrev")
    pb.emit("sp", lambda e: e.dma_start(
        out=cx.cols[:, 16:32],
        in_=dr["b_out"][i]),
        writes=[("bocol", i), ("cols",)], dma=True, tag="bout")
    for lo in range(8):
        wslots = {}
        for part, ot in ((0, lo), (1, lo + 8)):
            wslot = cx.wst[:, (2 * lo + part) % 4 * 1024:
                           ((2 * lo + part) % 4 + 1) * 1024]
            wkey = ("wostream", (2 * lo + part) % 4)
            wslots[part] = (wslot, wkey)
            pb.emit("pool", lambda e, ot=ot, wslot=wslot: e.dma_start(
                out=wslot,
                in_=dr["w_out_t"][i, ot].rearrange("p a c -> p (a c)")),
                reads=[("idone", i)], writes=[wkey, ("wst",)], dma=True,
                tag="wold")
        for r in range(2):
            for part in range(2):
                wslot, wkey = wslots[part]
                for ha in range(8):
                    for si in range(2):
                        s = 2 * r + si
                        bank = part * 2 + si
                        pb.emit("pe", lambda e, s=s, ha=ha, bank=bank,
                                wslot=wslot: e.matmul(
                                    ps[bank][:, :],
                                    wslot[:, ha * 128:(ha + 1) * 128],
                                    ygfm[:, ha * 2048 + s * 512:
                                         ha * 2048 + (s + 1) * 512],
                                    start=(ha == 0), stop=(ha == 7)),
                                reads=[wkey, ("ygfm", ha), ("A2",)],
                                writes=[("ps", bank)], tag="womm")
            for si in range(2):
                s = 2 * r + si
                sig = cx.w0b[:, s * 512:(s + 1) * 512]
                pb.emit("act", lambda e, si=si, lo=lo, sig=sig: e.activation(
                    sig, ps[2 + si][:, :], AF.Sigmoid,
                    bias=cx.cols[:, 16 + lo + 8:17 + lo + 8]),
                    reads=[("ps", 2 + si), ("bocol", i), ("cols",)],
                    writes=[("glu", "s"), ("w0",)], tag="sig")
                lin = cx.w1b[:, s * 512:(s + 1) * 512]
                pb.emit("act", lambda e, si=si, lo=lo, lin=lin: e.activation(
                    lin, ps[si][:, :], AF.Identity,
                    bias=cx.cols[:, 16 + lo:17 + lo]),
                    reads=[("ps", si), ("bocol", i), ("cols",)],
                    writes=[("glu", "l"), ("w1",)], tag="lin")
        zo = cx.dst[:, 0:2048]
        pb.emit("dve", lambda e, zo=zo: e.tensor_tensor(
            zo, cx.w1b[:, 0:2048], cx.w0b[:, 0:2048], ALU.mult),
            reads=[("glu", "s"), ("glu", "l"), ("w0",), ("w1",)],
            writes=[("zo", lo), ("dstream", 0)], tag="glu")
        pb.emit("sp", lambda e, lo=lo: e.dma_start(
            out=cx.w2, in_=cx.h_dram[lo * 128:(lo + 1) * 128, :]),
            reads=[("hdram", lo)], writes=[("hres", lo), ("w2",)], dma=True,
            tag="hld")
        pb.emit("dve", lambda e, lo=lo, zo=zo: e.tensor_tensor(
            cx.w2, cx.w2, zo, ALU.add),
            reads=[("hres", lo), ("zo", lo), ("dstream", 0)],
            writes=[("hres", lo), ("w2",)], tag="hadd")
        pb.emit("pool", lambda e, lo=lo: e.dma_start(
            out=cx.h_dram[lo * 128:(lo + 1) * 128, :], in_=cx.w2),
            reads=[("hres", lo), ("w2",)], writes=[("hdram", lo)], dma=True,
            tag="hst")
        kflush(lo + 1)
    kflush(9)
    assert kflushed[0] == (len(kq) if kq else 0)
    pb.emit("dve", lambda e: e.engine_nop(),
            reads=[("hdram", t) for t in range(8)],
            writes=[("ygT", lt) for lt in range(16)] +
                   [("ygfm", c) for c in range(8)] + [("ldone", i)],
            tag="wbar")


def _phase_l0_attn(cx):
    pb, ps = cx.pb, cx.ps
    _l0_module_ln(cx)
    _tap(cx, "h0", cx.h_dram, (D, T), F32, [("hdram", t) for t in range(8)])
    layers = list(_LAYER_LIST if _LAYER_LIST is not None
                  else range(_NLAYERS))
    for li, i in enumerate(layers):
        _l0_zT(cx, i)
        if "zT%d" % i in _DEBUG_TAPS:
            _tap(cx, "zT%d" % i, cx.av(0, 16384, BF16), (128, 16384), BF16,
                 [("zT", lb) for lb in range(16)])
        if li == 0:
            _l0_kbuild_lambda(cx, i)
            _l0_kbuild_rest(cx, i)
        if "kT%d" % i in _DEBUG_TAPS:
            _tap(cx, "kT%d" % i, cx.av(65536, 16384, BF16), (128, 16384), BF16,
                 [("kT", t) for t in range(16)])
        nxt = layers[li + 1] if li + 1 < len(layers) else None
        kq, cuts = None, None
        if nxt is not None:
            kq = []
            cuts = _l0_kbuild_rest(cx, nxt, defer=kq)
        _l0_conv(cx, i, nxt=nxt, kq=kq, vcut=(cuts[0] if cuts else 0))
        if "ygT%d" % i in _DEBUG_TAPS:
            _tap(cx, "ygT%d" % i, cx.av(0, 16384, BF16), (128, 16384), BF16,
                 [("ygT", lt) for lt in range(16)])
        _l0_wout(cx, i, kq=kq, cuts=cuts)
        _tap(cx, "h%d" % (i + 1), cx.h_dram, (D, T), F32,
             [("hdram", t) for t in range(8)])
    sq = cx.av(178176, 2048, F32)
    for t in range(8):
        buf = cx.w0 if t % 2 == 0 else cx.w1
        rkey = ("w0",) if t % 2 == 0 else ("w1",)
        pb.emit("sp", lambda e, t=t, buf=buf: e.dma_start(
            out=buf, in_=cx.x_scr[t * 128:(t + 1) * 128, :]),
            reads=[("xscr", t)], writes=[("xres", t)] + [rkey], dma=True,
            tag="xld")
        pb.emit("pool", lambda e, t=t: e.dma_start(
            out=cx.w2, in_=cx.h_dram[t * 128:(t + 1) * 128, :]),
            reads=[("hdram", t)], writes=[("xres2", t), ("w2",)], dma=True,
            tag="xld2")
        pb.emit("dve", lambda e, t=t, buf=buf: e.tensor_tensor(
            buf, buf, cx.w2, ALU.add),
            reads=[("xres", t), ("xres2", t), ("w2",)] + [rkey],
            writes=[("xres", t)] + [rkey], tag="xadd")
        pb.emit("act", lambda e, buf=buf: e.activation(sq, buf, AF.Square),
                reads=[("xres", t)] + [rkey],
                writes=[("lnsq", "ffn"), ("dstream", 0)], tag="xsq")
        for s in range(4):
            pb.emit("pe", lambda e, t=t, s=s, buf=buf: e.matmul(
                ps[s][0:1, 0:512], cx.ones_col_f32,
                buf[:, s * 512:(s + 1) * 512], start=(t == 0), stop=(t == 7)),
                reads=[("xres", t), ("c", "ocf")] + [rkey],
                writes=[("ps", s)], tag="xlns")
            pb.emit("pe", lambda e, t=t, s=s: e.matmul(
                ps[4 + s][0:1, 0:512], cx.ones_col_f32,
                sq[:, s * 512:(s + 1) * 512], start=(t == 0), stop=(t == 7)),
                reads=[("lnsq", "ffn"), ("dstream", 0), ("c", "ocf")],
                writes=[("ps", 4 + s)], tag="xlns2")
        pb.emit("pool", lambda e, t=t, buf=buf: e.dma_start(
            out=cx.x_scr[t * 128:(t + 1) * 128, :], in_=buf),
            reads=[("xres", t)] + [rkey], writes=[("xscr", t)], dma=True,
            tag="xst")


def _phase_l0_ffn(cx):
    pb, ps, dr = cx.pb, cx.ps, cx.dr
    mb = cx.sm[:, 0:2048]
    rb = cx.w2
    _ln_finish(cx, "ffn", mb, rb, [("sm",)], [("w2",)])
    xq = [cx.av(t * 2048, 2048, FP8) for t in range(8)]
    for t in range(8):
        buf = cx.w0 if t % 2 == 0 else cx.w1
        rkey = ("w0",) if t % 2 == 0 else ("w1",)
        bkey = ("fxb", t % 2)
        pb.emit("sp", lambda e, t=t, buf=buf: e.dma_start(
            out=buf, in_=cx.x_scr[t * 128:(t + 1) * 128, :]),
            reads=[("xscr", t)], writes=[bkey, rkey], dma=True, tag="fxld")
        eng = "dve" if t % 2 == 0 else "pool"
        pb.emit(eng, lambda e, buf=buf: e.tensor_tensor(
            buf, buf, mb, ALU.subtract),
            reads=[bkey, ("mb", "ffn"), ("sm",)], writes=[bkey, rkey],
            tag="fxsub")
        pb.emit(eng, lambda e, t=t, buf=buf, rkey=rkey: e.tensor_tensor(
            xq[t], buf, rb, ALU.mult),
            reads=[bkey, rkey, ("rb", "ffn"), ("w2",)],
            writes=[("xq", t), ("A1",)], tag="fxq")
    qc = cx.av(61440, 256, FP8)
    pb.emit("sp", lambda e: e.dma_start(
        out=cx.cols[:, 32:64], in_=dr["b1"]),
        writes=[("bcol", 1), ("cols",)], dma=True, tag="b1ld")
    pb.emit("sp", lambda e: e.dma_start(
        out=cx.cols[:, 64:72], in_=dr["b2"]),
        writes=[("bcol", 2), ("cols",)], dma=True, tag="b2ld")
    pb.emit("dve", lambda e: e.tensor_copy(qc[:, 0:40], cx.cols[:, 32:72]),
            reads=[("bcol", 1), ("bcol", 2), ("cols",)],
            writes=[("qc",), ("A2",)], tag="bq1")
    pb.emit("dve", lambda e: e.tensor_copy(cx.cols[:, 32:72], qc[:, 0:40]),
            reads=[("qc",), ("A2",)], writes=[("bcol", 1), ("bcol", 2),
                                              ("cols",)], tag="bq2")

    h1q = [cx.av(16384 + nt * 1024, 1024, FP8) for nt in range(32)]
    wq1 = [cx.av(49152 + j * 1024, 1024, FP8) for j in range(2)]
    wq2 = [cx.av(51200 + j * 4096, 4096, FP8) for j in range(2)]
    msk = [cx.av(59392 + j * 1024, 1024, FP8) for j in range(2)]
    for half in range(2):
        hc = slice(half * 1024, (half + 1) * 1024)
        for nt in range(32):
            wslot = wq1[nt % 2]
            wkey = ("wq1", nt % 2)
            fkey = ("w1",)
            pb.emit("sp", lambda e, nt=nt: e.dma_start(
                out=cx.w1[:, 0:1024],
                in_=dr["w1_t"][nt].rearrange("p a c -> p (a c)")),
                writes=[("wf1",), fkey], dma=True, tag="w1ld")
            pb.emit("dve", lambda e, wslot=wslot: e.tensor_copy(
                wslot, cx.w1b.bitcast(F32)[:, 0:1024] if False else
                cx.w1[:, 0:1024]),
                reads=[("wf1",), fkey], writes=[wkey, ("A2",)], tag="w1q")
            for a in range(8):
                for s in range(2):
                    bank = (nt % 4) * 2 + s
                    pb.emit("pe", lambda e, nt=nt, s=s, a=a, bank=bank,
                            wslot=wslot, hc=hc: e.matmul(
                                ps[bank][:, :], wslot[:, a * 128:(a + 1) * 128],
                                xq[a][:, hc][:, s * 512:(s + 1) * 512],
                                start=(a == 0), stop=(a == 7)),
                            reads=[wkey, ("xq", a), ("A1",), ("A2",)],
                            writes=[("ps", bank)], tag="h1mm")
            for s in range(2):
                bank = (nt % 4) * 2 + s
                pb.emit("dve", lambda e, nt=nt, s=s, bank=bank: e.tensor_scalar(
                    cx.w0b[:, s * 512:(s + 1) * 512], ps[bank][:, :],
                    cx.cols[:, 32 + nt:33 + nt], 0.0, ALU.add, ALU.max),
                    reads=[("ps", bank), ("bcol", 1), ("cols",)],
                    writes=[("h1bf", nt % 2), ("w0",)], tag="h1ev")
            pb.emit("dve", lambda e, nt=nt: e.tensor_copy(
                h1q[nt], cx.w0b[:, 0:1024]),
                reads=[("h1bf", nt % 2), ("w0",)],
                writes=[("h1q", nt), ("A1",), ("A2",)], tag="h1q")
        for dt_ in range(8):
            wslot = wq2[dt_ % 2]
            wkey = ("wq2", dt_ % 2)
            for j in range(2):
                pb.emit("sp" if j == 0 else "pool",
                        lambda e, dt_=dt_, j=j: e.dma_start(
                            out=cx.w1[:, 0:2048] if j == 0
                            else cx.w0[:, 0:2048],
                            in_=dr["w2_t"][dt_, j].rearrange(
                                "p a c -> p (a c)")),
                        writes=[("wf2", j), ("w1",) if j == 0 else ("w0",)],
                        dma=True, tag="w2ld")
                pb.emit("dve" if j == 0 else "pool",
                        lambda e, wslot=wslot, j=j: e.tensor_copy(
                            wslot[:, j * 2048:(j + 1) * 2048],
                            cx.w1[:, 0:2048] if j == 0
                            else cx.w0[:, 0:2048]),
                        reads=[("wf2", j), ("w1",) if j == 0 else ("w0",)],
                        writes=[wkey, ("A2",)], tag="w2q")
            mslot = msk[dt_ % 2]
            pb.emit("pool", lambda e, dt_=dt_, mslot=mslot, hc=hc:
                    e.dma_start(out=mslot,
                                in_=dr["dropmask_fm"][
                                    dt_ * 128:(dt_ + 1) * 128, hc]),
                    writes=[("msk", dt_ % 2), ("A2",)], dma=True, tag="mskld")
            pb.emit("sp", lambda e, dt_=dt_, hc=hc: e.dma_start(
                out=cx.w1[:, 0:1024],
                in_=cx.x_scr[dt_ * 128:(dt_ + 1) * 128, hc]),
                reads=[("xscr", dt_)], writes=[("xr", dt_ % 2), ("w1",)],
                dma=True, tag="fxrld")
            for nt in range(32):
                for s in range(2):
                    bank = (dt_ % 2) * 4 + s
                    pb.emit("pe", lambda e, s=s, nt=nt, bank=bank,
                            wslot=wslot: e.matmul(
                                ps[bank][:, :],
                                wslot[:, nt * 128:(nt + 1) * 128],
                                h1q[nt][:, s * 512:(s + 1) * 512],
                                start=(nt == 0), stop=(nt == 31)),
                            reads=[wkey, ("h1q", nt), ("A1",), ("A2",)],
                            writes=[("ps", bank)], tag="h2mm")
            for s in range(2):
                bank = (dt_ % 2) * 4 + s
                pb.emit("dve", lambda e, dt_=dt_, s=s, bank=bank:
                        e.tensor_scalar(
                            cx.w0b[:, s * 512:(s + 1) * 512], ps[bank][:, :],
                            cx.cols[:, 64 + dt_:65 + dt_], None, ALU.add),
                        reads=[("ps", bank), ("bcol", 2), ("cols",)],
                        writes=[("h2bf", dt_ % 2), ("w0",)], tag="h2ev")
                pb.emit("dve", lambda e, s=s, mslot=mslot, dt_=dt_:
                        e.tensor_tensor(
                            cx.w0[:, 1024 + s * 512:1024 + (s + 1) * 512],
                            cx.w0b[:, s * 512:(s + 1) * 512],
                            mslot[:, s * 512:(s + 1) * 512], ALU.mult),
                        reads=[("h2bf", dt_ % 2), ("msk", dt_ % 2), ("w0",),
                               ("A2",)],
                        writes=[("h2f", dt_ % 2), ("w0",)], tag="h2msk")
                pb.emit("dve", lambda e, s=s: e.tensor_tensor(
                    cx.w1[:, s * 512:(s + 1) * 512],
                    cx.w1[:, s * 512:(s + 1) * 512],
                    cx.w0[:, 1024 + s * 512:1024 + (s + 1) * 512], ALU.add),
                    reads=[("h2f", dt_ % 2), ("xr", dt_ % 2), ("w0",),
                           ("w1",)],
                    writes=[("xr", dt_ % 2), ("w1",)], tag="fxadd")
            pb.emit("pool", lambda e, dt_=dt_, hc=hc: e.dma_start(
                out=cx.x_scr[dt_ * 128:(dt_ + 1) * 128, hc],
                in_=cx.w1[:, 0:1024]),
                reads=[("xr", dt_ % 2), ("w1",)], writes=[("xscr", dt_)],
                dma=True, tag="fxst")


def _tokmaj_proj(cx, w2name, half, xln, evac, tag):
    pb, ps, dr = cx.pb, cx.ps, cx.dr
    wstb = cx.wst
    for chunk in range(2):
        buf = cx.w0 if chunk == 0 else cx.w1
        fkey = ("w0",) if chunk == 0 else ("w1",)
        pb.emit("sp" if chunk == 0 else "pool",
                lambda e, chunk=chunk, buf=buf: e.dma_start(
                    out=buf[:, 0:2048],
                    in_=dr[w2name][half, :, chunk * 4:(chunk + 1) * 4, :]
                    .rearrange("p a c -> p (a c)")),
                writes=[("tmw", tag, chunk), fkey], dma=True,
                tag="tmld" + tag)
        pb.emit("dve", lambda e, chunk=chunk, buf=buf: e.tensor_copy(
            wstb[:, chunk * 2048:(chunk + 1) * 2048], buf[:, 0:2048]),
            reads=[("tmw", tag, chunk), fkey],
            writes=[("tmq", tag, chunk), ("wst",)], tag="tmq" + tag)
    for t in range(16):
        bank = t % 2
        for a in range(8):
            pb.emit("pe", lambda e, t=t, a=a, bank=bank: e.matmul(
                ps[bank][:, :], xln[a][:, t * 128:(t + 1) * 128],
                wstb[:, a * 512:(a + 1) * 512],
                start=(a == 0), stop=(a == 7)),
                reads=[("tmq", tag, a // 4), ("wst",), ("xln", a), ("A1",)],
                writes=[("ps", bank)], tag="tmmm" + tag)
        evac(t, bank)


def _phase_l1_attn(cx):
    pb, ps, dr = cx.pb, cx.ps, cx.dr
    mb = cx.sm[:, 0:2048]
    rb = cx.w2
    _ln_full(cx, cx.x_scr, lambda t: ("xscr", t), "la", mb, rb,
             [("sm",)], [("w2",)])
    xln = [cx.av(t * 4096, 2048, BF16) for t in range(8)]
    for t in range(8):
        buf = cx.w0 if t % 2 == 0 else cx.w1
        rkey = ("w0",) if t % 2 == 0 else ("w1",)
        bkey = ("lab", t % 2)
        pb.emit("sp", lambda e, t=t, buf=buf: e.dma_start(
            out=buf, in_=cx.x_scr[t * 128:(t + 1) * 128, :]),
            reads=[("xscr", t)], writes=[bkey, rkey], dma=True, tag="lald")
        eng = "dve" if t % 2 == 0 else "pool"
        pb.emit(eng, lambda e, buf=buf: e.tensor_tensor(
            buf, buf, mb, ALU.subtract),
            reads=[bkey, ("mb", "la"), ("sm",)], writes=[bkey, rkey],
            tag="lasub")
        pb.emit(eng, lambda e, t=t, buf=buf, rkey=rkey: e.tensor_tensor(
            xln[t], buf, rb, ALU.mult),
            reads=[bkey, rkey, ("rb", "la"), ("w2",)],
            writes=[("xln", t), ("A1",)], tag="laxq")
    pb.emit("sp", lambda e: e.dma_start(
        out=cx.dst[:, 0:2048],
        in_=dr["spanmask"].rearrange("p a c -> p (a c)")),
        writes=[("spm",), ("dstream", 0)], dma=True, tag="spmld")
    spm = lambda v: cx.dst[:, v * 512:(v + 1) * 512]
    pb.emit("sp", lambda e: e.dma_start(
        out=cx.cols[:, 32:40], in_=dr["l1_attn_bo"]),
        writes=[("bocol", "la"), ("cols",)], dma=True, tag="bola")

    qT = [cx.av(32768 + j * 4096, 2048, BF16) for j in range(4)]
    kT = [cx.av(49152 + j * 4096, 2048, BF16) for j in range(4)]
    ktok = [cx.av(65536 + t * 1024, 512, BF16) for t in range(16)]
    vaug = [cx.av(81920 + t * 1040, 520, BF16) for t in range(16)]
    Ssb = [cx.av(98560 + t * 1024, 512, BF16) for t in range(4)]
    oTs = [cx.av(114944 + j * 4096, 2048, BF16) for j in range(4)]
    ssb_pair = [cx.av(131328, 65, BF16, npart=64, p0=0),
                cx.av(131328, 65, BF16, npart=64, p0=64)]
    ssb32_pair = [cx.av(103680, 65, F32, npart=64, p0=0),
                  cx.av(103680, 65, F32, npart=64, p0=64)]
    drow = cx.av(132368, 512, F32, npart=1)
    osc = cx.av(102656, 512, BF16, npart=64)

    for half in range(2):
        dh0 = half * 512
        for pi, (wname, kind) in enumerate((("l1_attn_wq", "q"),
                                            ("l1_attn_wk", "k"))):
            for j in range(4):
                slot = cx.wst[:, (j % 4) * 1024:(j % 4) * 1024 + 1024]
                wkey = ("wsl", j % 4)
                buf = cx.w1 if j % 2 == 0 else cx.w0
                fkey = ("w1",) if j % 2 == 0 else ("w0",)
                pb.emit("sp" if j % 2 == 0 else "pool",
                        lambda e, wname=wname, j=j, dh0=dh0, buf=buf:
                        e.dma_start(out=buf[:, 0:1024],
                                    in_=dr[wname][dh0 // 128 + j]
                                    .rearrange("p a c -> p (a c)")),
                        writes=[("wf", j % 4), fkey], dma=True, tag="lwld")
                pb.emit("dve", lambda e, slot=slot, buf=buf: e.tensor_copy(
                    slot, buf[:, 0:1024]),
                    reads=[("wf", j % 4), fkey], writes=[wkey, ("wst",)],
                    tag="lwq")
                if kind in ("q", "k"):
                    tiles, tkey = (qT, "qT") if kind == "q" else (kT, "kTl1")
                    for a in range(8):
                        for s in range(4):
                            bank = (j % 2) * 4 + s
                            pb.emit("pe", lambda e, s=s, a=a, slot=slot,
                                    bank=bank:
                                    e.matmul(ps[bank][:, :],
                                             slot[:, a * 128:(a + 1) * 128],
                                             xln[a][:, s * 512:(s + 1) * 512],
                                             start=(a == 0), stop=(a == 7)),
                                    reads=[wkey, ("wst",), ("xln", a),
                                           ("A1",)],
                                    writes=[("ps", bank)], tag="prmm")
                    for s in range(4):
                        bank = (j % 2) * 4 + s
                        pb.emit("dve", lambda e, j=j, s=s, tiles=tiles,
                                bank=bank:
                                e.tensor_scalar(
                                    tiles[j][:, s * 512:(s + 1) * 512],
                                    ps[bank][:, :], 0.0, 0.001, ALU.max,
                                    ALU.add),
                                reads=[("ps", bank)],
                                writes=[(tkey, j), ("A2",)], tag="prev")
        def vevac(t, bank):
            pb.emit("dve", lambda e, t=t, bank=bank: e.tensor_copy(
                vaug[t].rearrange("p (h c) -> p h c", c=65)[:, :, 0:64],
                ps[bank][:, :].rearrange("p (h c) -> p h c", c=64)),
                reads=[("ps", bank)],
                writes=[("va", t), ("B1",), ("B2",)], tag="vev")
        _tokmaj_proj(cx, "l1_attn_wv2", half, xln, vevac, "av%d" % half)

        def ktevac(t, bank):
            pb.emit("dve", lambda e, t=t, bank=bank: e.tensor_scalar(
                ktok[t], ps[bank][:, :], 0.0, 0.001, ALU.max, ALU.add),
                reads=[("ps", bank)], writes=[("kk", t), ("B1",)],
                tag="kkev")
        _tokmaj_proj(cx, "l1_attn_wk2", half, xln, ktevac, "ak%d" % half)
        for t in range(16):
            pb.emit("dve", lambda e, t=t: e.memset(
                vaug[t].rearrange("p (h c) -> p h c", c=65)[:, :, 64:65], 1.0),
                writes=[("va", t), ("B1",), ("B2",)], tag="vone")
        for hl in range(8):
            hrow = (hl % 2) * 64
            jt = hl // 2
            ssb = ssb_pair[hl % 2]
            ssb32 = ssb32_pair[hl % 2]
            spsum = ps[6][hrow:hrow + 64, 0:65]
            for s in range(4):
                tisl = slice(s * 512, (s + 1) * 512)
                for v in range(4):
                    tj = 4 * s + v
                    bank = v % 3
                    pb.emit("pe", lambda e, tj=tj, v=v, bank=bank, jt=jt,
                            hrow=hrow, tisl=tisl: e.matmul(
                                ps[bank][:, :],
                                kT[jt][hrow:hrow + 64,
                                       tj * 128:(tj + 1) * 128],
                                qT[jt][hrow:hrow + 64, tisl],
                                start=True, stop=True),
                            reads=[("kTl1", jt), ("qT", jt), ("A2",)],
                            writes=[("ps", bank)], tag="smm")
                    pb.emit("dve", lambda e, v=v, bank=bank: e.tensor_tensor(
                        Ssb[v], ps[bank][:, :], spm(v), ALU.mult),
                        reads=[("ps", bank), ("spm",), ("dstream", 0)],
                        writes=[("Ssb", v), ("B2",)], tag="sevm")
                first = True
                if s > 0:
                    pb.emit("pe", lambda e, jt=jt, hrow=hrow, tisl=tisl,
                            ssb=ssb: e.matmul(
                                ps[4][0:65, :], ssb,
                                qT[jt][hrow:hrow + 64, tisl],
                                start=True, stop=False),
                            reads=[("ssb",), ("qT", jt), ("A2",), ("B2",)],
                            writes=[("ps", 4)], tag="ocross")
                    first = False
                for v in range(4):
                    tj = 4 * s + v
                    pb.emit("pe", lambda e, tj=tj, v=v, hl=hl,
                            st=(first and v == 0), sp_=(v == 3): e.matmul(
                                ps[4][0:65, :],
                                vaug[tj][:, hl * 65:(hl + 1) * 65],
                                Ssb[v], start=st, stop=sp_),
                            reads=[("va", tj), ("Ssb", v), ("B2",)],
                            writes=[("ps", 4)], tag="omm")
                pb.emit("dve", lambda e: e.tensor_copy(drow, ps[4][64:65, :]),
                        reads=[("ps", 4)], writes=[("drow",), ("B2",)],
                        tag="dev")
                pb.emit("act", lambda e: e.activation(drow, drow, AF.Ln),
                        reads=[("drow",), ("B2",)],
                        writes=[("drow",), ("B2",)], tag="dln")
                pb.emit("act", lambda e: e.activation(drow, drow, AF.Exp,
                                                      scale=-1.0),
                        reads=[("drow",), ("B2",)],
                        writes=[("drow",), ("B2",)], tag="dexp")
                pb.emit("pe", lambda e: e.matmul(
                    ps[5][:, :], cx.ones_row_f32, drow, start=True,
                    stop=True),
                    reads=[("drow",), ("B2",), ("c", "orf")],
                    writes=[("ps", 5)], tag="dbc")
                pb.emit("dve", lambda e: e.tensor_copy(
                    cx.w0b[0:64, 0:512], ps[4][0:64, :]),
                    reads=[("ps", 4)], writes=[("oev",), ("w0",)], tag="oev")
                if hl % 2 == 0:
                    pb.emit("dve", lambda e, jt=jt, tisl=tisl:
                            e.tensor_tensor(oTs[jt][0:64, tisl],
                                            cx.w0b[0:64, 0:512],
                                            ps[5][0:64, :], ALU.mult),
                            reads=[("oev",), ("w0",), ("ps", 5)],
                            writes=[("oTs", jt), ("B2",)], tag="oscl")
                else:
                    pb.emit("dve", lambda e: e.tensor_tensor(
                        osc, cx.w0b[0:64, 0:512], ps[5][0:64, :], ALU.mult),
                        reads=[("oev",), ("w0",), ("ps", 5)],
                        writes=[("osc",), ("B2",)], tag="oscl2")
                    pb.emit("pe", lambda e: e.matmul(
                        ps[3][:, :], cx.shift64_bf, osc, start=True,
                        stop=True),
                        reads=[("osc",), ("B2",), ("c", "sh64")],
                        writes=[("ps", 3)], tag="oshift")
                    pb.emit("dve", lambda e, jt=jt, tisl=tisl: e.tensor_copy(
                        oTs[jt][64:128, tisl], ps[3][64:128, :]),
                        reads=[("ps", 3)], writes=[("oTs", jt), ("B2",)],
                        tag="osev")
                for c in range(4):
                    tj = 4 * s + c
                    pb.emit("pe", lambda e, tj=tj, hl=hl, st=(c == 0),
                            sp_=(c == 3), spsum=spsum: e.matmul(
                                spsum,
                                ktok[tj][:, hl * 64:(hl + 1) * 64],
                                vaug[tj][:, hl * 65:(hl + 1) * 65],
                                start=st, stop=sp_),
                            reads=[("kk", tj), ("va", tj), ("B1",), ("B2",)],
                            writes=[("ps", 6)], tag="stup")
                if s < 3:
                    if s == 0:
                        pb.emit("dve", lambda e, ssb32=ssb32, spsum=spsum:
                                e.tensor_copy(ssb32, spsum),
                                reads=[("ps", 6)],
                                writes=[("ssb32",), ("B2",)], tag="st32i")
                    else:
                        pb.emit("dve", lambda e, ssb32=ssb32, spsum=spsum:
                                e.tensor_tensor(ssb32, ssb32, spsum, ALU.add),
                                reads=[("ps", 6), ("ssb32",)],
                                writes=[("ssb32",), ("B2",)], tag="st32a")
                    pb.emit("dve", lambda e, ssb=ssb, ssb32=ssb32:
                            e.tensor_copy(ssb, ssb32),
                            reads=[("ssb32",), ("B2",)],
                            writes=[("ssb",), ("B2",)], tag="stev")
        for dm in range(8):
            slot = cx.wst[:, 2048:2048 + 512]
            wkey = ("wslo",)
            pb.emit("sp", lambda e, dm=dm, dh0=dh0: e.dma_start(
                out=cx.w1[:, 0:512],
                in_=dr["l1_attn_wo"][dh0 // 512, dm].rearrange(
                    "p a c -> p (a c)")),
                writes=[("wfo",), ("w1",)], dma=True, tag="wold2")
            pb.emit("dve", lambda e, slot=slot: e.tensor_copy(
                slot, cx.w1[:, 0:512]),
                reads=[("wfo",), ("w1",)], writes=[wkey, ("wst",)], tag="woq2")
            pb.emit("sp", lambda e, dm=dm: e.dma_start(
                out=cx.w0, in_=cx.x_scr[dm * 128:(dm + 1) * 128, :]),
                reads=[("xscr", dm)], writes=[("lax",), ("w0",)], dma=True,
                tag="laxld")
            for j in range(4):
                for s in range(4):
                    pb.emit("pe", lambda e, s=s, j=j, slot=slot: e.matmul(
                        ps[s][:, :], slot[:, j * 128:(j + 1) * 128],
                        oTs[j][:, s * 512:(s + 1) * 512],
                        start=(j == 0), stop=(j == 3)),
                        reads=[wkey, ("wst",), ("oTs", j), ("B2",)],
                        writes=[("ps", s)], tag="wo2mm")
            for s in range(4):
                pb.emit("dve", lambda e, s=s: e.tensor_tensor(
                    cx.w0[:, s * 512:(s + 1) * 512],
                    cx.w0[:, s * 512:(s + 1) * 512], ps[s][:, :], ALU.add),
                    reads=[("ps", s), ("lax",)], writes=[("lax",), ("w0",)],
                    tag="laacc")
            if half == 0:
                pb.emit("dve", lambda e, dm=dm: e.tensor_scalar(
                    cx.w0, cx.w0, cx.cols[:, 32 + dm:33 + dm], None, ALU.add),
                    reads=[("lax",), ("bocol", "la"), ("cols",), ("w0",)],
                    writes=[("lax",), ("w0",)], tag="laboadd")
            pb.emit("pool", lambda e, dm=dm: e.dma_start(
                out=cx.x_scr[dm * 128:(dm + 1) * 128, :], in_=cx.w0),
                reads=[("lax",), ("w0",)], writes=[("xscr", dm)], dma=True,
                tag="laxst")


def _phase_l1_ffn(cx):
    pb, ps, dr = cx.pb, cx.ps, cx.dr
    mb = cx.sm[:, 0:2048]
    rb = cx.w2
    _ln_full(cx, cx.x_scr, lambda t: ("xscr", t), "lf", mb, rb,
             [("sm",)], [("w2",)])
    xln = [cx.av(t * 4096, 2048, BF16) for t in range(8)]
    for t in range(8):
        buf = cx.w0 if t % 2 == 0 else cx.w1
        rkey = ("w0",) if t % 2 == 0 else ("w1",)
        bkey = ("lfb", t % 2)
        pb.emit("sp", lambda e, t=t, buf=buf: e.dma_start(
            out=buf, in_=cx.x_scr[t * 128:(t + 1) * 128, :]),
            reads=[("xscr", t)], writes=[bkey, rkey], dma=True, tag="lfld")
        eng = "dve" if t % 2 == 0 else "pool"
        pb.emit(eng, lambda e, buf=buf: e.tensor_tensor(
            buf, buf, mb, ALU.subtract),
            reads=[bkey, ("mb", "lf"), ("sm",)], writes=[bkey, rkey],
            tag="lfsub")
        pb.emit(eng, lambda e, t=t, buf=buf, rkey=rkey: e.tensor_tensor(
            xln[t], buf, rb, ALU.mult),
            reads=[bkey, rkey, ("rb", "lf"), ("w2",)],
            writes=[("xln", t), ("A1",)], tag="lfxq")
    rw = cx.rw
    gate = rw[0:1, :]
    gm = cx.av(99584, 2048, F32, npart=1)
    trow = cx.cf32[0:1, 260:276]
    pb.emit("sp", lambda e: e.dma_start(
        out=cx.cols[:, 40:48], in_=dr["l1_ffn_wg"]),
        writes=[("wg",), ("cols",)], dma=True, tag="wgld")
    wgb = cx.cbf[:, 388:396]
    pb.emit("dve", lambda e: e.tensor_copy(wgb, cx.cols[:, 40:48]),
            reads=[("wg",), ("cols",)], writes=[("wgb",)], tag="wgq")
    for s in range(4):
        for a in range(8):
            pb.emit("pe", lambda e, s=s, a=a: e.matmul(
                ps[7][0:1, 0:512], wgb[:, a:a + 1],
                xln[a][:, s * 512:(s + 1) * 512], start=(a == 0),
                stop=(a == 7)),
                reads=[("wgb",), ("xln", a), ("A1",)],
                writes=[("ps", 7)], tag="gmm")
        pb.emit("dve", lambda e, s=s: e.tensor_copy(
            gate[:, s * 512:(s + 1) * 512], ps[7][0:1, 0:512]),
            reads=[("ps", 7)], writes=[("gate",), ("rw",)], tag="gev")
    pb.emit("act", lambda e: e.activation(gate, gate, AF.Sigmoid),
            reads=[("gate",), ("rw",)], writes=[("gate",), ("rw",)],
            tag="gsig")
    gcols = cx.cols[:, 72:88]
    for t in range(16):
        pb.emit("pe", lambda e, t=t: e.transpose(
            ps[6][:, 0:1], gate[:, t * 128:(t + 1) * 128],
            cx.ones_col_f32[0:1, 0:1]),
            reads=[("gate",), ("rw",), ("c", "ocf")], writes=[("ps", 6)],
            tag="gtr")
        pb.emit("dve", lambda e, t=t: e.tensor_copy(
            gcols[:, t:t + 1], ps[6][:, 0:1]),
            reads=[("ps", 6)], writes=[("gcols",), ("cols",)], tag="gcev")
    tbc = cx.cols[:, 88:89]
    cntc = cx.cols[:, 89:90]
    cmpt = cx.w1[:, 0:16]
    pb.emit("dve", lambda e: e.memset(trow[:, 0:1], 0.0),
            writes=[("bis",), ("cf32",)], tag="bi0")
    pb.emit("dve", lambda e: e.memset(trow[:, 1:2], 1.0),
            writes=[("bis",), ("cf32",)], tag="bi1")
    for it in range(24):
        pb.emit("dve", lambda e: e.tensor_tensor(
            trow[:, 2:3], trow[:, 0:1], trow[:, 1:2], ALU.add),
            reads=[("bis",)], writes=[("bis",), ("cf32",)], tag="bt1")
        pb.emit("dve", lambda e: e.tensor_single_scalar(
            trow[:, 2:3], trow[:, 2:3], 0.5, ALU.mult),
            reads=[("bis",)], writes=[("bis",), ("cf32",)], tag="bt2")
        pb.emit("pe", lambda e: e.matmul(
            ps[7][:, 0:1], cx.ones_row_f32, trow[:, 2:3], start=True,
            stop=True),
            reads=[("bis",), ("cf32",), ("c", "orf")], writes=[("ps", 7)],
            tag="btb")
        pb.emit("dve", lambda e: e.tensor_copy(tbc, ps[7][:, 0:1]),
                reads=[("ps", 7)], writes=[("tbc",), ("cols",)], tag="btc")
        pb.emit("dve", lambda e: e.tensor_single_scalar(
            cmpt, gcols, tbc, ALU.is_gt),
            reads=[("gcols",), ("tbc",), ("cols",)],
            writes=[("cmpt",), ("w1",)], tag="bt3")
        pb.emit("dve", lambda e: e.reduce_sum(
            cntc, cmpt, axis=mybir.AxisListType.X),
            reads=[("cmpt",), ("w1",)], writes=[("cntc",), ("cols",)],
            tag="bt4")
        pb.emit("pe", lambda e: e.matmul(
            ps[7][0:1, 0:1], cx.ones_col_f32, cntc, start=True, stop=True),
            reads=[("cntc",), ("cols",), ("c", "ocf")], writes=[("ps", 7)],
            tag="btm")
        pb.emit("dve", lambda e: e.tensor_copy(trow[:, 3:4], ps[7][0:1, 0:1]),
                reads=[("ps", 7)], writes=[("bis",), ("cf32",)], tag="btcnt")
        pb.emit("dve", lambda e: e.tensor_single_scalar(
            trow[:, 4:5], trow[:, 3:4], 1024.0, ALU.is_ge),
            reads=[("bis",)], writes=[("bis",), ("cf32",)], tag="bt5")
        pb.emit("dve", lambda e: e.tensor_tensor(
            trow[:, 5:6], trow[:, 2:3], trow[:, 0:1], ALU.subtract),
            reads=[("bis",)], writes=[("bis",), ("cf32",)], tag="bt6")
        pb.emit("dve", lambda e: e.tensor_tensor(
            trow[:, 5:6], trow[:, 5:6], trow[:, 4:5], ALU.mult),
            reads=[("bis",)], writes=[("bis",), ("cf32",)], tag="bt7")
        pb.emit("dve", lambda e: e.tensor_tensor(
            trow[:, 0:1], trow[:, 0:1], trow[:, 5:6], ALU.add),
            reads=[("bis",)], writes=[("bis",), ("cf32",)], tag="bt8")
        pb.emit("dve", lambda e: e.tensor_tensor(
            trow[:, 5:6], trow[:, 1:2], trow[:, 2:3], ALU.subtract),
            reads=[("bis",)], writes=[("bis",), ("cf32",)], tag="bt9")
        pb.emit("dve", lambda e: e.tensor_single_scalar(
            trow[:, 6:7], trow[:, 4:5], -1.0, ALU.mult),
            reads=[("bis",)], writes=[("bis",), ("cf32",)], tag="bt10")
        pb.emit("dve", lambda e: e.tensor_single_scalar(
            trow[:, 6:7], trow[:, 6:7], 1.0, ALU.add),
            reads=[("bis",)], writes=[("bis",), ("cf32",)], tag="bt11")
        pb.emit("dve", lambda e: e.tensor_tensor(
            trow[:, 5:6], trow[:, 5:6], trow[:, 6:7], ALU.mult),
            reads=[("bis",)], writes=[("bis",), ("cf32",)], tag="bt12")
        pb.emit("dve", lambda e: e.tensor_tensor(
            trow[:, 1:2], trow[:, 1:2], trow[:, 5:6], ALU.subtract),
            reads=[("bis",)], writes=[("bis",), ("cf32",)], tag="bt13")
    tmp = cx.av(107776, 2048, F32, npart=1)
    pb.emit("dve", lambda e: e.tensor_single_scalar(
        tmp, gate, trow[:, 0:1], ALU.is_gt),
        reads=[("gate",), ("bis",)], writes=[("mrow",), ("rw",)], tag="mrow")
    pb.emit("dve", lambda e: e.tensor_tensor(gm, gate, tmp, ALU.mult),
            reads=[("gate",), ("mrow",)], writes=[("gm",), ("rw",)],
            tag="gmrow")
    _tap(cx, "gate", gate, (1, 2048), F32, [("gate",), ("rw",)])
    _tap(cx, "maskrow", tmp, (1, 2048), F32, [("mrow",)])
    _tap(cx, "gmrow", gm, (1, 2048), F32, [("gm",)])
    for t in range(16):
        pb.emit("pe", lambda e, t=t: e.transpose(
            ps[6][:, 0:1].bitcast(F32) if False else ps[6][:, 0:1],
            tmp[:, t * 128:(t + 1) * 128], cx.ones_col_f32[0:1, 0:1]),
            reads=[("mrow",), ("rw",), ("c", "ocf")], writes=[("ps", 6)],
            tag="mtr")
        pb.emit("dve", lambda e, t=t: e.tensor_copy(
            cx.cols[:, 48 + t:49 + t], ps[6][:, 0:1]),
            reads=[("ps", 6)], writes=[("mcol", t), ("cols",)], tag="mcev")

    _tap(cx, "mcols", cx.cols[:, 48:64], (128, 16), F32,
         [("mcol", t) for t in range(16)] + [("cols",)])
    qT = [cx.av(32768 + j * 4096, 2048, BF16) for j in range(4)]
    kT = [cx.av(49152 + j * 4096, 2048, BF16) for j in range(4)]
    vaug = [cx.av(65536 + t * 1040, 520, BF16) for t in range(16)]
    ST = [cx.av(82176 + t * 1024, 512, BF16) for t in range(16)]
    oTs = [cx.av(115968 + j * 4096, 2048, BF16) for j in range(4)]
    drow = cx.av(132352, 512, F32, npart=1)
    osc = cx.av(98560, 512, BF16, npart=64)

    for half in range(2):
        dh0 = half * 512
        for pi, (wname, kind) in enumerate((("l1_ffn_wq", "q"),
                                            ("l1_ffn_wk", "k"))):
            for j in range(4):
                slot = cx.wst[:, (j % 4) * 1024:(j % 4) * 1024 + 1024]
                wkey = ("wslf", j % 4)
                buf = cx.w1 if j % 2 == 0 else cx.w0
                fkey = ("w1",) if j % 2 == 0 else ("w0",)
                pb.emit("sp" if j % 2 == 0 else "pool",
                        lambda e, wname=wname, j=j, dh0=dh0, buf=buf:
                        e.dma_start(out=buf[:, 0:1024],
                                    in_=dr[wname][dh0 // 128 + j]
                                    .rearrange("p a c -> p (a c)")),
                        writes=[("wff", j % 4), fkey], dma=True, tag="fwld")
                pb.emit("dve", lambda e, slot=slot, buf=buf: e.tensor_copy(
                    slot, buf[:, 0:1024]),
                    reads=[("wff", j % 4), fkey], writes=[wkey, ("wst",)],
                    tag="fwq")
                if kind in ("q", "k"):
                    tiles, tkey = (qT, "qTf") if kind == "q" else (kT, "kTf")
                    scl = 1.0 if kind == "q" else 0.125
                    for a in range(8):
                        for s in range(4):
                            bank = (j % 2) * 4 + s
                            pb.emit("pe", lambda e, s=s, a=a, slot=slot,
                                    bank=bank:
                                    e.matmul(ps[bank][:, :],
                                             slot[:, a * 128:(a + 1) * 128],
                                             xln[a][:, s * 512:(s + 1) * 512],
                                             start=(a == 0), stop=(a == 7)),
                                    reads=[wkey, ("wst",), ("xln", a),
                                           ("A1",)],
                                    writes=[("ps", bank)], tag="fprmm")
                    for s in range(4):
                        bank = (j % 2) * 4 + s
                        pb.emit("dve", lambda e, j=j, s=s, tiles=tiles,
                                scl=scl, bank=bank: e.tensor_single_scalar(
                                    tiles[j][:, s * 512:(s + 1) * 512],
                                    ps[bank][:, :], scl, ALU.mult),
                                reads=[("ps", bank)],
                                writes=[(tkey, j), ("A2",)], tag="fprev")
        def fvevac(t, bank):
            pb.emit("dve", lambda e, t=t, bank=bank: e.tensor_scalar(
                vaug[t].rearrange("p (h c) -> p h c", c=65)[:, :, 0:64],
                ps[bank][:, :].rearrange("p (h c) -> p h c", c=64),
                cx.cols[:, 48 + t:49 + t], None, ALU.mult),
                reads=[("ps", bank), ("mcol", t), ("cols",)],
                writes=[("vaf", t), ("B1",), ("B2",)], tag="fvev")
        _tokmaj_proj(cx, "l1_ffn_wv2", half, xln, fvevac, "fv%d" % half)
        for t in range(16):
            pb.emit("dve", lambda e, t=t: e.tensor_copy(
                vaug[t].rearrange("p (h c) -> p h c", c=65)[:, :, 64:65],
                cx.cols[:, 48 + t:49 + t].unsqueeze(2).broadcast_to(
                    (128, 1, 8)).rearrange("p one h -> p h one")
                if False else
                cx.cols[:, 48 + t:49 + t].unsqueeze(1).broadcast_to(
                    (128, 8, 1))),
                reads=[("mcol", t), ("cols",)],
                writes=[("vaf", t), ("B1",), ("B2",)], tag="fvone")
        iters = []
        for hl in range(8):
            hrow = (hl % 2) * 64
            jt = hl // 2
            for s in range(4):
                tisl = slice(s * 512, (s + 1) * 512)
                A, B, C = [], [], []
                emA = lambda *a, **k: A.append((a, k))
                emB = lambda *a, **k: B.append((a, k))
                emC = lambda *a, **k: C.append((a, k))
                for tj in range(16):
                    bank = tj % 3
                    emA("pe", lambda e, tj=tj, bank=bank, jt=jt,
                        hrow=hrow, tisl=tisl: e.matmul(
                            ps[bank][:, :],
                            kT[jt][hrow:hrow + 64,
                                   tj * 128:(tj + 1) * 128],
                            qT[jt][hrow:hrow + 64, tisl],
                            start=True, stop=True),
                        reads=[("kTf", jt), ("qTf", jt), ("A2",)],
                        writes=[("ps", bank)], tag="fsmm")
                    emA("act", lambda e, tj=tj, bank=bank: e.activation(
                        ST[tj], ps[bank][:, :], AF.Exp),
                        reads=[("ps", bank)],
                        writes=[("ST", tj), ("B1",), ("B2",)], tag="fexp")
                for tj in range(16):
                    emB("pe", lambda e, tj=tj, hl=hl: e.matmul(
                        ps[4][0:65, :], vaug[tj][:, hl * 65:(hl + 1) * 65],
                        ST[tj], start=(tj == 0), stop=(tj == 15)),
                        reads=[("vaf", tj), ("ST", tj), ("B1",), ("B2",)],
                        writes=[("ps", 4)], tag="fomm")
                emC("dve", lambda e: e.tensor_copy(drow, ps[4][64:65, :]),
                    reads=[("ps", 4)], writes=[("drow",), ("B2",)],
                    tag="fdev")
                emC("act", lambda e: e.activation(drow, drow, AF.Ln),
                    reads=[("drow",), ("B2",)],
                    writes=[("drow",), ("B2",)], tag="fdln")
                emC("act", lambda e: e.activation(drow, drow, AF.Exp,
                                                  scale=-1.0),
                    reads=[("drow",), ("B2",)],
                    writes=[("drow",), ("B2",)], tag="fdexp")
                emC("dve", lambda e, tisl=tisl: e.tensor_tensor(
                    drow, drow, gm[:, tisl], ALU.mult),
                    reads=[("drow",), ("gm",), ("rw",), ("B2",)],
                    writes=[("drow",), ("B2",)], tag="fdgm")
                emC("pe", lambda e: e.matmul(
                    ps[5][:, :], cx.ones_row_f32, drow, start=True,
                    stop=True),
                    reads=[("drow",), ("B2",), ("c", "orf")],
                    writes=[("ps", 5)], tag="fdbc")
                emC("dve", lambda e: e.tensor_copy(
                    cx.w0b[0:64, 0:512], ps[4][0:64, :]),
                    reads=[("ps", 4)], writes=[("foev",), ("w0",)],
                    tag="foev")
                if hl % 2 == 0:
                    emC("dve", lambda e, jt=jt, tisl=tisl:
                        e.tensor_tensor(oTs[jt][0:64, tisl],
                                        cx.w0b[0:64, 0:512],
                                        ps[5][0:64, :], ALU.mult),
                        reads=[("foev",), ("w0",), ("ps", 5)],
                        writes=[("oTf", jt), ("B2",)], tag="foscl")
                else:
                    emC("dve", lambda e: e.tensor_tensor(
                        osc, cx.w0b[0:64, 0:512], ps[5][0:64, :], ALU.mult),
                        reads=[("foev",), ("w0",), ("ps", 5)],
                        writes=[("oscf",), ("B2",)], tag="foscl2")
                    emC("pe", lambda e: e.matmul(
                        ps[3][:, :], cx.shift64_bf, osc, start=True,
                        stop=True),
                        reads=[("oscf",), ("B2",), ("c", "sh64")],
                        writes=[("ps", 3)], tag="foshift")
                    emC("dve", lambda e, jt=jt, tisl=tisl: e.tensor_copy(
                        oTs[jt][64:128, tisl], ps[3][64:128, :]),
                        reads=[("ps", 3)], writes=[("oTf", jt), ("B2",)],
                        tag="fosev")
                iters.append((A, B, C))
        prevC = None
        for A, B, C in iters:
            for a, k in A:
                pb.emit(*a, **k)
            if prevC is not None:
                for a, k in prevC:
                    pb.emit(*a, **k)
            for a, k in B:
                pb.emit(*a, **k)
            prevC = C
        for a, k in prevC:
            pb.emit(*a, **k)
        for dm in range(8):
            slot = cx.wst[:, 2048:2048 + 512]
            wkey = ("wslof",)
            pb.emit("sp", lambda e, dm=dm, dh0=dh0: e.dma_start(
                out=cx.w1[:, 0:512],
                in_=dr["l1_ffn_wo"][dh0 // 512, dm].rearrange(
                    "p a c -> p (a c)")),
                writes=[("wfof",), ("w1",)], dma=True, tag="fwold")
            pb.emit("dve", lambda e, slot=slot: e.tensor_copy(
                slot, cx.w1[:, 0:512]),
                reads=[("wfof",), ("w1",)], writes=[wkey, ("wst",)],
                tag="fwoq")
            pb.emit("sp", lambda e, dm=dm: e.dma_start(
                out=cx.w0, in_=cx.x_scr[dm * 128:(dm + 1) * 128, :]),
                reads=[("xscr", dm)], writes=[("lfx",), ("w0",)], dma=True,
                tag="fxrld2")
            for j in range(4):
                for s in range(4):
                    pb.emit("pe", lambda e, s=s, j=j, slot=slot: e.matmul(
                        ps[s][:, :], slot[:, j * 128:(j + 1) * 128],
                        oTs[j][:, s * 512:(s + 1) * 512],
                        start=(j == 0), stop=(j == 3)),
                        reads=[wkey, ("wst",), ("oTf", j), ("B2",)],
                        writes=[("ps", s)], tag="fwomm")
            for s in range(4):
                pb.emit("dve", lambda e, s=s: e.tensor_tensor(
                    cx.w0[:, s * 512:(s + 1) * 512],
                    cx.w0[:, s * 512:(s + 1) * 512], ps[s][:, :], ALU.add),
                    reads=[("ps", s), ("lfx",)], writes=[("lfx",), ("w0",)],
                    tag="flacc")
            pb.emit("pool", lambda e, dm=dm: e.dma_start(
                out=cx.x_scr[dm * 128:(dm + 1) * 128, :], in_=cx.w0),
                reads=[("lfx",), ("w0",)], writes=[("xscr", dm)], dma=True,
                tag="fxst2")



def kernel(**inputs):
    per_core = _host_prep(inputs)
    nc, cx = build_program()
    res = run_bass_kernel_spmd(nc, per_core, core_ids=list(range(NCORES)))
    outs = [np.asarray(r["out_fm"]).T for r in res.results]
    return np.stack(outs, axis=0).astype(np.float32)


# version 22
# speedup vs baseline: 1.0024x; 1.0024x over previous
import math
import numpy as np
import ml_dtypes
import concourse.bass as bass
import concourse.mybir as mybir
from concourse.bass_utils import run_bass_kernel_spmd

F32 = mybir.dt.float32
BF16 = mybir.dt.bfloat16
FP8 = mybir.dt.float8e5
I32 = mybir.dt.int32
AF = mybir.ActivationFunctionType
ALU = mybir.AluOpType

D = 1024
T = 2048
NCORES = 8
FFTN = 4096
FT = 17
LT = 16

_DEBUG_TAPS = []
_PHASES = ("l0_attn", "l0_ffn", "l1_attn", "l1_ffn")
_NLAYERS = 4
_LAYER_LIST = None


_STREAMS = ("pe", "dve", "act", "pool", "sp")
_DMA_QUEUES = ("sp", "pool", "act")
_NSLOT = 8


class _Op:
    pass


class PB:
    serialize_all = False

    def __init__(self):
        self.ops = []
        self._prev = None
        self.last_writer = {}
        self.readers = {}
        self.qops = {q: [] for q in _DMA_QUEUES}

    def emit(self, stream, fn, reads=(), writes=(), dma=False, tag=""):
        op = _Op()
        op.stream, op.fn, op.is_dma, op.tag = stream, fn, dma, tag
        deps = []
        for k in reads:
            w = self.last_writer.get(k)
            if w is not None:
                deps.append(w)
        for k in writes:
            w = self.last_writer.get(k)
            if w is not None:
                deps.append(w)
            deps.extend(self.readers.get(k, ()))
        if dma:
            assert stream in _DMA_QUEUES, stream
            k = len(self.qops[stream])
            if k >= _NSLOT:
                deps.append(self.qops[stream][k - _NSLOT])
            self.qops[stream].append(op)
        if PB.serialize_all and self._prev is not None:
            deps.append(self._prev)
        self._prev = op
        op.deps = [d for d in dict.fromkeys(deps) if d is not op]
        for k in reads:
            self.readers.setdefault(k, []).append(op)
        for k in writes:
            self.last_writer[k] = op
            self.readers[k] = []
        self.ops.append(op)
        return op

    def finalize(self):
        ctrs = list(_STREAMS) + [("q", q, s) for q in _DMA_QUEUES
                                 for s in range(_NSLOT)]
        self.ctrs = ctrs
        self.cidx = {c: i for i, c in enumerate(ctrs)}
        NC = len(ctrs)
        qseq = {q: 0 for q in _DMA_QUEUES}
        seqs = {s: 0 for s in _STREAMS}
        for op in self.ops:
            if op.is_dma:
                k = qseq[op.stream]
                qseq[op.stream] = k + 1
                op.ctr = self.cidx[("q", op.stream, k % _NSLOT)]
                op.cval = (k // _NSLOT + 1) * 16
            else:
                seqs[op.stream] += 1
                op.ctr = self.cidx[op.stream]
                op.cval = seqs[op.stream]
        cur_done = {s: [0] * NC for s in _STREAMS}
        stream_retire = {s: [0] * NC for s in _STREAMS}
        waited = [set() for _ in range(NC)]
        for op in self.ops:
            s = op.stream
            vc = list(cur_done[s])
            op.waits = []
            for d in op.deps:
                if vc[d.ctr] < d.cval:
                    op.waits.append(d)
                    waited[d.ctr].add(d.cval)
                    dc = d.inherit
                    for j in range(NC):
                        if dc[j] > vc[j]:
                            vc[j] = dc[j]
                    if d.cval > vc[d.ctr]:
                        vc[d.ctr] = d.cval
            cur_done[s] = vc
            if op.is_dma:
                inh = list(vc)
                if op.cval > inh[op.ctr]:
                    inh[op.ctr] = op.cval
                op.inherit = inh
            else:
                inh = list(stream_retire[s])
                for j in range(NC):
                    if vc[j] > inh[j]:
                        inh[j] = vc[j]
                if op.cval > inh[op.ctr]:
                    inh[op.ctr] = op.cval
                op.inherit = inh
                stream_retire[s] = inh
        self.rank = {}
        for s in _STREAMS:
            j = self.cidx[s]
            self.rank[j] = {v: k + 1 for k, v in enumerate(sorted(waited[j]))}
        for op in self.ops:
            op.inc = op.is_dma or (op.cval in waited[op.ctr])

    def wait_val(self, d):
        return d.cval if d.is_dma else self.rank[d.ctr][d.cval]

    def inc_val(self, op):
        return 16 if op.is_dma else 1


def _keep(nc, cm):
    obj = cm.__enter__()
    nc._open_cms = getattr(nc, "_open_cms", [])
    nc._open_cms.append(cm)
    return obj



def _host_constants():
    c = {}
    l = np.arange(T, dtype=np.float64)
    f = np.arange(FT * 128, dtype=np.float64)
    ang = np.outer(l, f) * (2 * np.pi / FFTN)
    c["dftc"] = np.ascontiguousarray(
        np.cos(ang).reshape(LT, 128, FT, 128).transpose(2, 1, 0, 3)).astype(
        ml_dtypes.bfloat16)
    c["dfts"] = np.ascontiguousarray(
        (-np.sin(ang)).reshape(LT, 128, FT, 128).transpose(2, 1, 0, 3)).astype(
        ml_dtypes.bfloat16)
    wf = np.zeros(FT * 128)
    wf[0] = 1.0
    wf[1:2048] = 2.0
    wf[2048] = 1.0
    ang2 = np.outer(f, l) * (2 * np.pi / FFTN)
    c["idftc"] = np.ascontiguousarray(
        ((wf[:, None] * np.cos(ang2)) / FFTN).reshape(FT, 128, LT, 128)
        .transpose(2, 1, 0, 3)).astype(ml_dtypes.bfloat16)
    c["idfts"] = np.ascontiguousarray(
        ((-wf[:, None] * np.sin(ang2)) / FFTN).reshape(FT, 128, LT, 128)
        .transpose(2, 1, 0, 3)).astype(ml_dtypes.bfloat16)
    c["ident_bf"] = np.eye(128, dtype=ml_dtypes.bfloat16)
    c["ident_f32"] = np.eye(128, dtype=np.float32)
    tri = (np.arange(128)[None, :] >= np.arange(128)[:, None])
    c["triu_bf"] = tri.astype(ml_dtypes.bfloat16)
    c["ones_col_bf"] = np.ones((128, 1), dtype=ml_dtypes.bfloat16)
    tri = (np.arange(128)[None, :] >= np.arange(128)[:, None]).astype(np.float32)
    sh = np.zeros((64, 128), dtype=np.float32)
    sh[np.arange(64), 64 + np.arange(64)] = 1.0
    c["shift64_bf"] = sh.astype(ml_dtypes.bfloat16)
    sm_ = np.zeros((4, 128, 512), dtype=np.float32)
    for v in range(4):
        sm_[v, :, v * 128:(v + 1) * 128] = tri
        sm_[v, :, (v + 1) * 128:] = 1.0
    c["spanmask"] = np.ascontiguousarray(sm_.transpose(1, 0, 2)).astype(
        ml_dtypes.bfloat16)
    c["ones_col_f32"] = np.ones((128, 1), dtype=np.float32)
    c["ones_row_f32"] = np.ones((1, 128), dtype=np.float32)
    return c


def _host_prep(inputs):
    x = np.asarray(inputs["L0_attn_x"])
    outd = {}

    def cg(a):
        a = np.asarray(a)
        return np.ascontiguousarray(
            a.reshape(4, 4, 256, 32).transpose(0, 1, 3, 2).reshape(4, 128, 256)
        ).astype(np.float32)

    outd["are_cg"] = cg(inputs["L0_attn_A_re_log"])
    outd["aim_cg"] = cg(inputs["L0_attn_A_im"])
    outd["cre_cg"] = cg(inputs["L0_attn_C_re"])
    outd["cim_cg"] = cg(inputs["L0_attn_C_im"])
    logdt = np.asarray(inputs["L0_attn_log_dt"])
    outd["logdt_cg"] = np.ascontiguousarray(
        np.repeat(logdt.reshape(4, 4, 1, 256), 32, axis=2).reshape(4, 128, 256)
    ).astype(np.float32)
    outd["dskip_cg"] = np.ascontiguousarray(
        np.asarray(inputs["L0_attn_D_skip"]).reshape(4, 4, 256)).astype(np.float32)
    outd["ln_w"] = np.ascontiguousarray(
        np.asarray(inputs["L0_attn_ln_w"]).reshape(4, 8, 128)
        .transpose(0, 2, 1)).astype(np.float32)
    outd["ln_b"] = np.ascontiguousarray(
        np.asarray(inputs["L0_attn_ln_b"]).reshape(4, 8, 128)
        .transpose(0, 2, 1)).astype(np.float32)
    wot = np.asarray(inputs["L0_attn_W_out"]).transpose(0, 2, 1)
    outd["w_out_t"] = np.ascontiguousarray(
        wot.reshape(4, 8, 128, 16, 128).transpose(0, 3, 2, 1, 4)).astype(
        np.float32)
    outd["b_out"] = np.ascontiguousarray(
        np.asarray(inputs["L0_attn_b_out"]).reshape(4, 16, 128)
        .transpose(0, 2, 1)).astype(np.float32)
    w1t = np.asarray(inputs["L0_ffn_W1"]).T
    outd["w1_t"] = np.ascontiguousarray(
        w1t.reshape(8, 128, 32, 128).transpose(2, 1, 0, 3)).astype(np.float32)
    outd["b1"] = np.ascontiguousarray(
        np.asarray(inputs["L0_ffn_b1"]).reshape(32, 128).T).astype(np.float32)
    w2t = np.asarray(inputs["L0_ffn_W2"]).T
    outd["w2_t"] = np.ascontiguousarray(
        w2t.reshape(2, 16, 128, 8, 128).transpose(3, 0, 2, 1, 4)).astype(
        np.float32)
    outd["b2"] = np.ascontiguousarray(
        np.asarray(inputs["L0_ffn_b2"]).reshape(8, 128).T).astype(np.float32)
    for k in ("L1_attn_Wq", "L1_attn_Wk", "L1_attn_Wv",
              "L1_ffn_Wq", "L1_ffn_Wk", "L1_ffn_Wv"):
        w = np.asarray(inputs[k])
        outd[k.lower()] = np.ascontiguousarray(
            w.reshape(8, 128, 8, 128).transpose(2, 1, 0, 3)).astype(np.float32)
    for k in ("L1_attn_Wv", "L1_attn_Wk", "L1_ffn_Wv"):
        w = np.asarray(inputs[k])
        outd[k.lower() + "2"] = np.ascontiguousarray(
            w.reshape(8, 128, 2, 512).transpose(2, 1, 0, 3)).astype(
            np.float32)
    for k in ("L1_attn_Wo", "L1_ffn_Wo"):
        w = np.asarray(inputs[k])
        outd[k.lower()] = np.ascontiguousarray(
            w.reshape(2, 4, 128, 8, 128).transpose(0, 3, 2, 1, 4)).astype(
            np.float32)
    outd["l1_attn_bo"] = np.ascontiguousarray(
        np.asarray(inputs["L1_attn_bo"]).reshape(8, 128).T).astype(np.float32)
    outd["l1_ffn_wg"] = np.ascontiguousarray(
        np.asarray(inputs["L1_ffn_Wg"]).reshape(8, 128).T).astype(np.float32)
    import jax
    with jax.default_device(jax.devices("cpu")[0]):
        keep = np.asarray(jax.random.bernoulli(
            jax.random.key(42, impl="threefry2x32"), 0.8, (8, T, D)))
    maskv = keep.astype(np.float32) * 1.25
    consts = _host_constants()
    per_core = []
    for ci in range(NCORES):
        d = dict(outd)
        d.update(consts)
        d["x_fm"] = np.ascontiguousarray(x[ci].T).astype(np.float32)
        d["dropmask_fm"] = np.ascontiguousarray(maskv[ci].T).astype(
            ml_dtypes.float8_e5m2)
        per_core.append(d)
    return per_core



class Ctx:
    pass


def build_program():
    nc = bass.Bass("TRN2", target_bir_lowering=False, debug=False)
    pb = PB()
    cx = Ctx()
    cx.nc, cx.pb = nc, pb

    dr = {}

    def din(name, shape, dtype=F32):
        dr[name] = nc.dram_tensor(name, list(shape), dtype,
                                  kind="ExternalInput").ap()

    din("x_fm", (D, T))
    for nm, shp in [("are_cg", (4, 128, 256)), ("aim_cg", (4, 128, 256)),
                    ("cre_cg", (4, 128, 256)), ("cim_cg", (4, 128, 256)),
                    ("logdt_cg", (4, 128, 256)), ("dskip_cg", (4, 4, 256)),
                    ("ln_w", (4, 128, 8)), ("ln_b", (4, 128, 8)),
                    ("w_out_t", (4, 16, 128, 8, 128)), ("b_out", (4, 128, 16)),
                    ("w1_t", (32, 128, 8, 128)), ("b1", (128, 32)),
                    ("w2_t", (8, 2, 128, 16, 128)), ("b2", (128, 8)),
                    ("l1_attn_wq", (8, 128, 8, 128)),
                    ("l1_attn_wk", (8, 128, 8, 128)),
                    ("l1_attn_wv", (8, 128, 8, 128)),
                    ("l1_attn_wo", (2, 8, 128, 4, 128)),
                    ("l1_attn_bo", (128, 8)), ("l1_ffn_wg", (128, 8)),
                    ("l1_ffn_wq", (8, 128, 8, 128)),
                    ("l1_ffn_wk", (8, 128, 8, 128)),
                    ("l1_ffn_wv", (8, 128, 8, 128)),
                    ("l1_ffn_wo", (2, 8, 128, 4, 128)),
                    ("l1_attn_wv2", (2, 128, 8, 512)),
                    ("l1_attn_wk2", (2, 128, 8, 512)),
                    ("l1_ffn_wv2", (2, 128, 8, 512))]:
        din(nm, shp)
    for nm, shp in [("dftc", (FT, 128, LT, 128)), ("dfts", (FT, 128, LT, 128)),
                    ("idftc", (LT, 128, FT, 128)), ("idfts", (LT, 128, FT, 128)),
                    ("ident_bf", (128, 128)), ("triu_bf", (128, 128)),
                    ("ones_col_bf", (128, 1)),
                    ("spanmask", (128, 4, 512)),
                    ("shift64_bf", (64, 128))]:
        din(nm, shp, BF16)
    din("ident_f32", (128, 128))
    din("ones_col_f32", (128, 1))
    din("ones_row_f32", (1, 128))
    din("dropmask_fm", (D, T), FP8)
    cx.dr = dr
    cx.out = nc.dram_tensor("out_fm", [D, T], F32, kind="ExternalOutput").ap()
    cx.h_dram = nc.dram_tensor("h_scr", [D, T], F32, kind="Internal").ap()
    cx.x_scr = nc.dram_tensor("x_scr", [D, T], F32, kind="Internal").ap()
    cx.k_scr = nc.dram_tensor("k_scr", [T, D], BF16, kind="Internal").ap()

    cx.AR = _keep(nc, nc.sbuf_tensor("AR", [128, 97792], BF16))
    cx.rw = _keep(nc, nc.sbuf_tensor("rw", [128, 2048], F32))
    cx.cols = _keep(nc, nc.sbuf_tensor("cols", [128, 96], F32))
    cx.cbf = _keep(nc, nc.sbuf_tensor("cbf", [128, 512], BF16))
    cx.cf32 = _keep(nc, nc.sbuf_tensor("cf32", [128, 288], F32))
    cx.ps = [_keep(nc, nc.psum_tensor("ps%d" % i, [128, 512], F32))
             for i in range(8)]

    def av(byte_off, cols, dtype=BF16, npart=128, p0=0):
        esz = mybir.dt.size(dtype)
        assert byte_off % 2 == 0 and (byte_off + cols * esz) <= 97792 * 2
        v = cx.AR[p0:p0 + npart, byte_off // 2: (byte_off + cols * esz) // 2]
        if dtype != BF16:
            v = v.bitcast(dtype)
        return v

    cx.av = av
    cx.w0 = av(135168, 2048, F32)
    cx.w1 = av(143360, 2048, F32)
    cx.w2 = av(151552, 2048, F32)
    cx.sm = av(159744, 2560, F32)
    cx.wst = av(169984, 4096, BF16)
    cx.dst = av(178176, 8704, BF16)
    cx.w0b = av(135168, 4096, BF16)
    cx.w1b = av(143360, 4096, BF16)
    cx.w2b = av(151552, 4096, BF16)

    cx.shift64_bf = cx.cbf[0:64, 260:388]
    cx.ident_bf = cx.cbf[:, 0:128]
    cx.triu_bf = cx.cbf[:, 128:256]
    cx.ones_col_bf = cx.cbf[:, 256:257]
    cx.ident_f32 = cx.cf32[:, 0:128]
    cx.ones_col_f32 = cx.cf32[:, 128:129]
    cx.ones_row_f32 = cx.cf32[0:1, 130:258]

    _build_body(cx)
    pb.finalize()

    sems = {}
    for i, c in enumerate(pb.ctrs):
        name = c if isinstance(c, str) else "%s_%d" % (c[1], c[2])
        sems[i] = _keep(nc, nc.semaphore("S_" + name))

    with nc.Block() as block:
        def runner(s):
            def go(eng):
                for op in pb.ops:
                    if op.stream != s:
                        continue
                    for d in op.waits:
                        eng.wait_ge(sems[d.ctr], pb.wait_val(d))
                    inst = op.fn(eng)
                    if op.inc:
                        assert inst is not None, "inc on None op: " + op.tag
                        inst.then_inc(sems[op.ctr], pb.inc_val(op))
            return go
        block.tensor(runner("pe"))
        block.vector(runner("dve"))
        block.scalar(runner("act"))
        block.gpsimd(runner("pool"))
        block.sync(runner("sp"))
    return nc, cx


def _tap(cx, name, src_ap, shape, dtype, read_keys):
    if name not in _DEBUG_TAPS:
        return
    t = cx.nc.dram_tensor("tap_" + name, list(shape), dtype,
                          kind="ExternalOutput").ap()
    cx.pb.emit("pool", lambda e: e.dma_start(out=t, in_=src_ap),
               reads=list(read_keys), writes=[("tap", name)], dma=True,
               tag="tap_" + name)



def _build_body(cx):
    pb, dr = cx.pb, cx.dr
    pb.emit("sp", lambda e: e.dma_start(out=cx.shift64_bf,
                                         in_=dr["shift64_bf"]),
            writes=[("c", "sh64")], dma=True, tag="ldc_sh64")
    for src, dst, key in [("ident_bf", cx.ident_bf, "idb"),
                          ("triu_bf", cx.triu_bf, "tri"),
                          ("ones_col_bf", cx.ones_col_bf, "ocb"),
                          ("ident_f32", cx.ident_f32, "idf"),
                          ("ones_col_f32", cx.ones_col_f32, "ocf"),
                          ("ones_row_f32", cx.ones_row_f32, "orf")]:
        pb.emit("sp", lambda e, s=src, d=dst: e.dma_start(out=d, in_=dr[s]),
                writes=[("c", key)], dma=True, tag="ldc_" + key)
    for t in range(8):
        pb.emit(("sp", "pool", "act")[t % 3], lambda e, t=t: e.dma_start(
            out=cx.x_scr[t * 128:(t + 1) * 128, :],
            in_=dr["x_fm"][t * 128:(t + 1) * 128, :]),
            writes=[("xscr", t)], dma=True, tag="xinit")

    if "l0_attn" in _PHASES:
        _phase_l0_attn(cx)
    if "l0_ffn" in _PHASES:
        _phase_l0_ffn(cx)
    if "l1_attn" in _PHASES:
        _phase_l1_attn(cx)
    if "l1_ffn" in _PHASES:
        _phase_l1_ffn(cx)

    for t in range(8):
        pb.emit("sp", lambda e, t=t: e.dma_start(
            out=cx.out[t * 128:(t + 1) * 128, :],
            in_=cx.x_scr[t * 128:(t + 1) * 128, :]),
            reads=[("xscr", t)], writes=[("outhbm", t)], dma=True, tag="st_out")
    pb.emit("dve", lambda e: e.engine_nop(),
            reads=[("outhbm", t) for t in range(8)], tag="final")



def _ln_full(cx, src_dram, src_keys, tag, mb, rb, mbk, rbk):
    pb, ps = cx.pb, cx.ps
    sq = cx.av(178176, 2048, F32)
    for t in range(8):
        buf = cx.w0 if t % 2 == 0 else cx.w1
        rkey = ("w0",) if t % 2 == 0 else ("w1",)
        bkey = ("lnbuf", tag, t % 2)
        pb.emit("sp" if t % 2 == 0 else "pool",
                lambda e, t=t, buf=buf: e.dma_start(
                    out=buf, in_=src_dram[t * 128:(t + 1) * 128, :]),
                reads=[src_keys(t)], writes=[bkey, rkey], dma=True,
                tag="lnld" + tag)
        pb.emit("act", lambda e, buf=buf: e.activation(sq, buf, AF.Square),
                reads=[bkey, rkey], writes=[("lnsq", tag), ("dstream", 0)],
                tag="lnsq")
        for s in range(4):
            pb.emit("pe", lambda e, t=t, s=s, buf=buf: e.matmul(
                ps[s][0:1, 0:512], cx.ones_col_f32,
                buf[:, s * 512:(s + 1) * 512], start=(t == 0), stop=(t == 7)),
                reads=[bkey, rkey, ("c", "ocf")], writes=[("ps", s)],
                tag="lns")
            pb.emit("pe", lambda e, t=t, s=s: e.matmul(
                ps[4 + s][0:1, 0:512], cx.ones_col_f32,
                sq[:, s * 512:(s + 1) * 512], start=(t == 0), stop=(t == 7)),
                reads=[("lnsq", tag), ("dstream", 0), ("c", "ocf")],
                writes=[("ps", 4 + s)], tag="lns2")
    _ln_finish(cx, tag, mb, rb, mbk, rbk)


def _ln_finish(cx, tag, mb, rb, mbk, rbk):
    pb, ps = cx.pb, cx.ps
    mrow = mb[0:1, :]
    vrow = rb[0:1, :]
    for s in range(4):
        sl = slice(s * 512, (s + 1) * 512)
        pb.emit("dve", lambda e, sl=sl, s=s: e.tensor_scalar_mul(
            mrow[:, sl], ps[s][0:1, 0:512], 1.0 / D),
            reads=[("ps", s)], writes=[("r_mean", tag)] + mbk, tag="mean")
        pb.emit("dve", lambda e, sl=sl, s=s: e.tensor_scalar_mul(
            vrow[:, sl], ps[4 + s][0:1, 0:512], 1.0 / D),
            reads=[("ps", 4 + s)], writes=[("r_var", tag)] + rbk, tag="ex2")
        pb.emit("dve", lambda e, sl=sl, s=s: e.tensor_tensor(
            ps[s][0:1, 0:512], mrow[:, sl], mrow[:, sl], ALU.mult),
            reads=[("r_mean", tag)] + mbk, writes=[("ps", s)], tag="msq")
        pb.emit("dve", lambda e, sl=sl, s=s: e.tensor_tensor(
            vrow[:, sl], vrow[:, sl], ps[s][0:1, 0:512], ALU.subtract),
            reads=[("r_var", tag), ("ps", s)] + rbk,
            writes=[("r_var", tag)] + rbk, tag="vsub")
    pb.emit("dve", lambda e: e.tensor_scalar_add(vrow, vrow, 1e-5),
            reads=[("r_var", tag)] + rbk, writes=[("r_var", tag)] + rbk,
            tag="eps")
    pb.emit("act", lambda e: e.activation(vrow, vrow, AF.Ln),
            reads=[("r_var", tag)] + rbk, writes=[("r_var", tag)] + rbk,
            tag="lnv")
    pb.emit("act", lambda e: e.activation(vrow, vrow, AF.Exp, scale=-0.5),
            reads=[("r_var", tag)] + rbk, writes=[("r_rstd", tag)] + rbk,
            tag="rstd")
    for s in range(4):
        sl = slice(s * 512, (s + 1) * 512)
        pb.emit("pe", lambda e, sl=sl: e.matmul(
            ps[6][:, 0:512], cx.ones_row_f32, mrow[:, sl],
            start=True, stop=True),
            reads=[("r_mean", tag), ("c", "orf")] + mbk, writes=[("ps", 6)],
            tag="bcm")
        pb.emit("dve", lambda e, sl=sl: e.tensor_copy(mb[:, sl],
                                                      ps[6][:, 0:512]),
            reads=[("ps", 6)], writes=[("mb", tag)] + mbk, tag="bcmc")
        pb.emit("pe", lambda e, sl=sl: e.matmul(
            ps[7][:, 0:512], cx.ones_row_f32, vrow[:, sl],
            start=True, stop=True),
            reads=[("r_rstd", tag), ("c", "orf")] + rbk, writes=[("ps", 7)],
            tag="bcr")
        pb.emit("dve", lambda e, sl=sl: e.tensor_copy(rb[:, sl],
                                                      ps[7][:, 0:512]),
            reads=[("ps", 7)], writes=[("rb", tag)] + rbk, tag="bcrc")


def _l0_module_ln(cx):
    pb = cx.pb
    mb = cx.sm[:, 0:2048]
    rb = cx.w2
    _ln_full(cx, cx.x_scr, lambda t: ("xscr", t), "mln0", mb, rb,
             [("sm",)], [("w2",)])
    for t in range(8):
        buf = cx.w0 if t % 2 == 0 else cx.w1
        rkey = ("w0",) if t % 2 == 0 else ("w1",)
        bkey = ("mlnb", t % 2)
        pb.emit("sp", lambda e, t=t, buf=buf: e.dma_start(
            out=buf, in_=cx.x_scr[t * 128:(t + 1) * 128, :]),
            reads=[("xscr", t)], writes=[bkey, rkey], dma=True, tag="mlnld")
        eng = "dve" if t % 2 == 0 else "pool"
        pb.emit(eng, lambda e, buf=buf: e.tensor_tensor(
            buf, buf, mb, ALU.subtract),
            reads=[bkey, ("mb", "mln0"), ("sm",)], writes=[bkey, rkey],
            tag="mlnsub")
        pb.emit(eng, lambda e, buf=buf: e.tensor_tensor(
            buf, buf, rb, ALU.mult),
            reads=[bkey, ("rb", "mln0"), ("w2",)], writes=[bkey, rkey],
            tag="mlnmul")
        pb.emit("pool", lambda e, t=t, buf=buf: e.dma_start(
            out=cx.h_dram[t * 128:(t + 1) * 128, :], in_=buf),
            reads=[bkey, rkey], writes=[("hdram", t)], dma=True, tag="mlnst")


def _l0_zT(cx, i):
    pb, dr = cx.pb, cx.dr
    tag = "ch%d" % i
    mb = cx.sm[:, 0:2048]
    rb = cx.w2
    _ln_full(cx, cx.h_dram, lambda t: ("hdram", t), tag, mb, rb,
             [("sm",)], [("w2",)])
    pb.emit("sp", lambda e: e.dma_start(
        out=cx.cols[:, 0:8], in_=dr["ln_w"][i]),
        writes=[("lncol", "w"), ("cols",)], dma=True, tag="lnw")
    pb.emit("sp", lambda e: e.dma_start(
        out=cx.cols[:, 8:16], in_=dr["ln_b"][i]),
        writes=[("lncol", "b"), ("cols",)], dma=True, tag="lnb")
    zT = cx.av(0, 16384, BF16)
    for t in range(8):
        buf = cx.w0 if t % 2 == 0 else cx.w1
        rkey = ("w0",) if t % 2 == 0 else ("w1",)
        eng = "dve" if t % 2 == 0 else "pool"
        bkey = ("zbuf", i, t % 2)
        pb.emit("sp", lambda e, t=t, buf=buf: e.dma_start(
            out=buf, in_=cx.h_dram[t * 128:(t + 1) * 128, :]),
            reads=[("hdram", t)], writes=[bkey, rkey], dma=True, tag="zld")
        pb.emit(eng, lambda e, buf=buf: e.tensor_tensor(
            buf, buf, mb, ALU.subtract),
            reads=[bkey, ("mb", tag), ("sm",)], writes=[bkey, rkey],
            tag="zsub")
        pb.emit(eng, lambda e, buf=buf: e.tensor_tensor(
            buf, buf, rb, ALU.mult),
            reads=[bkey, ("rb", tag), ("w2",)], writes=[bkey, rkey],
            tag="zmul")
        pb.emit(eng, lambda e, t=t, buf=buf: e.tensor_scalar(
            buf, buf, cx.cols[:, t:t + 1], cx.cols[:, 8 + t:9 + t],
            ALU.mult, ALU.add),
            reads=[bkey, ("lncol", "w"), ("lncol", "b"), ("cols",)],
            writes=[bkey, rkey], tag="zwb")
        for lb in range(16):
            pk = lb % 2
            pst = cx.ps[pk][:, 0:128]
            pb.emit("pe", lambda e, lb=lb, buf=buf, pst=pst: e.transpose(
                pst, buf[:, lb * 128:(lb + 1) * 128], cx.ident_f32),
                reads=[bkey, rkey, ("c", "idf")], writes=[("ps", pk)],
                tag="ztr")
            pb.emit("dve", lambda e, lb=lb, t=t, pst=pst: e.tensor_copy(
                zT[:, lb * 1024 + t * 128: lb * 1024 + (t + 1) * 128], pst),
                reads=[("ps", pk)], writes=[("zT", lb), ("A1",)], tag="zTev")


def _l0_kbuild_lambda(cx, i):
    pb, ps, dr = cx.pb, cx.ps, cx.dr
    sm = cx.sm
    S = lambda j: sm[:, j * 256:(j + 1) * 256]
    kk = lambda j: ("ksm", i, j)

    def op(stream, fn, rd, wr, tg):
        pb.emit(stream, fn, reads=[kk(j) for j in rd],
                writes=[kk(j) for j in wr] + [("sm",)], tag=tg)

    for nm, slot in [("logdt_cg", 0), ("are_cg", 7), ("aim_cg", 8),
                     ("cre_cg", 3), ("cim_cg", 4)]:
        pb.emit("sp", lambda e, nm=nm, slot=slot: e.dma_start(
            out=S(slot), in_=dr[nm][i]), writes=[kk(slot), ("sm",)], dma=True,
            tag="kld" + nm)
    op("act", lambda e: e.activation(S(0), S(0), AF.Exp), [0], [0], "dt")
    op("act", lambda e: e.activation(S(7), S(7), AF.Exp), [7], [7], "eare")
    op("dve", lambda e: e.tensor_scalar_mul(S(7), S(7), -1.0), [7], [7], "neg")
    op("dve", lambda e: e.tensor_tensor(S(9), S(7), S(0), ALU.mult),
       [7, 0], [9], "dtar")
    op("dve", lambda e: e.tensor_tensor(S(1), S(8), S(0), ALU.mult),
       [8, 0], [1], "dtai")
    op("act", lambda e: e.activation(S(0), S(9), AF.Exp), [9], [0], "er")
    C1 = 6.28125
    C2 = 2 * math.pi - C1
    op("dve", lambda e: e.tensor_scalar(S(2), S(1), 1.0 / (2 * math.pi), 64.0,
                                        ALU.mult, ALU.add), [1], [2], "rr1")
    itmp = cx.w1[:, 1024:1280].bitcast(I32)
    pb.emit("dve", lambda e: e.tensor_copy(itmp, S(2)),
            reads=[kk(2), ("sm",)], writes=[("kitmp",), ("w1",)], tag="rr2a")
    pb.emit("dve", lambda e: e.tensor_copy(S(2), itmp),
            reads=[("kitmp",), ("w1",)], writes=[kk(2), ("sm",)], tag="rr2b")
    op("dve", lambda e: e.tensor_scalar_add(S(2), S(2), -64.0), [2], [2], "rr4")
    op("dve", lambda e: e.tensor_single_scalar(S(9), S(2), C1, ALU.mult),
       [2], [9], "rr5")
    op("dve", lambda e: e.tensor_tensor(S(1), S(1), S(9), ALU.subtract),
       [1, 9], [1], "rr6")
    op("dve", lambda e: e.tensor_single_scalar(S(9), S(2), C2, ALU.mult),
       [2], [9], "rr7")
    op("dve", lambda e: e.tensor_tensor(S(1), S(1), S(9), ALU.subtract),
       [1, 9], [1], "rr8")
    op("dve", lambda e: e.tensor_single_scalar(S(9), S(1), math.pi, ALU.is_gt),
       [1], [9], "f1")
    op("dve", lambda e: e.tensor_single_scalar(S(9), S(9), C1 + C2, ALU.mult),
       [9], [9], "f2")
    op("dve", lambda e: e.tensor_tensor(S(1), S(1), S(9), ALU.subtract),
       [1, 9], [1], "f3")
    op("dve", lambda e: e.tensor_single_scalar(S(9), S(1), -math.pi, ALU.is_lt),
       [1], [9], "f4")
    op("dve", lambda e: e.tensor_single_scalar(S(9), S(9), C1 + C2, ALU.mult),
       [9], [9], "f5")
    op("dve", lambda e: e.tensor_tensor(S(1), S(1), S(9), ALU.add),
       [1, 9], [1], "f6")
    op("act", lambda e: e.activation(S(2), S(1), AF.Sin), [1], [2], "sin")
    op("act", lambda e: e.activation(S(9), S(1), AF.Abs), [1], [9], "abs")
    op("dve", lambda e: e.tensor_scalar(S(9), S(9), -1.0, math.pi / 2,
                                        ALU.mult, ALU.add), [9], [9], "carg")
    op("act", lambda e: e.activation(S(9), S(9), AF.Sin), [9], [9], "cos")
    op("dve", lambda e: e.tensor_tensor(S(1), S(0), S(9), ALU.mult),
       [0, 9], [1], "lr")
    op("dve", lambda e: e.tensor_tensor(S(2), S(0), S(2), ALU.mult),
       [0, 2], [2], "li")
    op("dve", lambda e: e.tensor_tensor(S(0), S(7), S(7), ALU.mult),
       [7], [0], "a2")
    op("dve", lambda e: e.tensor_tensor(S(9), S(8), S(8), ALU.mult),
       [8], [9], "b2")
    op("dve", lambda e: e.tensor_tensor(S(0), S(0), S(9), ALU.add),
       [0, 9], [0], "ab")
    op("dve", lambda e: e.reciprocal(S(0), S(0)), [0], [0], "rec")
    op("dve", lambda e: e.tensor_scalar_add(S(9), S(1), -1.0), [1], [9], "nr")
    op("dve", lambda e: e.tensor_tensor(S(5), S(9), S(7), ALU.mult),
       [9, 7], [5], "e1a")
    op("dve", lambda e: e.tensor_tensor(S(6), S(2), S(8), ALU.mult),
       [2, 8], [6], "e1b")
    op("dve", lambda e: e.tensor_tensor(S(5), S(5), S(6), ALU.add),
       [5, 6], [5], "e1c")
    op("dve", lambda e: e.tensor_tensor(S(5), S(5), S(0), ALU.mult),
       [5, 0], [5], "e1")
    op("dve", lambda e: e.tensor_tensor(S(6), S(2), S(7), ALU.mult),
       [2, 7], [6], "e2a")
    op("dve", lambda e: e.tensor_tensor(S(7), S(9), S(8), ALU.mult),
       [9, 8], [7], "e2b")
    op("dve", lambda e: e.tensor_tensor(S(6), S(6), S(7), ALU.subtract),
       [6, 7], [6], "e2c")
    op("dve", lambda e: e.tensor_tensor(S(6), S(6), S(0), ALU.mult),
       [6, 0], [6], "e2")
    op("dve", lambda e: e.tensor_tensor(S(0), S(3), S(5), ALU.mult),
       [3, 5], [0], "cp1")
    op("dve", lambda e: e.tensor_tensor(S(9), S(4), S(6), ALU.mult),
       [4, 6], [9], "cp2")
    op("dve", lambda e: e.tensor_tensor(S(7), S(0), S(9), ALU.subtract),
       [0, 9], [7], "cpr")
    op("dve", lambda e: e.tensor_tensor(S(0), S(3), S(6), ALU.mult),
       [3, 6], [0], "cp3")
    op("dve", lambda e: e.tensor_tensor(S(9), S(4), S(5), ALU.mult),
       [4, 5], [9], "cp4")
    op("dve", lambda e: e.tensor_tensor(S(8), S(0), S(9), ALU.add),
       [0, 9], [8], "cpi")
    op("dve", lambda e: e.tensor_copy(S(5), S(1)), [1], [5], "l6r")
    op("dve", lambda e: e.tensor_copy(S(6), S(2)), [2], [6], "l6i")
    for _ in range(6):
        op("dve", lambda e: e.tensor_tensor(S(0), S(5), S(5), ALU.mult),
           [5], [0], "sqa")
        op("dve", lambda e: e.tensor_tensor(S(9), S(6), S(6), ALU.mult),
           [6], [9], "sqb")
        op("dve", lambda e: e.tensor_tensor(S(6), S(5), S(6), ALU.mult),
           [5, 6], [6], "sqc")
        op("dve", lambda e: e.tensor_single_scalar(S(6), S(6), 2.0, ALU.mult),
           [6], [6], "sqd")
        op("dve", lambda e: e.tensor_tensor(S(5), S(0), S(9), ALU.subtract),
           [0, 9], [5], "sqe")
    if ("ksm%d" % i) in _DEBUG_TAPS:
        _tap(cx, "ksm%d" % i, cx.sm, (128, 2560), F32,
             [kk(j) for j in range(10)] + [("sm",)])

    op("dve", lambda e: e.memset(S(3), 0.0), [], [3], "dskz")
    pb.emit("pool", lambda e: e.dma_start(
        out=cx.sm[0:97:32, 768:1024], in_=dr["dskip_cg"][i]),
        reads=[kk(3)], writes=[("dskip", i), ("sm",)], dma=True, tag="dskld")


def _l0_kbuild_rest(cx, i, defer=None):
    pb, ps, dr = cx.pb, cx.ps, cx.dr
    _sec = []

    def em(*a, **k):
        _sec.append((a, k))
    sm = cx.sm
    S = lambda j: sm[:, j * 256:(j + 1) * 256]
    kk = lambda j: ("ksm", i, j)
    vre = cx.av(65536, 8192, BF16).rearrange("p (g q) -> p g q", q=64)
    vim = cx.av(81920, 8192, BF16).rearrange("p (g q) -> p g q", q=64)
    ure = cx.av(98304, 4096, BF16).rearrange("p (g r) -> p g r", r=32)
    uim = cx.av(106496, 4096, BF16).rearrange("p (g r) -> p g r", r=32)
    lhs = [cx.av(114688, 4096, BF16), cx.av(122880, 4096, BF16)]
    stg = cx.av(131072, 2048, BF16)
    mr = cx.rw[:, 0:64]
    mi = cx.rw[:, 64:128]
    tr_ = cx.rw[:, 128:192]
    ti_ = cx.rw[:, 192:256]
    mrb16 = cx.rw[:, 256:288].bitcast(BF16)
    mib16 = cx.rw[:, 288:320].bitcast(BF16)
    t1f = cx.rw[:, 320:832]
    t2f = cx.rw[:, 832:1344]

    def mul_chain_ops(gw, lam_r, lam_i, kread, mkey):
        em("dve", lambda e: e.tensor_copy(mr[:, 0:gw], lam_r),
           reads=kread + [("rw",)], writes=[mkey, ("rw",)], tag="vmr")
        em("dve", lambda e: e.tensor_copy(mi[:, 0:gw], lam_i),
           reads=kread + [("rw",)], writes=[mkey, ("rw",)], tag="vmi")

    def msq(gw, mkey):
        mrh, mih = mr[:, 0:gw], mi[:, 0:gw]
        trh, tih = tr_[:, 0:gw], ti_[:, 0:gw]
        em("dve", lambda e: e.tensor_tensor(trh, mrh, mrh, ALU.mult),
           reads=[mkey, ("rw",)], writes=[("mt",), ("rw",)], tag="ms1")
        em("dve", lambda e: e.tensor_tensor(tih, mih, mih, ALU.mult),
           reads=[mkey, ("rw",)], writes=[("mt",), ("rw",)], tag="ms2")
        em("dve", lambda e: e.tensor_tensor(mih, mrh, mih, ALU.mult),
           reads=[mkey, ("rw",)], writes=[mkey, ("rw",)], tag="ms3")
        em("dve", lambda e: e.tensor_single_scalar(mih, mih, 2.0, ALU.mult),
           reads=[mkey, ("rw",)], writes=[mkey, ("rw",)], tag="ms4")
        em("dve", lambda e: e.tensor_tensor(mrh, trh, tih, ALU.subtract),
           reads=[("mt",), mkey, ("rw",)], writes=[mkey, ("rw",)], tag="ms5")

    def dbl_step(R, I, gw, c, w, mkey, vkey, reg, off=0, cast=True):
        if cast:
            em("dve", lambda e: e.tensor_copy(mrb16[:, 0:gw], mr[:, 0:gw]),
               reads=[mkey, ("rw",)], writes=[("mb",), ("rw",)], tag="dc1")
            em("dve", lambda e: e.tensor_copy(mib16[:, 0:gw], mi[:, 0:gw]),
               reads=[mkey, ("rw",)], writes=[("mb",), ("rw",)], tag="dc2")
        mrb = mrb16[:, 0:gw].unsqueeze(2).broadcast_to((128, gw, w))
        mib = mib16[:, 0:gw].unsqueeze(2).broadcast_to((128, gw, w))
        a, b_ = R[:, :, off:off + w], I[:, :, off:off + w]
        o_r, o_i = R[:, :, c + off:c + off + w], I[:, :, c + off:c + off + w]
        t1 = t1f.bitcast(BF16)[:, 0:gw * w].rearrange(
            "p (g w) -> p g w", w=w)
        t2 = t2f.bitcast(BF16)[:, 0:gw * w].rearrange(
            "p (g w) -> p g w", w=w)
        em("dve", lambda e: e.tensor_tensor(t1, a, mrb, ALU.mult),
           reads=[vkey, ("mb",), ("rw",)], writes=[("t1",), ("rw",)], tag="d1")
        em("dve", lambda e: e.tensor_tensor(t2, b_, mib, ALU.mult),
           reads=[vkey, ("mb",), ("rw",)], writes=[("t2",), ("rw",)], tag="d2")
        em("dve", lambda e: e.tensor_tensor(o_r, t1, t2, ALU.subtract),
           reads=[("t1",), ("t2",), ("rw",)], writes=[vkey, reg], tag="d3")
        em("dve", lambda e: e.tensor_tensor(t1, a, mib, ALU.mult),
           reads=[vkey, ("mb",), ("rw",)], writes=[("t1",), ("rw",)], tag="d4")
        em("dve", lambda e: e.tensor_tensor(t2, b_, mrb, ALU.mult),
           reads=[vkey, ("mb",), ("rw",)], writes=[("t2",), ("rw",)], tag="d5")
        em("dve", lambda e: e.tensor_tensor(o_i, t1, t2, ALU.add),
           reads=[("t1",), ("t2",), ("rw",)], writes=[vkey, reg], tag="d6")

    vsecs = []
    for ch in range(4):
        _sec = []
        vsecs.append(_sec)
        g0 = ch * 64
        vsl = slice(g0 % 128, g0 % 128 + 64)
        vkey = ("vb", i, ch, "v")
        R = vre[:, vsl, :]
        I = vim[:, vsl, :]
        em("dve", lambda e, R=R: e.memset(R[:, :, 0:1], 1.0),
           writes=[vkey, ("B1",)], tag="v0r")
        em("dve", lambda e, I=I: e.memset(I[:, :, 0:1], 0.0),
           writes=[vkey, ("B1",)], tag="v0i")
        em("dve", lambda e, R=R, g0=g0: e.tensor_copy(
            R[:, :, 1:2], S(1)[:, g0:g0 + 64]),
            reads=[kk(1)], writes=[vkey, ("B1",)], tag="v1r")
        em("dve", lambda e, I=I, g0=g0: e.tensor_copy(
            I[:, :, 1:2], S(2)[:, g0:g0 + 64]),
            reads=[kk(2)], writes=[vkey, ("B1",)], tag="v1i")
        mkey = ("vm", i, ch)
        mul_chain_ops(64, S(1)[:, g0:g0 + 64], S(2)[:, g0:g0 + 64],
                      [kk(1), kk(2)], mkey)
        c = 2
        while c < 64:
            w = min(c, 64 - c)
            msq(64, mkey)
            for off in range(0, w, 16):
                dbl_step(R, I, 64, c, min(16, w - off), mkey, vkey,
                         ("B1",), off=off, cast=(off == 0))
            c += w
    udves, umps = [], []
    _sec = []
    em("dve", lambda e: e.memset(lhs[0], 0.0),
       writes=[("lhsb", 0), ("B2",)], tag="msA")
    em("dve", lambda e: e.memset(lhs[1], 0.0),
       writes=[("lhsb", 1), ("B2",)], tag="msB")
    pre_u = _sec
    l3r = lhs[0].rearrange("p (gl c) -> p gl c", c=128)
    l3i = lhs[1].rearrange("p (gl c) -> p gl c", c=128)
    for b in range(8):
        _sec = []
        udves.append(_sec)
        g0 = b * 32
        usl = slice(g0 % 128, g0 % 128 + 32)
        vsl0 = g0 % 128
        vkey = ("vb", i, g0 // 64, "v")
        ukey = ("ub", i, b)
        Ru = ure[:, usl, :]
        Iu = uim[:, usl, :]
        em("dve", lambda e, Ru=Ru, g0=g0: e.tensor_single_scalar(
            Ru[:, :, 0:1], S(7)[:, g0:g0 + 32], 2.0, ALU.mult),
            reads=[kk(7)], writes=[ukey, ("B2",)], tag="u0r")
        em("dve", lambda e, Iu=Iu, g0=g0: e.tensor_single_scalar(
            Iu[:, :, 0:1], S(8)[:, g0:g0 + 32], 2.0, ALU.mult),
            reads=[kk(8)], writes=[ukey, ("B2",)], tag="u0i")
        mkey = ("um", i, b)
        mul_chain_ops(32, S(5)[:, g0:g0 + 32], S(6)[:, g0:g0 + 32],
                      [kk(5), kk(6)], mkey)
        c = 1
        while c < 32:
            w = min(c, 32 - c)
            if c > 1:
                msq(32, mkey)
            dbl_step(Ru, Iu, 32, c, w, mkey, ukey, ("B2",))
            c += w
        del c
        for c4 in range(4):
            rows = slice(c4 * 32, (c4 + 1) * 32)
            csl = slice(c4 * 32, c4 * 32 + 32)
            em("dve", lambda e, rows=rows, usl=usl, csl=csl:
               e.tensor_copy(l3r[rows, :, csl], ure[rows, usl, :]),
               reads=[ukey, ("B2",)], writes=[("lhsb", 0), ("B2",)],
               tag="lscR")
            em("dve", lambda e, rows=rows, usl=usl, csl=csl:
               e.tensor_single_scalar(l3i[rows, :, csl],
                                      uim[rows, usl, :], -1.0, ALU.mult),
               reads=[ukey, ("B2",)], writes=[("lhsb", 1), ("B2",)],
               tag="lscI")
        _sec = []
        umps.append(_sec)
        for j in range(8):
            bank = 4 + j % 4
            pk = ("ps", bank)
            for k in range(4):
                gl = 4 * j + k
                csl_ = slice(k * 64, (k + 1) * 64)
                em("pe", lambda e, gl=gl, vsl0=vsl0, bank=bank, csl_=csl_:
                   e.matmul(cx.ps[bank][:, csl_], l3r[:, gl, :],
                            vre[:, vsl0 + gl, :], start=True, stop=False),
                   reads=[("lhsb", 0), vkey, ("B1",), ("B2",)],
                   writes=[pk], tag="kmmR")
                em("pe", lambda e, gl=gl, vsl0=vsl0, bank=bank, csl_=csl_:
                   e.matmul(cx.ps[bank][:, csl_], l3i[:, gl, :],
                            vim[:, vsl0 + gl, :], start=False, stop=True),
                   reads=[("lhsb", 1), vkey, ("B1",), ("B2",)],
                   writes=[pk], tag="kmmI")
            em("act", lambda e, j=j, bank=bank: e.activation(
                stg.rearrange("p (q gl) -> p q gl", gl=32)[
                    :, :, 4 * j:4 * j + 4],
                cx.ps[bank][:, 0:256].rearrange("p (g q) -> p q g", g=4),
                AF.Copy),
                reads=[pk], writes=[("stg",), ("B2",)], tag="kev")
        gsg = slice(768 + b * 32, 768 + (b + 1) * 32)
        em("dve", lambda e, gsg=gsg: e.tensor_tensor(
            stg[:, 0:32], stg[:, 0:32], cx.sm[:, gsg], ALU.add),
            reads=[("stg",), ("dskip", i), ("sm",)], writes=[("stg",)],
            tag="kdsk")
        for c4 in range(4):
            src_ = stg[c4 * 32:(c4 + 1) * 32, :].rearrange(
                "r (q gl) -> r q gl", gl=32)
            dst_ = cx.k_scr.rearrange(
                "(r q) (c4 bb gl) -> r q c4 bb gl", q=64, c4=4, bb=8)[
                0:32, :, c4, b, :]
            em("sp", lambda e, src_=src_, dst_=dst_: e.dma_start(
                out=dst_, in_=src_),
                reads=[("stg",)], writes=[("kscr", b, c4)], dma=True,
                tag="kdump")
    _sec = []
    em("dve", lambda e: e.engine_nop(),
       reads=[("kscr", b, c4) for b in range(8) for c4 in range(4)],
       writes=[("vb", i, 0, "v"), ("vb", i, 1, "v"), ("vb", i, 2, "v"),
               ("vb", i, 3, "v")] +
              [("ub", i, b) for b in range(8)] +
              [("lhsb", 0), ("lhsb", 1), ("stg",), ("Bfree", i)], tag="kbar")
    if defer is None and ("kscr%d" % i) in _DEBUG_TAPS:
        _tap(cx, "kscr%d" % i, cx.k_scr, (T, D), BF16,
             [("kscr", b2, c2) for b2 in range(8) for c2 in range(4)])
    kT = cx.av(65536, 34816, BF16)
    for t in range(16):
        em("pool" if t % 2 == 0 else "act", lambda e, t=t: e.dma_start(
            out=kT[:, t * 1024:(t + 1) * 1024],
            in_=cx.k_scr[t * 128:(t + 1) * 128, :]),
            reads=[("Bfree", i)] +
                  [("kscr", b2, c2) for b2 in range(8) for c2 in range(4)],
            writes=[("kT", t), ("B1",)], dma=True, tag="kTld")
    tail = _sec
    groups = [vsecs[0] + vsecs[1],
              pre_u + udves[0],
              umps[0] + udves[1],
              umps[1] + udves[2],
              vsecs[2] + umps[2] + udves[3],
              umps[3] + udves[4],
              vsecs[3] + umps[4] + udves[5],
              umps[5] + udves[6],
              umps[6] + udves[7],
              umps[7] + tail]
    if defer is None:
        for sec in groups:
            for a, k in sec:
                pb.emit(*a, **k)
        return None
    cuts = []
    for sec in groups:
        defer.extend(sec)
        cuts.append(len(defer))
    return cuts


def _l0_conv(cx, i, nxt=None, kq=None, vcut=0):
    pb, ps = cx.pb, cx.ps
    zT = cx.av(0, 16384, BF16)
    kT = cx.av(65536, 34816, BF16)
    A2 = cx.av(32768, 16384, BF16)
    KREG = cx.av(65536, 34816, BF16)

    def p_re(ft):
        if ft < 16:
            return A2[:, ft * 1024:(ft + 1) * 1024]
        return cx.wst[:, 0:1024]

    def p_im(ft):
        if ft < 16:
            return KREG[:, 16384 + ft * 1024:16384 + (ft + 1) * 1024]
        return KREG[:, 32768:33792]

    for ft in range(FT):
        half = ft % 2
        cosb = cx.dst[:, half * 4352: half * 4352 + 2048]
        sinb = cx.dst[:, half * 4352 + 2048: half * 4352 + 4096]
        ck = ("dstream", half)
        pb.emit("sp", lambda e, ft=ft, cosb=cosb: e.dma_start(
            out=cosb, in_=cx.dr["dftc"][ft].rearrange("p a c -> p (a c)")),
            writes=[ck], dma=True, tag="dftc")
        pb.emit("sp", lambda e, ft=ft, sinb=sinb: e.dma_start(
            out=sinb, in_=cx.dr["dfts"][ft].rearrange("p a c -> p (a c)")),
            writes=[ck], dma=True, tag="dfts")
        for lb in range(16):
            st, sp_ = (lb == 0), (lb == 15)
            for comp, buf in ((0, cosb), (1, sinb)):
                for s in range(2):
                    base = 4 * ((2 * ft + s) % 2)
                    for kind in range(2):
                        bank = base + comp + 2 * kind
                        rhs_t = zT if kind == 0 else kT
                        rkey = ("zT", lb) if kind == 0 else ("kT", lb)
                        extra = [("A1",)] if kind == 0 else [("B1",)]
                        pb.emit("pe", lambda e, lb=lb, s=s, buf=buf,
                                bank=bank, rhs_t=rhs_t, st=st, sp_=sp_:
                                e.matmul(ps[bank][:, :],
                                         buf[:, lb * 128:(lb + 1) * 128],
                                         rhs_t[:, lb * 1024 + s * 512:
                                               lb * 1024 + (s + 1) * 512],
                                         start=st, stop=sp_),
                                reads=[ck, rkey] + extra,
                                writes=[("ps", bank)], tag="cfmm")
        for s in range(2):
            base = 4 * ((2 * ft + s) % 2)
            zr = cx.w0b[:, 0:512]
            zi = cx.w0b[:, 512:1024]
            kr = cx.w0b[:, 1024:1536]
            ki = cx.w0b[:, 1536:2048]
            for j, dstv in ((base + 0, zr), (base + 1, zi), (base + 2, kr),
                            (base + 3, ki)):
                pb.emit("dve", lambda e, j=j, dstv=dstv: e.tensor_copy(
                    dstv, ps[j][:, :]),
                    reads=[("ps", j)], writes=[("pw", i), ("w0",)],
                    tag="spev")
            t1 = cx.w1b[:, 0:512]
            t2 = cx.w1b[:, 512:1024]
            pr = p_re(ft)[:, s * 512:(s + 1) * 512]
            pi = p_im(ft)[:, s * 512:(s + 1) * 512]
            pb.emit("dve", lambda e, t1=t1, zr=zr, kr=kr: e.tensor_tensor(
                t1, zr, kr, ALU.mult), reads=[("pw", i), ("w0",)],
                writes=[("pwt", i), ("w1",)], tag="pw1")
            pb.emit("dve", lambda e, t2=t2, zi=zi, ki=ki: e.tensor_tensor(
                t2, zi, ki, ALU.mult), reads=[("pw", i), ("w0",)],
                writes=[("pwt", i), ("w1",)], tag="pw2")
            pb.emit("dve", lambda e, pr=pr, t1=t1, t2=t2: e.tensor_tensor(
                pr, t1, t2, ALU.subtract), reads=[("pwt", i), ("w1",)],
                writes=[("P", ft, 0)] + ([("A2",)] if ft < 16 else [("wst",)]),
                tag="pw3")
            pb.emit("dve", lambda e, t1=t1, zr=zr, ki=ki: e.tensor_tensor(
                t1, zr, ki, ALU.mult), reads=[("pw", i), ("w0",)],
                writes=[("pwt", i), ("w1",)], tag="pw4")
            pb.emit("dve", lambda e, t2=t2, zi=zi, kr=kr: e.tensor_tensor(
                t2, zi, kr, ALU.mult), reads=[("pw", i), ("w0",)],
                writes=[("pwt", i), ("w1",)], tag="pw5")
            pb.emit("dve", lambda e, pi=pi, t1=t1, t2=t2: e.tensor_tensor(
                pi, t1, t2, ALU.add), reads=[("pwt", i), ("w1",)],
                writes=[("P", ft, 1), ("B2",)], tag="pw6")
    pb.emit("dve", lambda e: e.engine_nop(),
            reads=[("P", ft, c) for ft in range(FT) for c in range(2)],
            writes=[("zT", lb) for lb in range(16)] +
                   [("kT", t) for t in range(16)] + [("convfwd", i)],
            tag="fbar")
    if nxt is not None:
        _l0_kbuild_lambda(cx, nxt)
    vflushed = [0]

    def vflush(j):
        if not kq:
            return
        hi = min(vcut, (vcut * j) // 10)
        while vflushed[0] < hi:
            a, kw = kq[vflushed[0]]
            pb.emit(*a, **kw)
            vflushed[0] += 1
    ygT = cx.av(0, 16384, BF16)
    for lt in range(16):
        half = lt % 2
        icb = cx.dst[:, half * 4352: half * 4352 + 2176]
        isb = cx.dst[:, half * 4352 + 2176: half * 4352 + 4352]
        ik = ("dstream", half)
        pb.emit("sp", lambda e, lt=lt, icb=icb: e.dma_start(
            out=icb, in_=cx.dr["idftc"][lt].rearrange("p a c -> p (a c)")),
            writes=[ik], dma=True, tag="idftc")
        pb.emit("sp", lambda e, lt=lt, isb=isb: e.dma_start(
            out=isb, in_=cx.dr["idfts"][lt].rearrange("p a c -> p (a c)")),
            writes=[ik], dma=True, tag="idfts")
        bank0 = (lt % 2) * 2
        n = 0
        for ft in range(FT):
            for comp in range(2):
                buf = icb if comp == 0 else isb
                pv = p_re(ft) if comp == 0 else p_im(ft)
                st = (n == 0)
                sp_ = (n == 2 * FT - 1)
                for s in range(2):
                    pb.emit("pe", lambda e, ft=ft, buf=buf, pv=pv, s=s,
                            st=st, sp_=sp_, bank0=bank0: e.matmul(
                                ps[bank0 + s][:, :],
                                buf[:, ft * 128:(ft + 1) * 128],
                                pv[:, s * 512:(s + 1) * 512], start=st,
                                stop=sp_),
                            reads=[ik, ("P", ft, comp), ("convfwd", i),
                                   ("A2",), ("B2",), ("wst",)],
                            writes=[("ps", bank0 + s)], tag="idft")
                n += 1
        for s in range(2):
            pb.emit("act", lambda e, lt=lt, s=s, bank0=bank0: e.activation(
                ygT[:, lt * 1024 + s * 512: lt * 1024 + (s + 1) * 512],
                ps[bank0 + s][:, :], AF.Gelu_apprx_tanh),
                reads=[("ps", bank0 + s)], writes=[("ygT", lt), ("A1",)],
                tag="gelu")
        vflush(lt + 1)
    pb.emit("dve", lambda e: e.engine_nop(),
            reads=[("ygT", lt) for lt in range(16)],
            writes=[("P", ft, c) for ft in range(FT) for c in range(2)] +
                   [("idone", i)], tag="ibar")


def _l0_wout(cx, i, kq=None, cuts=None):
    pb, ps, dr = cx.pb, cx.ps, cx.dr
    kflushed = [cuts[0] if cuts else 0]

    def kflush(j):
        if not kq:
            return
        hi = cuts[min(j, len(cuts) - 1)]
        while kflushed[0] < hi:
            a, kw = kq[kflushed[0]]
            pb.emit(*a, **kw)
            kflushed[0] += 1
    ygT = cx.av(0, 16384, BF16)
    ygfm = cx.av(32768, 16384, BF16)
    for c in range(8):
        for lt in range(16):
            pk = lt % 2
            pst = ps[pk][:, 0:64].bitcast(BF16)
            pb.emit("pe", lambda e, c=c, lt=lt, pst=pst: e.transpose(
                pst, ygT[:, lt * 1024 + c * 128:lt * 1024 + (c + 1) * 128],
                cx.ident_bf),
                reads=[("ygT", lt), ("idone", i), ("A1",), ("c", "idb")],
                writes=[("ps", pk)], tag="ytr")
            pb.emit("dve", lambda e, c=c, lt=lt, pst=pst: e.tensor_copy(
                ygfm[:, c * 2048 + lt * 128:c * 2048 + (lt + 1) * 128], pst),
                reads=[("ps", pk)], writes=[("ygfm", c), ("A2",)],
                tag="ytrev")
    pb.emit("sp", lambda e: e.dma_start(
        out=cx.cols[:, 16:32],
        in_=dr["b_out"][i]),
        writes=[("bocol", i), ("cols",)], dma=True, tag="bout")
    for lo in range(8):
        wslots = {}
        for part, ot in ((0, lo), (1, lo + 8)):
            wslot = cx.wst[:, (2 * lo + part) % 4 * 1024:
                           ((2 * lo + part) % 4 + 1) * 1024]
            wkey = ("wostream", (2 * lo + part) % 4)
            wslots[part] = (wslot, wkey)
            pb.emit("pool", lambda e, ot=ot, wslot=wslot: e.dma_start(
                out=wslot,
                in_=dr["w_out_t"][i, ot].rearrange("p a c -> p (a c)")),
                reads=[("idone", i)], writes=[wkey, ("wst",)], dma=True,
                tag="wold")
        for r in range(2):
            for part in range(2):
                wslot, wkey = wslots[part]
                for ha in range(8):
                    for si in range(2):
                        s = 2 * r + si
                        bank = part * 2 + si
                        pb.emit("pe", lambda e, s=s, ha=ha, bank=bank,
                                wslot=wslot: e.matmul(
                                    ps[bank][:, :],
                                    wslot[:, ha * 128:(ha + 1) * 128],
                                    ygfm[:, ha * 2048 + s * 512:
                                         ha * 2048 + (s + 1) * 512],
                                    start=(ha == 0), stop=(ha == 7)),
                                reads=[wkey, ("ygfm", ha), ("A2",)],
                                writes=[("ps", bank)], tag="womm")
            for si in range(2):
                s = 2 * r + si
                sig = cx.w0b[:, s * 512:(s + 1) * 512]
                pb.emit("act", lambda e, si=si, lo=lo, sig=sig: e.activation(
                    sig, ps[2 + si][:, :], AF.Sigmoid,
                    bias=cx.cols[:, 16 + lo + 8:17 + lo + 8]),
                    reads=[("ps", 2 + si), ("bocol", i), ("cols",)],
                    writes=[("glu", "s"), ("w0",)], tag="sig")
                lin = cx.w1b[:, s * 512:(s + 1) * 512]
                pb.emit("act", lambda e, si=si, lo=lo, lin=lin: e.activation(
                    lin, ps[si][:, :], AF.Identity,
                    bias=cx.cols[:, 16 + lo:17 + lo]),
                    reads=[("ps", si), ("bocol", i), ("cols",)],
                    writes=[("glu", "l"), ("w1",)], tag="lin")
        zo = cx.dst[:, 0:2048]
        pb.emit("dve", lambda e, zo=zo: e.tensor_tensor(
            zo, cx.w1b[:, 0:2048], cx.w0b[:, 0:2048], ALU.mult),
            reads=[("glu", "s"), ("glu", "l"), ("w0",), ("w1",)],
            writes=[("zo", lo), ("dstream", 0)], tag="glu")
        pb.emit("sp", lambda e, lo=lo: e.dma_start(
            out=cx.w2, in_=cx.h_dram[lo * 128:(lo + 1) * 128, :]),
            reads=[("hdram", lo)], writes=[("hres", lo), ("w2",)], dma=True,
            tag="hld")
        pb.emit("dve", lambda e, lo=lo, zo=zo: e.tensor_tensor(
            cx.w2, cx.w2, zo, ALU.add),
            reads=[("hres", lo), ("zo", lo), ("dstream", 0)],
            writes=[("hres", lo), ("w2",)], tag="hadd")
        pb.emit("pool", lambda e, lo=lo: e.dma_start(
            out=cx.h_dram[lo * 128:(lo + 1) * 128, :], in_=cx.w2),
            reads=[("hres", lo), ("w2",)], writes=[("hdram", lo)], dma=True,
            tag="hst")
        kflush(lo + 1)
    kflush(9)
    assert kflushed[0] == (len(kq) if kq else 0)
    pb.emit("dve", lambda e: e.engine_nop(),
            reads=[("hdram", t) for t in range(8)],
            writes=[("ygT", lt) for lt in range(16)] +
                   [("ygfm", c) for c in range(8)] + [("ldone", i)],
            tag="wbar")


def _phase_l0_attn(cx):
    pb, ps = cx.pb, cx.ps
    _l0_module_ln(cx)
    _tap(cx, "h0", cx.h_dram, (D, T), F32, [("hdram", t) for t in range(8)])
    layers = list(_LAYER_LIST if _LAYER_LIST is not None
                  else range(_NLAYERS))
    for li, i in enumerate(layers):
        _l0_zT(cx, i)
        if "zT%d" % i in _DEBUG_TAPS:
            _tap(cx, "zT%d" % i, cx.av(0, 16384, BF16), (128, 16384), BF16,
                 [("zT", lb) for lb in range(16)])
        if li == 0:
            _l0_kbuild_lambda(cx, i)
            _l0_kbuild_rest(cx, i)
        if "kT%d" % i in _DEBUG_TAPS:
            _tap(cx, "kT%d" % i, cx.av(65536, 16384, BF16), (128, 16384), BF16,
                 [("kT", t) for t in range(16)])
        nxt = layers[li + 1] if li + 1 < len(layers) else None
        kq, cuts = None, None
        if nxt is not None:
            kq = []
            cuts = _l0_kbuild_rest(cx, nxt, defer=kq)
        _l0_conv(cx, i, nxt=nxt, kq=kq, vcut=(cuts[0] if cuts else 0))
        if "ygT%d" % i in _DEBUG_TAPS:
            _tap(cx, "ygT%d" % i, cx.av(0, 16384, BF16), (128, 16384), BF16,
                 [("ygT", lt) for lt in range(16)])
        _l0_wout(cx, i, kq=kq, cuts=cuts)
        _tap(cx, "h%d" % (i + 1), cx.h_dram, (D, T), F32,
             [("hdram", t) for t in range(8)])
    sq = cx.av(178176, 2048, F32)
    for t in range(8):
        buf = cx.w0 if t % 2 == 0 else cx.w1
        rkey = ("w0",) if t % 2 == 0 else ("w1",)
        pb.emit("sp", lambda e, t=t, buf=buf: e.dma_start(
            out=buf, in_=cx.x_scr[t * 128:(t + 1) * 128, :]),
            reads=[("xscr", t)], writes=[("xres", t)] + [rkey], dma=True,
            tag="xld")
        pb.emit("pool", lambda e, t=t: e.dma_start(
            out=cx.w2, in_=cx.h_dram[t * 128:(t + 1) * 128, :]),
            reads=[("hdram", t)], writes=[("xres2", t), ("w2",)], dma=True,
            tag="xld2")
        pb.emit("dve", lambda e, t=t, buf=buf: e.tensor_tensor(
            buf, buf, cx.w2, ALU.add),
            reads=[("xres", t), ("xres2", t), ("w2",)] + [rkey],
            writes=[("xres", t)] + [rkey], tag="xadd")
        pb.emit("act", lambda e, buf=buf: e.activation(sq, buf, AF.Square),
                reads=[("xres", t)] + [rkey],
                writes=[("lnsq", "ffn"), ("dstream", 0)], tag="xsq")
        for s in range(4):
            pb.emit("pe", lambda e, t=t, s=s, buf=buf: e.matmul(
                ps[s][0:1, 0:512], cx.ones_col_f32,
                buf[:, s * 512:(s + 1) * 512], start=(t == 0), stop=(t == 7)),
                reads=[("xres", t), ("c", "ocf")] + [rkey],
                writes=[("ps", s)], tag="xlns")
            pb.emit("pe", lambda e, t=t, s=s: e.matmul(
                ps[4 + s][0:1, 0:512], cx.ones_col_f32,
                sq[:, s * 512:(s + 1) * 512], start=(t == 0), stop=(t == 7)),
                reads=[("lnsq", "ffn"), ("dstream", 0), ("c", "ocf")],
                writes=[("ps", 4 + s)], tag="xlns2")
        pb.emit("pool", lambda e, t=t, buf=buf: e.dma_start(
            out=cx.x_scr[t * 128:(t + 1) * 128, :], in_=buf),
            reads=[("xres", t)] + [rkey], writes=[("xscr", t)], dma=True,
            tag="xst")


def _phase_l0_ffn(cx):
    pb, ps, dr = cx.pb, cx.ps, cx.dr
    mb = cx.sm[:, 0:2048]
    rb = cx.w2
    _ln_finish(cx, "ffn", mb, rb, [("sm",)], [("w2",)])
    xq = [cx.av(t * 2048, 2048, FP8) for t in range(8)]
    for t in range(8):
        buf = cx.w0 if t % 2 == 0 else cx.w1
        rkey = ("w0",) if t % 2 == 0 else ("w1",)
        bkey = ("fxb", t % 2)
        pb.emit("sp", lambda e, t=t, buf=buf: e.dma_start(
            out=buf, in_=cx.x_scr[t * 128:(t + 1) * 128, :]),
            reads=[("xscr", t)], writes=[bkey, rkey], dma=True, tag="fxld")
        eng = "dve" if t % 2 == 0 else "pool"
        pb.emit(eng, lambda e, buf=buf: e.tensor_tensor(
            buf, buf, mb, ALU.subtract),
            reads=[bkey, ("mb", "ffn"), ("sm",)], writes=[bkey, rkey],
            tag="fxsub")
        pb.emit(eng, lambda e, t=t, buf=buf, rkey=rkey: e.tensor_tensor(
            xq[t], buf, rb, ALU.mult),
            reads=[bkey, rkey, ("rb", "ffn"), ("w2",)],
            writes=[("xq", t), ("A1",)], tag="fxq")
    qc = cx.av(61440, 256, FP8)
    pb.emit("sp", lambda e: e.dma_start(
        out=cx.cols[:, 32:64], in_=dr["b1"]),
        writes=[("bcol", 1), ("cols",)], dma=True, tag="b1ld")
    pb.emit("sp", lambda e: e.dma_start(
        out=cx.cols[:, 64:72], in_=dr["b2"]),
        writes=[("bcol", 2), ("cols",)], dma=True, tag="b2ld")
    pb.emit("dve", lambda e: e.tensor_copy(qc[:, 0:40], cx.cols[:, 32:72]),
            reads=[("bcol", 1), ("bcol", 2), ("cols",)],
            writes=[("qc",), ("A2",)], tag="bq1")
    pb.emit("dve", lambda e: e.tensor_copy(cx.cols[:, 32:72], qc[:, 0:40]),
            reads=[("qc",), ("A2",)], writes=[("bcol", 1), ("bcol", 2),
                                              ("cols",)], tag="bq2")

    h1q = [cx.av(16384 + nt * 1024, 1024, FP8) for nt in range(32)]
    wq1 = [cx.av(49152 + j * 1024, 1024, FP8) for j in range(2)]
    wq2 = [cx.av(51200 + j * 4096, 4096, FP8) for j in range(2)]
    msk = [cx.av(59392 + j * 1024, 1024, FP8) for j in range(2)]
    for half in range(2):
        hc = slice(half * 1024, (half + 1) * 1024)
        for nt in range(32):
            wslot = wq1[nt % 2]
            wkey = ("wq1", nt % 2)
            fkey = ("w1",)
            pb.emit("sp", lambda e, nt=nt: e.dma_start(
                out=cx.w1[:, 0:1024],
                in_=dr["w1_t"][nt].rearrange("p a c -> p (a c)")),
                writes=[("wf1",), fkey], dma=True, tag="w1ld")
            pb.emit("dve", lambda e, wslot=wslot: e.tensor_copy(
                wslot, cx.w1b.bitcast(F32)[:, 0:1024] if False else
                cx.w1[:, 0:1024]),
                reads=[("wf1",), fkey], writes=[wkey, ("A2",)], tag="w1q")
            for a in range(8):
                for s in range(2):
                    bank = (nt % 4) * 2 + s
                    pb.emit("pe", lambda e, nt=nt, s=s, a=a, bank=bank,
                            wslot=wslot, hc=hc: e.matmul(
                                ps[bank][:, :], wslot[:, a * 128:(a + 1) * 128],
                                xq[a][:, hc][:, s * 512:(s + 1) * 512],
                                start=(a == 0), stop=(a == 7)),
                            reads=[wkey, ("xq", a), ("A1",), ("A2",)],
                            writes=[("ps", bank)], tag="h1mm")
            for s in range(2):
                bank = (nt % 4) * 2 + s
                pb.emit("dve", lambda e, nt=nt, s=s, bank=bank: e.tensor_scalar(
                    cx.w0b[:, s * 512:(s + 1) * 512], ps[bank][:, :],
                    cx.cols[:, 32 + nt:33 + nt], 0.0, ALU.add, ALU.max),
                    reads=[("ps", bank), ("bcol", 1), ("cols",)],
                    writes=[("h1bf", nt % 2), ("w0",)], tag="h1ev")
            pb.emit("dve", lambda e, nt=nt: e.tensor_copy(
                h1q[nt], cx.w0b[:, 0:1024]),
                reads=[("h1bf", nt % 2), ("w0",)],
                writes=[("h1q", nt), ("A1",), ("A2",)], tag="h1q")
        for dt_ in range(8):
            wslot = wq2[dt_ % 2]
            wkey = ("wq2", dt_ % 2)
            for j in range(2):
                pb.emit("sp" if j == 0 else "pool",
                        lambda e, dt_=dt_, j=j: e.dma_start(
                            out=cx.w1[:, 0:2048] if j == 0
                            else cx.w0[:, 0:2048],
                            in_=dr["w2_t"][dt_, j].rearrange(
                                "p a c -> p (a c)")),
                        writes=[("wf2", j), ("w1",) if j == 0 else ("w0",)],
                        dma=True, tag="w2ld")
                pb.emit("dve" if j == 0 else "pool",
                        lambda e, wslot=wslot, j=j: e.tensor_copy(
                            wslot[:, j * 2048:(j + 1) * 2048],
                            cx.w1[:, 0:2048] if j == 0
                            else cx.w0[:, 0:2048]),
                        reads=[("wf2", j), ("w1",) if j == 0 else ("w0",)],
                        writes=[wkey, ("A2",)], tag="w2q")
            mslot = msk[dt_ % 2]
            pb.emit("pool", lambda e, dt_=dt_, mslot=mslot, hc=hc:
                    e.dma_start(out=mslot,
                                in_=dr["dropmask_fm"][
                                    dt_ * 128:(dt_ + 1) * 128, hc]),
                    writes=[("msk", dt_ % 2), ("A2",)], dma=True, tag="mskld")
            pb.emit("sp", lambda e, dt_=dt_, hc=hc: e.dma_start(
                out=cx.w2[:, 0:1024],
                in_=cx.x_scr[dt_ * 128:(dt_ + 1) * 128, hc]),
                reads=[("xscr", dt_)], writes=[("xr", dt_ % 2), ("w2",)],
                dma=True, tag="fxrld")
            for nt in range(32):
                for s in range(2):
                    bank = (dt_ % 2) * 4 + s
                    pb.emit("pe", lambda e, s=s, nt=nt, bank=bank,
                            wslot=wslot: e.matmul(
                                ps[bank][:, :],
                                wslot[:, nt * 128:(nt + 1) * 128],
                                h1q[nt][:, s * 512:(s + 1) * 512],
                                start=(nt == 0), stop=(nt == 31)),
                            reads=[wkey, ("h1q", nt), ("A1",), ("A2",)],
                            writes=[("ps", bank)], tag="h2mm")
            smb = cx.av(163840, 1024, BF16)
            for s in range(2):
                bank = (dt_ % 2) * 4 + s
                pb.emit("dve", lambda e, dt_=dt_, s=s, bank=bank, smb=smb:
                        e.tensor_scalar(
                            smb[:, s * 512:(s + 1) * 512], ps[bank][:, :],
                            cx.cols[:, 64 + dt_:65 + dt_], None, ALU.add),
                        reads=[("ps", bank), ("bcol", 2), ("cols",)],
                        writes=[("h2bf", dt_ % 2), ("sm",)], tag="h2ev")
                pb.emit("dve", lambda e, s=s, mslot=mslot, dt_=dt_, smb=smb:
                        e.tensor_tensor(
                            cx.sm[:, s * 512:(s + 1) * 512],
                            smb[:, s * 512:(s + 1) * 512],
                            mslot[:, s * 512:(s + 1) * 512], ALU.mult),
                        reads=[("h2bf", dt_ % 2), ("msk", dt_ % 2), ("sm",),
                               ("A2",)],
                        writes=[("h2f", dt_ % 2), ("sm",)], tag="h2msk")
                pb.emit("dve", lambda e, s=s: e.tensor_tensor(
                    cx.w2[:, s * 512:(s + 1) * 512],
                    cx.w2[:, s * 512:(s + 1) * 512],
                    cx.sm[:, s * 512:(s + 1) * 512], ALU.add),
                    reads=[("h2f", dt_ % 2), ("xr", dt_ % 2), ("sm",),
                           ("w2",)],
                    writes=[("xr", dt_ % 2), ("w2",)], tag="fxadd")
            pb.emit("pool", lambda e, dt_=dt_, hc=hc: e.dma_start(
                out=cx.x_scr[dt_ * 128:(dt_ + 1) * 128, hc],
                in_=cx.w2[:, 0:1024]),
                reads=[("xr", dt_ % 2), ("w2",)], writes=[("xscr", dt_)],
                dma=True, tag="fxst")


def _tokmaj_proj(cx, w2name, half, xln, evac, tag):
    pb, ps, dr = cx.pb, cx.ps, cx.dr
    wstb = cx.wst
    for chunk in range(2):
        buf = cx.w0 if chunk == 0 else cx.w1
        fkey = ("w0",) if chunk == 0 else ("w1",)
        pb.emit("sp" if chunk == 0 else "pool",
                lambda e, chunk=chunk, buf=buf: e.dma_start(
                    out=buf[:, 0:2048],
                    in_=dr[w2name][half, :, chunk * 4:(chunk + 1) * 4, :]
                    .rearrange("p a c -> p (a c)")),
                writes=[("tmw", tag, chunk), fkey], dma=True,
                tag="tmld" + tag)
        pb.emit("dve", lambda e, chunk=chunk, buf=buf: e.tensor_copy(
            wstb[:, chunk * 2048:(chunk + 1) * 2048], buf[:, 0:2048]),
            reads=[("tmw", tag, chunk), fkey],
            writes=[("tmq", tag, chunk), ("wst",)], tag="tmq" + tag)
    for t in range(16):
        bank = t % 2
        for a in range(8):
            pb.emit("pe", lambda e, t=t, a=a, bank=bank: e.matmul(
                ps[bank][:, :], xln[a][:, t * 128:(t + 1) * 128],
                wstb[:, a * 512:(a + 1) * 512],
                start=(a == 0), stop=(a == 7)),
                reads=[("tmq", tag, a // 4), ("wst",), ("xln", a), ("A1",)],
                writes=[("ps", bank)], tag="tmmm" + tag)
        evac(t, bank)


def _phase_l1_attn(cx):
    pb, ps, dr = cx.pb, cx.ps, cx.dr
    mb = cx.sm[:, 0:2048]
    rb = cx.w2
    _ln_full(cx, cx.x_scr, lambda t: ("xscr", t), "la", mb, rb,
             [("sm",)], [("w2",)])
    xln = [cx.av(t * 4096, 2048, BF16) for t in range(8)]
    for t in range(8):
        buf = cx.w0 if t % 2 == 0 else cx.w1
        rkey = ("w0",) if t % 2 == 0 else ("w1",)
        bkey = ("lab", t % 2)
        pb.emit("sp", lambda e, t=t, buf=buf: e.dma_start(
            out=buf, in_=cx.x_scr[t * 128:(t + 1) * 128, :]),
            reads=[("xscr", t)], writes=[bkey, rkey], dma=True, tag="lald")
        eng = "dve" if t % 2 == 0 else "pool"
        pb.emit(eng, lambda e, buf=buf: e.tensor_tensor(
            buf, buf, mb, ALU.subtract),
            reads=[bkey, ("mb", "la"), ("sm",)], writes=[bkey, rkey],
            tag="lasub")
        pb.emit(eng, lambda e, t=t, buf=buf, rkey=rkey: e.tensor_tensor(
            xln[t], buf, rb, ALU.mult),
            reads=[bkey, rkey, ("rb", "la"), ("w2",)],
            writes=[("xln", t), ("A1",)], tag="laxq")
    pb.emit("sp", lambda e: e.dma_start(
        out=cx.dst[:, 0:2048],
        in_=dr["spanmask"].rearrange("p a c -> p (a c)")),
        writes=[("spm",), ("dstream", 0)], dma=True, tag="spmld")
    spm = lambda v: cx.dst[:, v * 512:(v + 1) * 512]
    pb.emit("sp", lambda e: e.dma_start(
        out=cx.cols[:, 32:40], in_=dr["l1_attn_bo"]),
        writes=[("bocol", "la"), ("cols",)], dma=True, tag="bola")

    qT = [cx.av(32768 + j * 4096, 2048, BF16) for j in range(4)]
    kT = [cx.av(49152 + j * 4096, 2048, BF16) for j in range(4)]
    ktok = [cx.av(65536 + t * 1024, 512, BF16) for t in range(16)]
    vaug = [cx.av(81920 + t * 1040, 520, BF16) for t in range(16)]
    Ssb = [cx.av(98560 + t * 1024, 512, BF16) for t in range(4)]
    oTs = [cx.av(114944 + j * 4096, 2048, BF16) for j in range(4)]
    ssb_pair = [cx.av(131328, 65, BF16, npart=64, p0=0),
                cx.av(131328, 65, BF16, npart=64, p0=64)]
    ssb32_pair = [cx.av(103680, 65, F32, npart=64, p0=0),
                  cx.av(103680, 65, F32, npart=64, p0=64)]
    drow = cx.av(132368, 512, F32, npart=1)
    osc = cx.av(102656, 512, BF16, npart=64)

    for half in range(2):
        dh0 = half * 512
        for pi, (wname, kind) in enumerate((("l1_attn_wq", "q"),
                                            ("l1_attn_wk", "k"))):
            for j in range(4):
                slot = cx.wst[:, (j % 4) * 1024:(j % 4) * 1024 + 1024]
                wkey = ("wsl", j % 4)
                buf = cx.w1 if j % 2 == 0 else cx.w0
                fkey = ("w1",) if j % 2 == 0 else ("w0",)
                pb.emit("sp" if j % 2 == 0 else "pool",
                        lambda e, wname=wname, j=j, dh0=dh0, buf=buf:
                        e.dma_start(out=buf[:, 0:1024],
                                    in_=dr[wname][dh0 // 128 + j]
                                    .rearrange("p a c -> p (a c)")),
                        writes=[("wf", j % 4), fkey], dma=True, tag="lwld")
                pb.emit("dve", lambda e, slot=slot, buf=buf: e.tensor_copy(
                    slot, buf[:, 0:1024]),
                    reads=[("wf", j % 4), fkey], writes=[wkey, ("wst",)],
                    tag="lwq")
                if kind in ("q", "k"):
                    tiles, tkey = (qT, "qT") if kind == "q" else (kT, "kTl1")
                    for a in range(8):
                        for s in range(4):
                            bank = (j % 2) * 4 + s
                            pb.emit("pe", lambda e, s=s, a=a, slot=slot,
                                    bank=bank:
                                    e.matmul(ps[bank][:, :],
                                             slot[:, a * 128:(a + 1) * 128],
                                             xln[a][:, s * 512:(s + 1) * 512],
                                             start=(a == 0), stop=(a == 7)),
                                    reads=[wkey, ("wst",), ("xln", a),
                                           ("A1",)],
                                    writes=[("ps", bank)], tag="prmm")
                    for s in range(4):
                        bank = (j % 2) * 4 + s
                        pb.emit("dve", lambda e, j=j, s=s, tiles=tiles,
                                bank=bank:
                                e.tensor_scalar(
                                    tiles[j][:, s * 512:(s + 1) * 512],
                                    ps[bank][:, :], 0.0, 0.001, ALU.max,
                                    ALU.add),
                                reads=[("ps", bank)],
                                writes=[(tkey, j), ("A2",)], tag="prev")
        def vevac(t, bank):
            pb.emit("dve", lambda e, t=t, bank=bank: e.tensor_copy(
                vaug[t].rearrange("p (h c) -> p h c", c=65)[:, :, 0:64],
                ps[bank][:, :].rearrange("p (h c) -> p h c", c=64)),
                reads=[("ps", bank)],
                writes=[("va", t), ("B1",), ("B2",)], tag="vev")
        _tokmaj_proj(cx, "l1_attn_wv2", half, xln, vevac, "av%d" % half)

        def ktevac(t, bank):
            pb.emit("dve", lambda e, t=t, bank=bank: e.tensor_scalar(
                ktok[t], ps[bank][:, :], 0.0, 0.001, ALU.max, ALU.add),
                reads=[("ps", bank)], writes=[("kk", t), ("B1",)],
                tag="kkev")
        _tokmaj_proj(cx, "l1_attn_wk2", half, xln, ktevac, "ak%d" % half)
        for t in range(16):
            pb.emit("dve", lambda e, t=t: e.memset(
                vaug[t].rearrange("p (h c) -> p h c", c=65)[:, :, 64:65], 1.0),
                writes=[("va", t), ("B1",), ("B2",)], tag="vone")
        for hl in range(8):
            hrow = (hl % 2) * 64
            jt = hl // 2
            ssb = ssb_pair[hl % 2]
            ssb32 = ssb32_pair[hl % 2]
            spsum = ps[6][hrow:hrow + 64, 0:65]
            for s in range(4):
                tisl = slice(s * 512, (s + 1) * 512)
                for v in range(4):
                    tj = 4 * s + v
                    bank = v % 3
                    pb.emit("pe", lambda e, tj=tj, v=v, bank=bank, jt=jt,
                            hrow=hrow, tisl=tisl: e.matmul(
                                ps[bank][:, :],
                                kT[jt][hrow:hrow + 64,
                                       tj * 128:(tj + 1) * 128],
                                qT[jt][hrow:hrow + 64, tisl],
                                start=True, stop=True),
                            reads=[("kTl1", jt), ("qT", jt), ("A2",)],
                            writes=[("ps", bank)], tag="smm")
                    pb.emit("dve", lambda e, v=v, bank=bank: e.tensor_tensor(
                        Ssb[v], ps[bank][:, :], spm(v), ALU.mult),
                        reads=[("ps", bank), ("spm",), ("dstream", 0)],
                        writes=[("Ssb", v), ("B2",)], tag="sevm")
                first = True
                if s > 0:
                    pb.emit("pe", lambda e, jt=jt, hrow=hrow, tisl=tisl,
                            ssb=ssb: e.matmul(
                                ps[4][0:65, :], ssb,
                                qT[jt][hrow:hrow + 64, tisl],
                                start=True, stop=False),
                            reads=[("ssb",), ("qT", jt), ("A2",), ("B2",)],
                            writes=[("ps", 4)], tag="ocross")
                    first = False
                for v in range(4):
                    tj = 4 * s + v
                    pb.emit("pe", lambda e, tj=tj, v=v, hl=hl,
                            st=(first and v == 0), sp_=(v == 3): e.matmul(
                                ps[4][0:65, :],
                                vaug[tj][:, hl * 65:(hl + 1) * 65],
                                Ssb[v], start=st, stop=sp_),
                            reads=[("va", tj), ("Ssb", v), ("B2",)],
                            writes=[("ps", 4)], tag="omm")
                pb.emit("dve", lambda e: e.tensor_copy(drow, ps[4][64:65, :]),
                        reads=[("ps", 4)], writes=[("drow",), ("B2",)],
                        tag="dev")
                pb.emit("act", lambda e: e.activation(drow, drow, AF.Ln),
                        reads=[("drow",), ("B2",)],
                        writes=[("drow",), ("B2",)], tag="dln")
                pb.emit("act", lambda e: e.activation(drow, drow, AF.Exp,
                                                      scale=-1.0),
                        reads=[("drow",), ("B2",)],
                        writes=[("drow",), ("B2",)], tag="dexp")
                pb.emit("pe", lambda e: e.matmul(
                    ps[5][:, :], cx.ones_row_f32, drow, start=True,
                    stop=True),
                    reads=[("drow",), ("B2",), ("c", "orf")],
                    writes=[("ps", 5)], tag="dbc")
                pb.emit("dve", lambda e: e.tensor_copy(
                    cx.w0b[0:64, 0:512], ps[4][0:64, :]),
                    reads=[("ps", 4)], writes=[("oev",), ("w0",)], tag="oev")
                if hl % 2 == 0:
                    pb.emit("dve", lambda e, jt=jt, tisl=tisl:
                            e.tensor_tensor(oTs[jt][0:64, tisl],
                                            cx.w0b[0:64, 0:512],
                                            ps[5][0:64, :], ALU.mult),
                            reads=[("oev",), ("w0",), ("ps", 5)],
                            writes=[("oTs", jt), ("B2",)], tag="oscl")
                else:
                    pb.emit("dve", lambda e: e.tensor_tensor(
                        osc, cx.w0b[0:64, 0:512], ps[5][0:64, :], ALU.mult),
                        reads=[("oev",), ("w0",), ("ps", 5)],
                        writes=[("osc",), ("B2",)], tag="oscl2")
                    pb.emit("pe", lambda e: e.matmul(
                        ps[3][:, :], cx.shift64_bf, osc, start=True,
                        stop=True),
                        reads=[("osc",), ("B2",), ("c", "sh64")],
                        writes=[("ps", 3)], tag="oshift")
                    pb.emit("dve", lambda e, jt=jt, tisl=tisl: e.tensor_copy(
                        oTs[jt][64:128, tisl], ps[3][64:128, :]),
                        reads=[("ps", 3)], writes=[("oTs", jt), ("B2",)],
                        tag="osev")
                for c in range(4):
                    tj = 4 * s + c
                    pb.emit("pe", lambda e, tj=tj, hl=hl, st=(c == 0),
                            sp_=(c == 3), spsum=spsum: e.matmul(
                                spsum,
                                ktok[tj][:, hl * 64:(hl + 1) * 64],
                                vaug[tj][:, hl * 65:(hl + 1) * 65],
                                start=st, stop=sp_),
                            reads=[("kk", tj), ("va", tj), ("B1",), ("B2",)],
                            writes=[("ps", 6)], tag="stup")
                if s < 3:
                    if s == 0:
                        pb.emit("dve", lambda e, ssb32=ssb32, spsum=spsum:
                                e.tensor_copy(ssb32, spsum),
                                reads=[("ps", 6)],
                                writes=[("ssb32",), ("B2",)], tag="st32i")
                    else:
                        pb.emit("dve", lambda e, ssb32=ssb32, spsum=spsum:
                                e.tensor_tensor(ssb32, ssb32, spsum, ALU.add),
                                reads=[("ps", 6), ("ssb32",)],
                                writes=[("ssb32",), ("B2",)], tag="st32a")
                    pb.emit("dve", lambda e, ssb=ssb, ssb32=ssb32:
                            e.tensor_copy(ssb, ssb32),
                            reads=[("ssb32",), ("B2",)],
                            writes=[("ssb",), ("B2",)], tag="stev")
        for dm in range(8):
            slot = cx.wst[:, 2048:2048 + 512]
            wkey = ("wslo",)
            pb.emit("sp", lambda e, dm=dm, dh0=dh0: e.dma_start(
                out=cx.w1[:, 0:512],
                in_=dr["l1_attn_wo"][dh0 // 512, dm].rearrange(
                    "p a c -> p (a c)")),
                writes=[("wfo",), ("w1",)], dma=True, tag="wold2")
            pb.emit("dve", lambda e, slot=slot: e.tensor_copy(
                slot, cx.w1[:, 0:512]),
                reads=[("wfo",), ("w1",)], writes=[wkey, ("wst",)], tag="woq2")
            pb.emit("sp", lambda e, dm=dm: e.dma_start(
                out=cx.w0, in_=cx.x_scr[dm * 128:(dm + 1) * 128, :]),
                reads=[("xscr", dm)], writes=[("lax",), ("w0",)], dma=True,
                tag="laxld")
            for j in range(4):
                for s in range(4):
                    pb.emit("pe", lambda e, s=s, j=j, slot=slot: e.matmul(
                        ps[s][:, :], slot[:, j * 128:(j + 1) * 128],
                        oTs[j][:, s * 512:(s + 1) * 512],
                        start=(j == 0), stop=(j == 3)),
                        reads=[wkey, ("wst",), ("oTs", j), ("B2",)],
                        writes=[("ps", s)], tag="wo2mm")
            for s in range(4):
                pb.emit("dve", lambda e, s=s: e.tensor_tensor(
                    cx.w0[:, s * 512:(s + 1) * 512],
                    cx.w0[:, s * 512:(s + 1) * 512], ps[s][:, :], ALU.add),
                    reads=[("ps", s), ("lax",)], writes=[("lax",), ("w0",)],
                    tag="laacc")
            if half == 0:
                pb.emit("dve", lambda e, dm=dm: e.tensor_scalar(
                    cx.w0, cx.w0, cx.cols[:, 32 + dm:33 + dm], None, ALU.add),
                    reads=[("lax",), ("bocol", "la"), ("cols",), ("w0",)],
                    writes=[("lax",), ("w0",)], tag="laboadd")
            pb.emit("pool", lambda e, dm=dm: e.dma_start(
                out=cx.x_scr[dm * 128:(dm + 1) * 128, :], in_=cx.w0),
                reads=[("lax",), ("w0",)], writes=[("xscr", dm)], dma=True,
                tag="laxst")


def _phase_l1_ffn(cx):
    pb, ps, dr = cx.pb, cx.ps, cx.dr
    mb = cx.sm[:, 0:2048]
    rb = cx.w2
    _ln_full(cx, cx.x_scr, lambda t: ("xscr", t), "lf", mb, rb,
             [("sm",)], [("w2",)])
    xln = [cx.av(t * 4096, 2048, BF16) for t in range(8)]
    for t in range(8):
        buf = cx.w0 if t % 2 == 0 else cx.w1
        rkey = ("w0",) if t % 2 == 0 else ("w1",)
        bkey = ("lfb", t % 2)
        pb.emit("sp", lambda e, t=t, buf=buf: e.dma_start(
            out=buf, in_=cx.x_scr[t * 128:(t + 1) * 128, :]),
            reads=[("xscr", t)], writes=[bkey, rkey], dma=True, tag="lfld")
        eng = "dve" if t % 2 == 0 else "pool"
        pb.emit(eng, lambda e, buf=buf: e.tensor_tensor(
            buf, buf, mb, ALU.subtract),
            reads=[bkey, ("mb", "lf"), ("sm",)], writes=[bkey, rkey],
            tag="lfsub")
        pb.emit(eng, lambda e, t=t, buf=buf, rkey=rkey: e.tensor_tensor(
            xln[t], buf, rb, ALU.mult),
            reads=[bkey, rkey, ("rb", "lf"), ("w2",)],
            writes=[("xln", t), ("A1",)], tag="lfxq")
    rw = cx.rw
    gate = rw[0:1, :]
    gm = cx.av(99584, 2048, F32, npart=1)
    trow = cx.cf32[0:1, 260:276]
    pb.emit("sp", lambda e: e.dma_start(
        out=cx.cols[:, 40:48], in_=dr["l1_ffn_wg"]),
        writes=[("wg",), ("cols",)], dma=True, tag="wgld")
    wgb = cx.cbf[:, 388:396]
    pb.emit("dve", lambda e: e.tensor_copy(wgb, cx.cols[:, 40:48]),
            reads=[("wg",), ("cols",)], writes=[("wgb",)], tag="wgq")
    for s in range(4):
        for a in range(8):
            pb.emit("pe", lambda e, s=s, a=a: e.matmul(
                ps[7][0:1, 0:512], wgb[:, a:a + 1],
                xln[a][:, s * 512:(s + 1) * 512], start=(a == 0),
                stop=(a == 7)),
                reads=[("wgb",), ("xln", a), ("A1",)],
                writes=[("ps", 7)], tag="gmm")
        pb.emit("dve", lambda e, s=s: e.tensor_copy(
            gate[:, s * 512:(s + 1) * 512], ps[7][0:1, 0:512]),
            reads=[("ps", 7)], writes=[("gate",), ("rw",)], tag="gev")
    pb.emit("act", lambda e: e.activation(gate, gate, AF.Sigmoid),
            reads=[("gate",), ("rw",)], writes=[("gate",), ("rw",)],
            tag="gsig")
    gcols = cx.cols[:, 72:88]
    for t in range(16):
        pb.emit("pe", lambda e, t=t: e.transpose(
            ps[6][:, 0:1], gate[:, t * 128:(t + 1) * 128],
            cx.ones_col_f32[0:1, 0:1]),
            reads=[("gate",), ("rw",), ("c", "ocf")], writes=[("ps", 6)],
            tag="gtr")
        pb.emit("dve", lambda e, t=t: e.tensor_copy(
            gcols[:, t:t + 1], ps[6][:, 0:1]),
            reads=[("ps", 6)], writes=[("gcols",), ("cols",)], tag="gcev")
    tbc = cx.cols[:, 88:89]
    cntc = cx.cols[:, 89:90]
    cmpt = cx.w1[:, 0:16]
    pb.emit("dve", lambda e: e.memset(trow[:, 0:1], 0.0),
            writes=[("bis",), ("cf32",)], tag="bi0")
    pb.emit("dve", lambda e: e.memset(trow[:, 1:2], 1.0),
            writes=[("bis",), ("cf32",)], tag="bi1")
    for it in range(24):
        pb.emit("dve", lambda e: e.tensor_tensor(
            trow[:, 2:3], trow[:, 0:1], trow[:, 1:2], ALU.add),
            reads=[("bis",)], writes=[("bis",), ("cf32",)], tag="bt1")
        pb.emit("dve", lambda e: e.tensor_single_scalar(
            trow[:, 2:3], trow[:, 2:3], 0.5, ALU.mult),
            reads=[("bis",)], writes=[("bis",), ("cf32",)], tag="bt2")
        pb.emit("pe", lambda e: e.matmul(
            ps[7][:, 0:1], cx.ones_row_f32, trow[:, 2:3], start=True,
            stop=True),
            reads=[("bis",), ("cf32",), ("c", "orf")], writes=[("ps", 7)],
            tag="btb")
        pb.emit("dve", lambda e: e.tensor_copy(tbc, ps[7][:, 0:1]),
                reads=[("ps", 7)], writes=[("tbc",), ("cols",)], tag="btc")
        pb.emit("dve", lambda e: e.tensor_single_scalar(
            cmpt, gcols, tbc, ALU.is_gt),
            reads=[("gcols",), ("tbc",), ("cols",)],
            writes=[("cmpt",), ("w1",)], tag="bt3")
        pb.emit("dve", lambda e: e.reduce_sum(
            cntc, cmpt, axis=mybir.AxisListType.X),
            reads=[("cmpt",), ("w1",)], writes=[("cntc",), ("cols",)],
            tag="bt4")
        pb.emit("pe", lambda e: e.matmul(
            ps[7][0:1, 0:1], cx.ones_col_f32, cntc, start=True, stop=True),
            reads=[("cntc",), ("cols",), ("c", "ocf")], writes=[("ps", 7)],
            tag="btm")
        pb.emit("dve", lambda e: e.tensor_copy(trow[:, 3:4], ps[7][0:1, 0:1]),
                reads=[("ps", 7)], writes=[("bis",), ("cf32",)], tag="btcnt")
        pb.emit("dve", lambda e: e.tensor_single_scalar(
            trow[:, 4:5], trow[:, 3:4], 1024.0, ALU.is_ge),
            reads=[("bis",)], writes=[("bis",), ("cf32",)], tag="bt5")
        pb.emit("dve", lambda e: e.tensor_tensor(
            trow[:, 5:6], trow[:, 2:3], trow[:, 0:1], ALU.subtract),
            reads=[("bis",)], writes=[("bis",), ("cf32",)], tag="bt6")
        pb.emit("dve", lambda e: e.tensor_tensor(
            trow[:, 5:6], trow[:, 5:6], trow[:, 4:5], ALU.mult),
            reads=[("bis",)], writes=[("bis",), ("cf32",)], tag="bt7")
        pb.emit("dve", lambda e: e.tensor_tensor(
            trow[:, 0:1], trow[:, 0:1], trow[:, 5:6], ALU.add),
            reads=[("bis",)], writes=[("bis",), ("cf32",)], tag="bt8")
        pb.emit("dve", lambda e: e.tensor_tensor(
            trow[:, 5:6], trow[:, 1:2], trow[:, 2:3], ALU.subtract),
            reads=[("bis",)], writes=[("bis",), ("cf32",)], tag="bt9")
        pb.emit("dve", lambda e: e.tensor_single_scalar(
            trow[:, 6:7], trow[:, 4:5], -1.0, ALU.mult),
            reads=[("bis",)], writes=[("bis",), ("cf32",)], tag="bt10")
        pb.emit("dve", lambda e: e.tensor_single_scalar(
            trow[:, 6:7], trow[:, 6:7], 1.0, ALU.add),
            reads=[("bis",)], writes=[("bis",), ("cf32",)], tag="bt11")
        pb.emit("dve", lambda e: e.tensor_tensor(
            trow[:, 5:6], trow[:, 5:6], trow[:, 6:7], ALU.mult),
            reads=[("bis",)], writes=[("bis",), ("cf32",)], tag="bt12")
        pb.emit("dve", lambda e: e.tensor_tensor(
            trow[:, 1:2], trow[:, 1:2], trow[:, 5:6], ALU.subtract),
            reads=[("bis",)], writes=[("bis",), ("cf32",)], tag="bt13")
    tmp = cx.av(107776, 2048, F32, npart=1)
    pb.emit("dve", lambda e: e.tensor_single_scalar(
        tmp, gate, trow[:, 0:1], ALU.is_gt),
        reads=[("gate",), ("bis",)], writes=[("mrow",), ("rw",)], tag="mrow")
    pb.emit("dve", lambda e: e.tensor_tensor(gm, gate, tmp, ALU.mult),
            reads=[("gate",), ("mrow",)], writes=[("gm",), ("rw",)],
            tag="gmrow")
    _tap(cx, "gate", gate, (1, 2048), F32, [("gate",), ("rw",)])
    _tap(cx, "maskrow", tmp, (1, 2048), F32, [("mrow",)])
    _tap(cx, "gmrow", gm, (1, 2048), F32, [("gm",)])
    for t in range(16):
        pb.emit("pe", lambda e, t=t: e.transpose(
            ps[6][:, 0:1].bitcast(F32) if False else ps[6][:, 0:1],
            tmp[:, t * 128:(t + 1) * 128], cx.ones_col_f32[0:1, 0:1]),
            reads=[("mrow",), ("rw",), ("c", "ocf")], writes=[("ps", 6)],
            tag="mtr")
        pb.emit("dve", lambda e, t=t: e.tensor_copy(
            cx.cols[:, 48 + t:49 + t], ps[6][:, 0:1]),
            reads=[("ps", 6)], writes=[("mcol", t), ("cols",)], tag="mcev")

    _tap(cx, "mcols", cx.cols[:, 48:64], (128, 16), F32,
         [("mcol", t) for t in range(16)] + [("cols",)])
    qT = [cx.av(32768 + j * 4096, 2048, BF16) for j in range(4)]
    kT = [cx.av(49152 + j * 4096, 2048, BF16) for j in range(4)]
    vaug = [cx.av(65536 + t * 1040, 520, BF16) for t in range(16)]
    ST = [cx.av(82176 + t * 1024, 512, BF16) for t in range(16)]
    oTs = [cx.av(115968 + j * 4096, 2048, BF16) for j in range(4)]
    drow = cx.av(132352, 512, F32, npart=1)
    osc = cx.av(98560, 512, BF16, npart=64)

    for half in range(2):
        dh0 = half * 512
        for pi, (wname, kind) in enumerate((("l1_ffn_wq", "q"),
                                            ("l1_ffn_wk", "k"))):
            for j in range(4):
                slot = cx.wst[:, (j % 4) * 1024:(j % 4) * 1024 + 1024]
                wkey = ("wslf", j % 4)
                buf = cx.w1 if j % 2 == 0 else cx.w0
                fkey = ("w1",) if j % 2 == 0 else ("w0",)
                pb.emit("sp" if j % 2 == 0 else "pool",
                        lambda e, wname=wname, j=j, dh0=dh0, buf=buf:
                        e.dma_start(out=buf[:, 0:1024],
                                    in_=dr[wname][dh0 // 128 + j]
                                    .rearrange("p a c -> p (a c)")),
                        writes=[("wff", j % 4), fkey], dma=True, tag="fwld")
                pb.emit("dve", lambda e, slot=slot, buf=buf: e.tensor_copy(
                    slot, buf[:, 0:1024]),
                    reads=[("wff", j % 4), fkey], writes=[wkey, ("wst",)],
                    tag="fwq")
                if kind in ("q", "k"):
                    tiles, tkey = (qT, "qTf") if kind == "q" else (kT, "kTf")
                    scl = 1.0 if kind == "q" else 0.125
                    for a in range(8):
                        for s in range(4):
                            bank = (j % 2) * 4 + s
                            pb.emit("pe", lambda e, s=s, a=a, slot=slot,
                                    bank=bank:
                                    e.matmul(ps[bank][:, :],
                                             slot[:, a * 128:(a + 1) * 128],
                                             xln[a][:, s * 512:(s + 1) * 512],
                                             start=(a == 0), stop=(a == 7)),
                                    reads=[wkey, ("wst",), ("xln", a),
                                           ("A1",)],
                                    writes=[("ps", bank)], tag="fprmm")
                    for s in range(4):
                        bank = (j % 2) * 4 + s
                        pb.emit("dve", lambda e, j=j, s=s, tiles=tiles,
                                scl=scl, bank=bank: e.tensor_single_scalar(
                                    tiles[j][:, s * 512:(s + 1) * 512],
                                    ps[bank][:, :], scl, ALU.mult),
                                reads=[("ps", bank)],
                                writes=[(tkey, j), ("A2",)], tag="fprev")
        def fvevac(t, bank):
            pb.emit("dve", lambda e, t=t, bank=bank: e.tensor_scalar(
                vaug[t].rearrange("p (h c) -> p h c", c=65)[:, :, 0:64],
                ps[bank][:, :].rearrange("p (h c) -> p h c", c=64),
                cx.cols[:, 48 + t:49 + t], None, ALU.mult),
                reads=[("ps", bank), ("mcol", t), ("cols",)],
                writes=[("vaf", t), ("B1",), ("B2",)], tag="fvev")
        _tokmaj_proj(cx, "l1_ffn_wv2", half, xln, fvevac, "fv%d" % half)
        for t in range(16):
            pb.emit("dve", lambda e, t=t: e.tensor_copy(
                vaug[t].rearrange("p (h c) -> p h c", c=65)[:, :, 64:65],
                cx.cols[:, 48 + t:49 + t].unsqueeze(2).broadcast_to(
                    (128, 1, 8)).rearrange("p one h -> p h one")
                if False else
                cx.cols[:, 48 + t:49 + t].unsqueeze(1).broadcast_to(
                    (128, 8, 1))),
                reads=[("mcol", t), ("cols",)],
                writes=[("vaf", t), ("B1",), ("B2",)], tag="fvone")
        iters = []
        for hl in range(8):
            hrow = (hl % 2) * 64
            jt = hl // 2
            for s in range(4):
                tisl = slice(s * 512, (s + 1) * 512)
                A, B, C = [], [], []
                emA = lambda *a, **k: A.append((a, k))
                emB = lambda *a, **k: B.append((a, k))
                emC = lambda *a, **k: C.append((a, k))
                for tj in range(16):
                    bank = tj % 3
                    emA("pe", lambda e, tj=tj, bank=bank, jt=jt,
                        hrow=hrow, tisl=tisl: e.matmul(
                            ps[bank][:, :],
                            kT[jt][hrow:hrow + 64,
                                   tj * 128:(tj + 1) * 128],
                            qT[jt][hrow:hrow + 64, tisl],
                            start=True, stop=True),
                        reads=[("kTf", jt), ("qTf", jt), ("A2",)],
                        writes=[("ps", bank)], tag="fsmm")
                    emA("act", lambda e, tj=tj, bank=bank: e.activation(
                        ST[tj], ps[bank][:, :], AF.Exp),
                        reads=[("ps", bank)],
                        writes=[("ST", tj), ("B1",), ("B2",)], tag="fexp")
                for tj in range(16):
                    emB("pe", lambda e, tj=tj, hl=hl: e.matmul(
                        ps[4][0:65, :], vaug[tj][:, hl * 65:(hl + 1) * 65],
                        ST[tj], start=(tj == 0), stop=(tj == 15)),
                        reads=[("vaf", tj), ("ST", tj), ("B1",), ("B2",)],
                        writes=[("ps", 4)], tag="fomm")
                emC("dve", lambda e: e.tensor_copy(drow, ps[4][64:65, :]),
                    reads=[("ps", 4)], writes=[("drow",), ("B2",)],
                    tag="fdev")
                emC("act", lambda e: e.activation(drow, drow, AF.Ln),
                    reads=[("drow",), ("B2",)],
                    writes=[("drow",), ("B2",)], tag="fdln")
                emC("act", lambda e: e.activation(drow, drow, AF.Exp,
                                                  scale=-1.0),
                    reads=[("drow",), ("B2",)],
                    writes=[("drow",), ("B2",)], tag="fdexp")
                emC("dve", lambda e, tisl=tisl: e.tensor_tensor(
                    drow, drow, gm[:, tisl], ALU.mult),
                    reads=[("drow",), ("gm",), ("rw",), ("B2",)],
                    writes=[("drow",), ("B2",)], tag="fdgm")
                emC("pe", lambda e: e.matmul(
                    ps[5][:, :], cx.ones_row_f32, drow, start=True,
                    stop=True),
                    reads=[("drow",), ("B2",), ("c", "orf")],
                    writes=[("ps", 5)], tag="fdbc")
                emC("dve", lambda e: e.tensor_copy(
                    cx.w0b[0:64, 0:512], ps[4][0:64, :]),
                    reads=[("ps", 4)], writes=[("foev",), ("w0",)],
                    tag="foev")
                if hl % 2 == 0:
                    emC("dve", lambda e, jt=jt, tisl=tisl:
                        e.tensor_tensor(oTs[jt][0:64, tisl],
                                        cx.w0b[0:64, 0:512],
                                        ps[5][0:64, :], ALU.mult),
                        reads=[("foev",), ("w0",), ("ps", 5)],
                        writes=[("oTf", jt), ("B2",)], tag="foscl")
                else:
                    emC("dve", lambda e: e.tensor_tensor(
                        osc, cx.w0b[0:64, 0:512], ps[5][0:64, :], ALU.mult),
                        reads=[("foev",), ("w0",), ("ps", 5)],
                        writes=[("oscf",), ("B2",)], tag="foscl2")
                    emC("pe", lambda e: e.matmul(
                        ps[3][:, :], cx.shift64_bf, osc, start=True,
                        stop=True),
                        reads=[("oscf",), ("B2",), ("c", "sh64")],
                        writes=[("ps", 3)], tag="foshift")
                    emC("dve", lambda e, jt=jt, tisl=tisl: e.tensor_copy(
                        oTs[jt][64:128, tisl], ps[3][64:128, :]),
                        reads=[("ps", 3)], writes=[("oTf", jt), ("B2",)],
                        tag="fosev")
                iters.append((A, B, C))
        prevC = None
        for A, B, C in iters:
            for a, k in A:
                pb.emit(*a, **k)
            if prevC is not None:
                for a, k in prevC:
                    pb.emit(*a, **k)
            for a, k in B:
                pb.emit(*a, **k)
            prevC = C
        for a, k in prevC:
            pb.emit(*a, **k)
        for dm in range(8):
            slot = cx.wst[:, 2048:2048 + 512]
            wkey = ("wslof",)
            pb.emit("sp", lambda e, dm=dm, dh0=dh0: e.dma_start(
                out=cx.w1[:, 0:512],
                in_=dr["l1_ffn_wo"][dh0 // 512, dm].rearrange(
                    "p a c -> p (a c)")),
                writes=[("wfof",), ("w1",)], dma=True, tag="fwold")
            pb.emit("dve", lambda e, slot=slot: e.tensor_copy(
                slot, cx.w1[:, 0:512]),
                reads=[("wfof",), ("w1",)], writes=[wkey, ("wst",)],
                tag="fwoq")
            pb.emit("sp", lambda e, dm=dm: e.dma_start(
                out=cx.w0, in_=cx.x_scr[dm * 128:(dm + 1) * 128, :]),
                reads=[("xscr", dm)], writes=[("lfx",), ("w0",)], dma=True,
                tag="fxrld2")
            for j in range(4):
                for s in range(4):
                    pb.emit("pe", lambda e, s=s, j=j, slot=slot: e.matmul(
                        ps[s][:, :], slot[:, j * 128:(j + 1) * 128],
                        oTs[j][:, s * 512:(s + 1) * 512],
                        start=(j == 0), stop=(j == 3)),
                        reads=[wkey, ("wst",), ("oTf", j), ("B2",)],
                        writes=[("ps", s)], tag="fwomm")
            for s in range(4):
                pb.emit("dve", lambda e, s=s: e.tensor_tensor(
                    cx.w0[:, s * 512:(s + 1) * 512],
                    cx.w0[:, s * 512:(s + 1) * 512], ps[s][:, :], ALU.add),
                    reads=[("ps", s), ("lfx",)], writes=[("lfx",), ("w0",)],
                    tag="flacc")
            pb.emit("pool", lambda e, dm=dm: e.dma_start(
                out=cx.x_scr[dm * 128:(dm + 1) * 128, :], in_=cx.w0),
                reads=[("lfx",), ("w0",)], writes=[("xscr", dm)], dma=True,
                tag="fxst2")



def kernel(**inputs):
    per_core = _host_prep(inputs)
    nc, cx = build_program()
    res = run_bass_kernel_spmd(nc, per_core, core_ids=list(range(NCORES)))
    outs = [np.asarray(r["out_fm"]).T for r in res.results]
    return np.stack(outs, axis=0).astype(np.float32)


# version 23
# speedup vs baseline: 1.0117x; 1.0093x over previous
import math
import numpy as np
import ml_dtypes
import concourse.bass as bass
import concourse.mybir as mybir
from concourse.bass_utils import run_bass_kernel_spmd

F32 = mybir.dt.float32
BF16 = mybir.dt.bfloat16
FP8 = mybir.dt.float8e5
I32 = mybir.dt.int32
AF = mybir.ActivationFunctionType
ALU = mybir.AluOpType

D = 1024
T = 2048
NCORES = 8
FFTN = 4096
FT = 17
LT = 16

_DEBUG_TAPS = []
_PHASES = ("l0_attn", "l0_ffn", "l1_attn", "l1_ffn")
_NLAYERS = 4
_LAYER_LIST = None


_STREAMS = ("pe", "dve", "act", "pool", "sp")
_DMA_QUEUES = ("sp", "pool", "act")
_NSLOT = 8


class _Op:
    pass


class PB:
    serialize_all = False

    def __init__(self):
        self.ops = []
        self._prev = None
        self.last_writer = {}
        self.readers = {}
        self.qops = {q: [] for q in _DMA_QUEUES}

    def emit(self, stream, fn, reads=(), writes=(), dma=False, tag=""):
        op = _Op()
        op.stream, op.fn, op.is_dma, op.tag = stream, fn, dma, tag
        deps = []
        for k in reads:
            w = self.last_writer.get(k)
            if w is not None:
                deps.append(w)
        for k in writes:
            w = self.last_writer.get(k)
            if w is not None:
                deps.append(w)
            deps.extend(self.readers.get(k, ()))
        if dma:
            assert stream in _DMA_QUEUES, stream
            k = len(self.qops[stream])
            if k >= _NSLOT:
                deps.append(self.qops[stream][k - _NSLOT])
            self.qops[stream].append(op)
        if PB.serialize_all and self._prev is not None:
            deps.append(self._prev)
        self._prev = op
        op.deps = [d for d in dict.fromkeys(deps) if d is not op]
        for k in reads:
            self.readers.setdefault(k, []).append(op)
        for k in writes:
            self.last_writer[k] = op
            self.readers[k] = []
        self.ops.append(op)
        return op

    def finalize(self):
        ctrs = list(_STREAMS) + [("q", q, s) for q in _DMA_QUEUES
                                 for s in range(_NSLOT)]
        self.ctrs = ctrs
        self.cidx = {c: i for i, c in enumerate(ctrs)}
        NC = len(ctrs)
        qseq = {q: 0 for q in _DMA_QUEUES}
        seqs = {s: 0 for s in _STREAMS}
        for op in self.ops:
            if op.is_dma:
                k = qseq[op.stream]
                qseq[op.stream] = k + 1
                op.ctr = self.cidx[("q", op.stream, k % _NSLOT)]
                op.cval = (k // _NSLOT + 1) * 16
            else:
                seqs[op.stream] += 1
                op.ctr = self.cidx[op.stream]
                op.cval = seqs[op.stream]
        cur_done = {s: [0] * NC for s in _STREAMS}
        stream_retire = {s: [0] * NC for s in _STREAMS}
        waited = [set() for _ in range(NC)]
        for op in self.ops:
            s = op.stream
            vc = list(cur_done[s])
            op.waits = []
            for d in op.deps:
                if vc[d.ctr] < d.cval:
                    op.waits.append(d)
                    waited[d.ctr].add(d.cval)
                    dc = d.inherit
                    for j in range(NC):
                        if dc[j] > vc[j]:
                            vc[j] = dc[j]
                    if d.cval > vc[d.ctr]:
                        vc[d.ctr] = d.cval
            cur_done[s] = vc
            if op.is_dma:
                inh = list(vc)
                if op.cval > inh[op.ctr]:
                    inh[op.ctr] = op.cval
                op.inherit = inh
            else:
                inh = list(stream_retire[s])
                for j in range(NC):
                    if vc[j] > inh[j]:
                        inh[j] = vc[j]
                if op.cval > inh[op.ctr]:
                    inh[op.ctr] = op.cval
                op.inherit = inh
                stream_retire[s] = inh
        self.rank = {}
        for s in _STREAMS:
            j = self.cidx[s]
            self.rank[j] = {v: k + 1 for k, v in enumerate(sorted(waited[j]))}
        for op in self.ops:
            op.inc = op.is_dma or (op.cval in waited[op.ctr])

    def wait_val(self, d):
        return d.cval if d.is_dma else self.rank[d.ctr][d.cval]

    def inc_val(self, op):
        return 16 if op.is_dma else 1


def _keep(nc, cm):
    obj = cm.__enter__()
    nc._open_cms = getattr(nc, "_open_cms", [])
    nc._open_cms.append(cm)
    return obj



def _host_constants():
    c = {}
    l = np.arange(T, dtype=np.float64)
    f = np.arange(FT * 128, dtype=np.float64)
    ang = np.outer(l, f) * (2 * np.pi / FFTN)
    c["dftc"] = np.ascontiguousarray(
        np.cos(ang).reshape(LT, 128, FT, 128).transpose(2, 1, 0, 3)).astype(
        ml_dtypes.bfloat16)
    c["dfts"] = np.ascontiguousarray(
        (-np.sin(ang)).reshape(LT, 128, FT, 128).transpose(2, 1, 0, 3)).astype(
        ml_dtypes.bfloat16)
    wf = np.zeros(FT * 128)
    wf[0] = 1.0
    wf[1:2048] = 2.0
    wf[2048] = 1.0
    ang2 = np.outer(f, l) * (2 * np.pi / FFTN)
    c["idftc"] = np.ascontiguousarray(
        ((wf[:, None] * np.cos(ang2)) / FFTN).reshape(FT, 128, LT, 128)
        .transpose(2, 1, 0, 3)).astype(ml_dtypes.bfloat16)
    c["idfts"] = np.ascontiguousarray(
        ((-wf[:, None] * np.sin(ang2)) / FFTN).reshape(FT, 128, LT, 128)
        .transpose(2, 1, 0, 3)).astype(ml_dtypes.bfloat16)
    c["ident_bf"] = np.eye(128, dtype=ml_dtypes.bfloat16)
    c["ident_f32"] = np.eye(128, dtype=np.float32)
    tri = (np.arange(128)[None, :] >= np.arange(128)[:, None])
    c["triu_bf"] = tri.astype(ml_dtypes.bfloat16)
    c["ones_col_bf"] = np.ones((128, 1), dtype=ml_dtypes.bfloat16)
    tri = (np.arange(128)[None, :] >= np.arange(128)[:, None]).astype(np.float32)
    sh = np.zeros((64, 128), dtype=np.float32)
    sh[np.arange(64), 64 + np.arange(64)] = 1.0
    c["shift64_bf"] = sh.astype(ml_dtypes.bfloat16)
    sm_ = np.zeros((4, 128, 512), dtype=np.float32)
    for v in range(4):
        sm_[v, :, v * 128:(v + 1) * 128] = tri
        sm_[v, :, (v + 1) * 128:] = 1.0
    c["spanmask"] = np.ascontiguousarray(sm_.transpose(1, 0, 2)).astype(
        ml_dtypes.bfloat16)
    c["ones_col_f32"] = np.ones((128, 1), dtype=np.float32)
    c["ones_row_f32"] = np.ones((1, 128), dtype=np.float32)
    return c


def _host_prep(inputs):
    x = np.asarray(inputs["L0_attn_x"])
    outd = {}

    def cg(a):
        a = np.asarray(a)
        return np.ascontiguousarray(
            a.reshape(4, 4, 256, 32).transpose(0, 1, 3, 2).reshape(4, 128, 256)
        ).astype(np.float32)

    outd["are_cg"] = cg(inputs["L0_attn_A_re_log"])
    outd["aim_cg"] = cg(inputs["L0_attn_A_im"])
    outd["cre_cg"] = cg(inputs["L0_attn_C_re"])
    outd["cim_cg"] = cg(inputs["L0_attn_C_im"])
    logdt = np.asarray(inputs["L0_attn_log_dt"])
    outd["logdt_cg"] = np.ascontiguousarray(
        np.repeat(logdt.reshape(4, 4, 1, 256), 32, axis=2).reshape(4, 128, 256)
    ).astype(np.float32)
    outd["dskip_cg"] = np.ascontiguousarray(
        np.asarray(inputs["L0_attn_D_skip"]).reshape(4, 4, 256)).astype(np.float32)
    outd["ln_w"] = np.ascontiguousarray(
        np.asarray(inputs["L0_attn_ln_w"]).reshape(4, 8, 128)
        .transpose(0, 2, 1)).astype(np.float32)
    outd["ln_b"] = np.ascontiguousarray(
        np.asarray(inputs["L0_attn_ln_b"]).reshape(4, 8, 128)
        .transpose(0, 2, 1)).astype(np.float32)
    wot = np.asarray(inputs["L0_attn_W_out"]).transpose(0, 2, 1)
    outd["w_out_t"] = np.ascontiguousarray(
        wot.reshape(4, 8, 128, 16, 128).transpose(0, 3, 2, 1, 4)).astype(
        np.float32)
    outd["b_out"] = np.ascontiguousarray(
        np.asarray(inputs["L0_attn_b_out"]).reshape(4, 16, 128)
        .transpose(0, 2, 1)).astype(np.float32)
    w1t = np.asarray(inputs["L0_ffn_W1"]).T
    outd["w1_t"] = np.ascontiguousarray(
        w1t.reshape(8, 128, 32, 128).transpose(2, 1, 0, 3)).astype(np.float32)
    outd["b1"] = np.ascontiguousarray(
        np.asarray(inputs["L0_ffn_b1"]).reshape(32, 128).T).astype(np.float32)
    w2t = np.asarray(inputs["L0_ffn_W2"]).T
    outd["w2_t"] = np.ascontiguousarray(
        w2t.reshape(2, 16, 128, 8, 128).transpose(3, 0, 2, 1, 4)).astype(
        np.float32)
    outd["b2"] = np.ascontiguousarray(
        np.asarray(inputs["L0_ffn_b2"]).reshape(8, 128).T).astype(np.float32)
    for k in ("L1_attn_Wq", "L1_attn_Wk", "L1_attn_Wv",
              "L1_ffn_Wq", "L1_ffn_Wk", "L1_ffn_Wv"):
        w = np.asarray(inputs[k])
        outd[k.lower()] = np.ascontiguousarray(
            w.reshape(8, 128, 8, 128).transpose(2, 1, 0, 3)).astype(np.float32)
    for k in ("L1_attn_Wv", "L1_attn_Wk", "L1_ffn_Wv"):
        w = np.asarray(inputs[k])
        outd[k.lower() + "2"] = np.ascontiguousarray(
            w.reshape(8, 128, 2, 512).transpose(2, 1, 0, 3)).astype(
            np.float32)
    for k in ("L1_attn_Wo", "L1_ffn_Wo"):
        w = np.asarray(inputs[k])
        outd[k.lower()] = np.ascontiguousarray(
            w.reshape(2, 4, 128, 8, 128).transpose(0, 3, 2, 1, 4)).astype(
            np.float32)
    outd["l1_attn_bo"] = np.ascontiguousarray(
        np.asarray(inputs["L1_attn_bo"]).reshape(8, 128).T).astype(np.float32)
    outd["l1_ffn_wg"] = np.ascontiguousarray(
        np.asarray(inputs["L1_ffn_Wg"]).reshape(8, 128).T).astype(np.float32)
    import jax
    with jax.default_device(jax.devices("cpu")[0]):
        keep = np.asarray(jax.random.bernoulli(
            jax.random.key(42, impl="threefry2x32"), 0.8, (8, T, D)))
    maskv = keep.astype(np.float32) * 1.25
    consts = _host_constants()
    per_core = []
    for ci in range(NCORES):
        d = dict(outd)
        d.update(consts)
        d["x_fm"] = np.ascontiguousarray(x[ci].T).astype(np.float32)
        d["dropmask_fm"] = np.ascontiguousarray(maskv[ci].T).astype(
            ml_dtypes.float8_e5m2)
        per_core.append(d)
    return per_core



class Ctx:
    pass


def build_program():
    nc = bass.Bass("TRN2", target_bir_lowering=False, debug=False)
    pb = PB()
    cx = Ctx()
    cx.nc, cx.pb = nc, pb

    dr = {}

    def din(name, shape, dtype=F32):
        dr[name] = nc.dram_tensor(name, list(shape), dtype,
                                  kind="ExternalInput").ap()

    din("x_fm", (D, T))
    for nm, shp in [("are_cg", (4, 128, 256)), ("aim_cg", (4, 128, 256)),
                    ("cre_cg", (4, 128, 256)), ("cim_cg", (4, 128, 256)),
                    ("logdt_cg", (4, 128, 256)), ("dskip_cg", (4, 4, 256)),
                    ("ln_w", (4, 128, 8)), ("ln_b", (4, 128, 8)),
                    ("w_out_t", (4, 16, 128, 8, 128)), ("b_out", (4, 128, 16)),
                    ("w1_t", (32, 128, 8, 128)), ("b1", (128, 32)),
                    ("w2_t", (8, 2, 128, 16, 128)), ("b2", (128, 8)),
                    ("l1_attn_wq", (8, 128, 8, 128)),
                    ("l1_attn_wk", (8, 128, 8, 128)),
                    ("l1_attn_wv", (8, 128, 8, 128)),
                    ("l1_attn_wo", (2, 8, 128, 4, 128)),
                    ("l1_attn_bo", (128, 8)), ("l1_ffn_wg", (128, 8)),
                    ("l1_ffn_wq", (8, 128, 8, 128)),
                    ("l1_ffn_wk", (8, 128, 8, 128)),
                    ("l1_ffn_wv", (8, 128, 8, 128)),
                    ("l1_ffn_wo", (2, 8, 128, 4, 128)),
                    ("l1_attn_wv2", (2, 128, 8, 512)),
                    ("l1_attn_wk2", (2, 128, 8, 512)),
                    ("l1_ffn_wv2", (2, 128, 8, 512))]:
        din(nm, shp)
    for nm, shp in [("dftc", (FT, 128, LT, 128)), ("dfts", (FT, 128, LT, 128)),
                    ("idftc", (LT, 128, FT, 128)), ("idfts", (LT, 128, FT, 128)),
                    ("ident_bf", (128, 128)), ("triu_bf", (128, 128)),
                    ("ones_col_bf", (128, 1)),
                    ("spanmask", (128, 4, 512)),
                    ("shift64_bf", (64, 128))]:
        din(nm, shp, BF16)
    din("ident_f32", (128, 128))
    din("ones_col_f32", (128, 1))
    din("ones_row_f32", (1, 128))
    din("dropmask_fm", (D, T), FP8)
    cx.dr = dr
    cx.out = nc.dram_tensor("out_fm", [D, T], F32, kind="ExternalOutput").ap()
    cx.h_dram = nc.dram_tensor("h_scr", [D, T], F32, kind="Internal").ap()
    cx.x_scr = nc.dram_tensor("x_scr", [D, T], F32, kind="Internal").ap()
    cx.k_scr = nc.dram_tensor("k_scr", [T, D], BF16, kind="Internal").ap()

    cx.AR = _keep(nc, nc.sbuf_tensor("AR", [128, 97792], BF16))
    cx.rw = _keep(nc, nc.sbuf_tensor("rw", [128, 2048], F32))
    cx.cols = _keep(nc, nc.sbuf_tensor("cols", [128, 96], F32))
    cx.cbf = _keep(nc, nc.sbuf_tensor("cbf", [128, 512], BF16))
    cx.cf32 = _keep(nc, nc.sbuf_tensor("cf32", [128, 288], F32))
    cx.ps = [_keep(nc, nc.psum_tensor("ps%d" % i, [128, 512], F32))
             for i in range(8)]

    def av(byte_off, cols, dtype=BF16, npart=128, p0=0):
        esz = mybir.dt.size(dtype)
        assert byte_off % 2 == 0 and (byte_off + cols * esz) <= 97792 * 2
        v = cx.AR[p0:p0 + npart, byte_off // 2: (byte_off + cols * esz) // 2]
        if dtype != BF16:
            v = v.bitcast(dtype)
        return v

    cx.av = av
    cx.w0 = av(135168, 2048, F32)
    cx.w1 = av(143360, 2048, F32)
    cx.w2 = av(151552, 2048, F32)
    cx.sm = av(159744, 2560, F32)
    cx.wst = av(169984, 4096, BF16)
    cx.dst = av(178176, 8704, BF16)
    cx.w0b = av(135168, 4096, BF16)
    cx.w1b = av(143360, 4096, BF16)
    cx.w2b = av(151552, 4096, BF16)

    cx.shift64_bf = cx.cbf[0:64, 260:388]
    cx.ident_bf = cx.cbf[:, 0:128]
    cx.triu_bf = cx.cbf[:, 128:256]
    cx.ones_col_bf = cx.cbf[:, 256:257]
    cx.ident_f32 = cx.cf32[:, 0:128]
    cx.ones_col_f32 = cx.cf32[:, 128:129]
    cx.ones_row_f32 = cx.cf32[0:1, 130:258]

    _build_body(cx)
    pb.finalize()

    sems = {}
    for i, c in enumerate(pb.ctrs):
        name = c if isinstance(c, str) else "%s_%d" % (c[1], c[2])
        sems[i] = _keep(nc, nc.semaphore("S_" + name))

    with nc.Block() as block:
        def runner(s):
            def go(eng):
                for op in pb.ops:
                    if op.stream != s:
                        continue
                    for d in op.waits:
                        eng.wait_ge(sems[d.ctr], pb.wait_val(d))
                    inst = op.fn(eng)
                    if op.inc:
                        assert inst is not None, "inc on None op: " + op.tag
                        inst.then_inc(sems[op.ctr], pb.inc_val(op))
            return go
        block.tensor(runner("pe"))
        block.vector(runner("dve"))
        block.scalar(runner("act"))
        block.gpsimd(runner("pool"))
        block.sync(runner("sp"))
    return nc, cx


def _tap(cx, name, src_ap, shape, dtype, read_keys):
    if name not in _DEBUG_TAPS:
        return
    t = cx.nc.dram_tensor("tap_" + name, list(shape), dtype,
                          kind="ExternalOutput").ap()
    cx.pb.emit("pool", lambda e: e.dma_start(out=t, in_=src_ap),
               reads=list(read_keys), writes=[("tap", name)], dma=True,
               tag="tap_" + name)



def _build_body(cx):
    pb, dr = cx.pb, cx.dr
    pb.emit("sp", lambda e: e.dma_start(out=cx.shift64_bf,
                                         in_=dr["shift64_bf"]),
            writes=[("c", "sh64")], dma=True, tag="ldc_sh64")
    for src, dst, key in [("ident_bf", cx.ident_bf, "idb"),
                          ("triu_bf", cx.triu_bf, "tri"),
                          ("ones_col_bf", cx.ones_col_bf, "ocb"),
                          ("ident_f32", cx.ident_f32, "idf"),
                          ("ones_col_f32", cx.ones_col_f32, "ocf"),
                          ("ones_row_f32", cx.ones_row_f32, "orf")]:
        pb.emit("sp", lambda e, s=src, d=dst: e.dma_start(out=d, in_=dr[s]),
                writes=[("c", key)], dma=True, tag="ldc_" + key)
    for t in range(8):
        pb.emit(("sp", "pool", "act")[t % 3], lambda e, t=t: e.dma_start(
            out=cx.x_scr[t * 128:(t + 1) * 128, :],
            in_=dr["x_fm"][t * 128:(t + 1) * 128, :]),
            writes=[("xscr", t)], dma=True, tag="xinit")

    if "l0_attn" in _PHASES:
        _phase_l0_attn(cx)
    if "l0_ffn" in _PHASES:
        _phase_l0_ffn(cx)
    if "l1_attn" in _PHASES:
        _phase_l1_attn(cx)
    if "l1_ffn" in _PHASES:
        _phase_l1_ffn(cx)

    for t in range(8):
        pb.emit("sp", lambda e, t=t: e.dma_start(
            out=cx.out[t * 128:(t + 1) * 128, :],
            in_=cx.x_scr[t * 128:(t + 1) * 128, :]),
            reads=[("xscr", t)], writes=[("outhbm", t)], dma=True, tag="st_out")
    pb.emit("dve", lambda e: e.engine_nop(),
            reads=[("outhbm", t) for t in range(8)], tag="final")



def _ln_full(cx, src_dram, src_keys, tag, mb, rb, mbk, rbk):
    pb, ps = cx.pb, cx.ps
    sq = cx.av(178176, 2048, F32)
    for t in range(8):
        buf = cx.w0 if t % 2 == 0 else cx.w1
        rkey = ("w0",) if t % 2 == 0 else ("w1",)
        bkey = ("lnbuf", tag, t % 2)
        pb.emit("sp" if t % 2 == 0 else "pool",
                lambda e, t=t, buf=buf: e.dma_start(
                    out=buf, in_=src_dram[t * 128:(t + 1) * 128, :]),
                reads=[src_keys(t)], writes=[bkey, rkey], dma=True,
                tag="lnld" + tag)
        pb.emit("act", lambda e, buf=buf: e.activation(sq, buf, AF.Square),
                reads=[bkey, rkey], writes=[("lnsq", tag), ("dstream", 0)],
                tag="lnsq")
        for s in range(4):
            pb.emit("pe", lambda e, t=t, s=s, buf=buf: e.matmul(
                ps[s][0:1, 0:512], cx.ones_col_f32,
                buf[:, s * 512:(s + 1) * 512], start=(t == 0), stop=(t == 7)),
                reads=[bkey, rkey, ("c", "ocf")], writes=[("ps", s)],
                tag="lns")
            pb.emit("pe", lambda e, t=t, s=s: e.matmul(
                ps[4 + s][0:1, 0:512], cx.ones_col_f32,
                sq[:, s * 512:(s + 1) * 512], start=(t == 0), stop=(t == 7)),
                reads=[("lnsq", tag), ("dstream", 0), ("c", "ocf")],
                writes=[("ps", 4 + s)], tag="lns2")
    _ln_finish(cx, tag, mb, rb, mbk, rbk)


def _ln_finish(cx, tag, mb, rb, mbk, rbk):
    pb, ps = cx.pb, cx.ps
    mrow = mb[0:1, :]
    vrow = rb[0:1, :]
    for s in range(4):
        sl = slice(s * 512, (s + 1) * 512)
        pb.emit("dve", lambda e, sl=sl, s=s: e.tensor_scalar_mul(
            mrow[:, sl], ps[s][0:1, 0:512], 1.0 / D),
            reads=[("ps", s)], writes=[("r_mean", tag)] + mbk, tag="mean")
        pb.emit("dve", lambda e, sl=sl, s=s: e.tensor_scalar_mul(
            vrow[:, sl], ps[4 + s][0:1, 0:512], 1.0 / D),
            reads=[("ps", 4 + s)], writes=[("r_var", tag)] + rbk, tag="ex2")
        pb.emit("dve", lambda e, sl=sl, s=s: e.tensor_tensor(
            ps[s][0:1, 0:512], mrow[:, sl], mrow[:, sl], ALU.mult),
            reads=[("r_mean", tag)] + mbk, writes=[("ps", s)], tag="msq")
        pb.emit("dve", lambda e, sl=sl, s=s: e.tensor_tensor(
            vrow[:, sl], vrow[:, sl], ps[s][0:1, 0:512], ALU.subtract),
            reads=[("r_var", tag), ("ps", s)] + rbk,
            writes=[("r_var", tag)] + rbk, tag="vsub")
    pb.emit("dve", lambda e: e.tensor_scalar_add(vrow, vrow, 1e-5),
            reads=[("r_var", tag)] + rbk, writes=[("r_var", tag)] + rbk,
            tag="eps")
    pb.emit("act", lambda e: e.activation(vrow, vrow, AF.Ln),
            reads=[("r_var", tag)] + rbk, writes=[("r_var", tag)] + rbk,
            tag="lnv")
    pb.emit("act", lambda e: e.activation(vrow, vrow, AF.Exp, scale=-0.5),
            reads=[("r_var", tag)] + rbk, writes=[("r_rstd", tag)] + rbk,
            tag="rstd")
    for s in range(4):
        sl = slice(s * 512, (s + 1) * 512)
        pb.emit("pe", lambda e, sl=sl: e.matmul(
            ps[6][:, 0:512], cx.ones_row_f32, mrow[:, sl],
            start=True, stop=True),
            reads=[("r_mean", tag), ("c", "orf")] + mbk, writes=[("ps", 6)],
            tag="bcm")
        pb.emit("dve", lambda e, sl=sl: e.tensor_copy(mb[:, sl],
                                                      ps[6][:, 0:512]),
            reads=[("ps", 6)], writes=[("mb", tag)] + mbk, tag="bcmc")
        pb.emit("pe", lambda e, sl=sl: e.matmul(
            ps[7][:, 0:512], cx.ones_row_f32, vrow[:, sl],
            start=True, stop=True),
            reads=[("r_rstd", tag), ("c", "orf")] + rbk, writes=[("ps", 7)],
            tag="bcr")
        pb.emit("dve", lambda e, sl=sl: e.tensor_copy(rb[:, sl],
                                                      ps[7][:, 0:512]),
            reads=[("ps", 7)], writes=[("rb", tag)] + rbk, tag="bcrc")


def _l0_module_ln(cx):
    pb = cx.pb
    mb = cx.sm[:, 0:2048]
    rb = cx.w2
    _ln_full(cx, cx.x_scr, lambda t: ("xscr", t), "mln0", mb, rb,
             [("sm",)], [("w2",)])
    for t in range(8):
        buf = cx.w0 if t % 2 == 0 else cx.w1
        rkey = ("w0",) if t % 2 == 0 else ("w1",)
        bkey = ("mlnb", t % 2)
        pb.emit("sp", lambda e, t=t, buf=buf: e.dma_start(
            out=buf, in_=cx.x_scr[t * 128:(t + 1) * 128, :]),
            reads=[("xscr", t)], writes=[bkey, rkey], dma=True, tag="mlnld")
        eng = "dve" if t % 2 == 0 else "pool"
        pb.emit(eng, lambda e, buf=buf: e.tensor_tensor(
            buf, buf, mb, ALU.subtract),
            reads=[bkey, ("mb", "mln0"), ("sm",)], writes=[bkey, rkey],
            tag="mlnsub")
        pb.emit(eng, lambda e, buf=buf: e.tensor_tensor(
            buf, buf, rb, ALU.mult),
            reads=[bkey, ("rb", "mln0"), ("w2",)], writes=[bkey, rkey],
            tag="mlnmul")
        pb.emit("pool", lambda e, t=t, buf=buf: e.dma_start(
            out=cx.h_dram[t * 128:(t + 1) * 128, :], in_=buf),
            reads=[bkey, rkey], writes=[("hdram", t)], dma=True, tag="mlnst")


def _l0_zT(cx, i):
    pb, dr = cx.pb, cx.dr
    tag = "ch%d" % i
    mb = cx.sm[:, 0:2048]
    rb = cx.w2
    _ln_full(cx, cx.h_dram, lambda t: ("hdram", t), tag, mb, rb,
             [("sm",)], [("w2",)])
    pb.emit("sp", lambda e: e.dma_start(
        out=cx.cols[:, 0:8], in_=dr["ln_w"][i]),
        writes=[("lncol", "w"), ("cols",)], dma=True, tag="lnw")
    pb.emit("sp", lambda e: e.dma_start(
        out=cx.cols[:, 8:16], in_=dr["ln_b"][i]),
        writes=[("lncol", "b"), ("cols",)], dma=True, tag="lnb")
    zT = cx.av(0, 16384, BF16)
    for t in range(8):
        buf = cx.w0 if t % 2 == 0 else cx.w1
        rkey = ("w0",) if t % 2 == 0 else ("w1",)
        eng = "dve" if t % 2 == 0 else "pool"
        bkey = ("zbuf", i, t % 2)
        pb.emit("sp", lambda e, t=t, buf=buf: e.dma_start(
            out=buf, in_=cx.h_dram[t * 128:(t + 1) * 128, :]),
            reads=[("hdram", t)], writes=[bkey, rkey], dma=True, tag="zld")
        pb.emit(eng, lambda e, buf=buf: e.tensor_tensor(
            buf, buf, mb, ALU.subtract),
            reads=[bkey, ("mb", tag), ("sm",)], writes=[bkey, rkey],
            tag="zsub")
        pb.emit(eng, lambda e, buf=buf: e.tensor_tensor(
            buf, buf, rb, ALU.mult),
            reads=[bkey, ("rb", tag), ("w2",)], writes=[bkey, rkey],
            tag="zmul")
        pb.emit(eng, lambda e, t=t, buf=buf: e.tensor_scalar(
            buf, buf, cx.cols[:, t:t + 1], cx.cols[:, 8 + t:9 + t],
            ALU.mult, ALU.add),
            reads=[bkey, ("lncol", "w"), ("lncol", "b"), ("cols",)],
            writes=[bkey, rkey], tag="zwb")
        for lb in range(16):
            pk = lb % 2
            pst = cx.ps[pk][:, 0:128]
            pb.emit("pe", lambda e, lb=lb, buf=buf, pst=pst: e.transpose(
                pst, buf[:, lb * 128:(lb + 1) * 128], cx.ident_f32),
                reads=[bkey, rkey, ("c", "idf")], writes=[("ps", pk)],
                tag="ztr")
            pb.emit("dve", lambda e, lb=lb, t=t, pst=pst: e.tensor_copy(
                zT[:, lb * 1024 + t * 128: lb * 1024 + (t + 1) * 128], pst),
                reads=[("ps", pk)], writes=[("zT", lb), ("A1",)], tag="zTev")


def _l0_kbuild_lambda(cx, i):
    pb, ps, dr = cx.pb, cx.ps, cx.dr
    sm = cx.sm
    S = lambda j: sm[:, j * 256:(j + 1) * 256]
    kk = lambda j: ("ksm", i, j)

    def op(stream, fn, rd, wr, tg):
        pb.emit(stream, fn, reads=[kk(j) for j in rd],
                writes=[kk(j) for j in wr] + [("sm",)], tag=tg)

    for nm, slot in [("logdt_cg", 0), ("are_cg", 7), ("aim_cg", 8),
                     ("cre_cg", 3), ("cim_cg", 4)]:
        pb.emit("sp", lambda e, nm=nm, slot=slot: e.dma_start(
            out=S(slot), in_=dr[nm][i]), writes=[kk(slot), ("sm",)], dma=True,
            tag="kld" + nm)
    op("act", lambda e: e.activation(S(0), S(0), AF.Exp), [0], [0], "dt")
    op("act", lambda e: e.activation(S(7), S(7), AF.Exp), [7], [7], "eare")
    op("dve", lambda e: e.tensor_scalar_mul(S(7), S(7), -1.0), [7], [7], "neg")
    op("dve", lambda e: e.tensor_tensor(S(9), S(7), S(0), ALU.mult),
       [7, 0], [9], "dtar")
    op("dve", lambda e: e.tensor_tensor(S(1), S(8), S(0), ALU.mult),
       [8, 0], [1], "dtai")
    op("act", lambda e: e.activation(S(0), S(9), AF.Exp), [9], [0], "er")
    C1 = 6.28125
    C2 = 2 * math.pi - C1
    op("dve", lambda e: e.tensor_scalar(S(2), S(1), 1.0 / (2 * math.pi), 64.0,
                                        ALU.mult, ALU.add), [1], [2], "rr1")
    itmp = cx.w1[:, 1024:1280].bitcast(I32)
    pb.emit("dve", lambda e: e.tensor_copy(itmp, S(2)),
            reads=[kk(2), ("sm",)], writes=[("kitmp",), ("w1",)], tag="rr2a")
    pb.emit("dve", lambda e: e.tensor_copy(S(2), itmp),
            reads=[("kitmp",), ("w1",)], writes=[kk(2), ("sm",)], tag="rr2b")
    op("dve", lambda e: e.tensor_scalar_add(S(2), S(2), -64.0), [2], [2], "rr4")
    op("dve", lambda e: e.tensor_single_scalar(S(9), S(2), C1, ALU.mult),
       [2], [9], "rr5")
    op("dve", lambda e: e.tensor_tensor(S(1), S(1), S(9), ALU.subtract),
       [1, 9], [1], "rr6")
    op("dve", lambda e: e.tensor_single_scalar(S(9), S(2), C2, ALU.mult),
       [2], [9], "rr7")
    op("dve", lambda e: e.tensor_tensor(S(1), S(1), S(9), ALU.subtract),
       [1, 9], [1], "rr8")
    op("dve", lambda e: e.tensor_single_scalar(S(9), S(1), math.pi, ALU.is_gt),
       [1], [9], "f1")
    op("dve", lambda e: e.tensor_single_scalar(S(9), S(9), C1 + C2, ALU.mult),
       [9], [9], "f2")
    op("dve", lambda e: e.tensor_tensor(S(1), S(1), S(9), ALU.subtract),
       [1, 9], [1], "f3")
    op("dve", lambda e: e.tensor_single_scalar(S(9), S(1), -math.pi, ALU.is_lt),
       [1], [9], "f4")
    op("dve", lambda e: e.tensor_single_scalar(S(9), S(9), C1 + C2, ALU.mult),
       [9], [9], "f5")
    op("dve", lambda e: e.tensor_tensor(S(1), S(1), S(9), ALU.add),
       [1, 9], [1], "f6")
    op("act", lambda e: e.activation(S(2), S(1), AF.Sin), [1], [2], "sin")
    op("act", lambda e: e.activation(S(9), S(1), AF.Abs), [1], [9], "abs")
    op("dve", lambda e: e.tensor_scalar(S(9), S(9), -1.0, math.pi / 2,
                                        ALU.mult, ALU.add), [9], [9], "carg")
    op("act", lambda e: e.activation(S(9), S(9), AF.Sin), [9], [9], "cos")
    op("dve", lambda e: e.tensor_tensor(S(1), S(0), S(9), ALU.mult),
       [0, 9], [1], "lr")
    op("dve", lambda e: e.tensor_tensor(S(2), S(0), S(2), ALU.mult),
       [0, 2], [2], "li")
    op("dve", lambda e: e.tensor_tensor(S(0), S(7), S(7), ALU.mult),
       [7], [0], "a2")
    op("dve", lambda e: e.tensor_tensor(S(9), S(8), S(8), ALU.mult),
       [8], [9], "b2")
    op("dve", lambda e: e.tensor_tensor(S(0), S(0), S(9), ALU.add),
       [0, 9], [0], "ab")
    op("dve", lambda e: e.reciprocal(S(0), S(0)), [0], [0], "rec")
    op("dve", lambda e: e.tensor_scalar_add(S(9), S(1), -1.0), [1], [9], "nr")
    op("dve", lambda e: e.tensor_tensor(S(5), S(9), S(7), ALU.mult),
       [9, 7], [5], "e1a")
    op("dve", lambda e: e.tensor_tensor(S(6), S(2), S(8), ALU.mult),
       [2, 8], [6], "e1b")
    op("dve", lambda e: e.tensor_tensor(S(5), S(5), S(6), ALU.add),
       [5, 6], [5], "e1c")
    op("dve", lambda e: e.tensor_tensor(S(5), S(5), S(0), ALU.mult),
       [5, 0], [5], "e1")
    op("dve", lambda e: e.tensor_tensor(S(6), S(2), S(7), ALU.mult),
       [2, 7], [6], "e2a")
    op("dve", lambda e: e.tensor_tensor(S(7), S(9), S(8), ALU.mult),
       [9, 8], [7], "e2b")
    op("dve", lambda e: e.tensor_tensor(S(6), S(6), S(7), ALU.subtract),
       [6, 7], [6], "e2c")
    op("dve", lambda e: e.tensor_tensor(S(6), S(6), S(0), ALU.mult),
       [6, 0], [6], "e2")
    op("dve", lambda e: e.tensor_tensor(S(0), S(3), S(5), ALU.mult),
       [3, 5], [0], "cp1")
    op("dve", lambda e: e.tensor_tensor(S(9), S(4), S(6), ALU.mult),
       [4, 6], [9], "cp2")
    op("dve", lambda e: e.tensor_tensor(S(7), S(0), S(9), ALU.subtract),
       [0, 9], [7], "cpr")
    op("dve", lambda e: e.tensor_tensor(S(0), S(3), S(6), ALU.mult),
       [3, 6], [0], "cp3")
    op("dve", lambda e: e.tensor_tensor(S(9), S(4), S(5), ALU.mult),
       [4, 5], [9], "cp4")
    op("dve", lambda e: e.tensor_tensor(S(8), S(0), S(9), ALU.add),
       [0, 9], [8], "cpi")
    op("dve", lambda e: e.tensor_copy(S(5), S(1)), [1], [5], "l6r")
    op("dve", lambda e: e.tensor_copy(S(6), S(2)), [2], [6], "l6i")
    for _ in range(6):
        op("dve", lambda e: e.tensor_tensor(S(0), S(5), S(5), ALU.mult),
           [5], [0], "sqa")
        op("dve", lambda e: e.tensor_tensor(S(9), S(6), S(6), ALU.mult),
           [6], [9], "sqb")
        op("dve", lambda e: e.tensor_tensor(S(6), S(5), S(6), ALU.mult),
           [5, 6], [6], "sqc")
        op("dve", lambda e: e.tensor_single_scalar(S(6), S(6), 2.0, ALU.mult),
           [6], [6], "sqd")
        op("dve", lambda e: e.tensor_tensor(S(5), S(0), S(9), ALU.subtract),
           [0, 9], [5], "sqe")
    if ("ksm%d" % i) in _DEBUG_TAPS:
        _tap(cx, "ksm%d" % i, cx.sm, (128, 2560), F32,
             [kk(j) for j in range(10)] + [("sm",)])

    op("dve", lambda e: e.memset(S(3), 0.0), [], [3], "dskz")
    pb.emit("pool", lambda e: e.dma_start(
        out=cx.sm[0:97:32, 768:1024], in_=dr["dskip_cg"][i]),
        reads=[kk(3)], writes=[("dskip", i), ("sm",)], dma=True, tag="dskld")


def _l0_kbuild_rest(cx, i, defer=None):
    pb, ps, dr = cx.pb, cx.ps, cx.dr
    _sec = []

    def em(*a, **k):
        _sec.append((a, k))
    sm = cx.sm
    S = lambda j: sm[:, j * 256:(j + 1) * 256]
    kk = lambda j: ("ksm", i, j)
    vre = cx.av(65536, 8192, BF16).rearrange("p (g q) -> p g q", q=64)
    vim = cx.av(81920, 8192, BF16).rearrange("p (g q) -> p g q", q=64)
    ure = cx.av(98304, 4096, BF16).rearrange("p (g r) -> p g r", r=32)
    uim = cx.av(106496, 4096, BF16).rearrange("p (g r) -> p g r", r=32)
    lhs = [cx.av(114688, 4096, BF16), cx.av(122880, 4096, BF16)]
    stg = cx.av(131072, 2048, BF16)
    mr = cx.rw[:, 0:64]
    mi = cx.rw[:, 64:128]
    tr_ = cx.rw[:, 128:192]
    ti_ = cx.rw[:, 192:256]
    mrb16 = cx.rw[:, 256:288].bitcast(BF16)
    mib16 = cx.rw[:, 288:320].bitcast(BF16)
    t1f = cx.rw[:, 320:832]
    t2f = cx.rw[:, 832:1344]

    def mul_chain_ops(gw, lam_r, lam_i, kread, mkey):
        em("dve", lambda e: e.tensor_copy(mr[:, 0:gw], lam_r),
           reads=kread + [("rw",)], writes=[mkey, ("rw",)], tag="vmr")
        em("dve", lambda e: e.tensor_copy(mi[:, 0:gw], lam_i),
           reads=kread + [("rw",)], writes=[mkey, ("rw",)], tag="vmi")

    def msq(gw, mkey):
        mrh, mih = mr[:, 0:gw], mi[:, 0:gw]
        trh, tih = tr_[:, 0:gw], ti_[:, 0:gw]
        em("dve", lambda e: e.tensor_tensor(trh, mrh, mrh, ALU.mult),
           reads=[mkey, ("rw",)], writes=[("mt",), ("rw",)], tag="ms1")
        em("dve", lambda e: e.tensor_tensor(tih, mih, mih, ALU.mult),
           reads=[mkey, ("rw",)], writes=[("mt",), ("rw",)], tag="ms2")
        em("dve", lambda e: e.tensor_tensor(mih, mrh, mih, ALU.mult),
           reads=[mkey, ("rw",)], writes=[mkey, ("rw",)], tag="ms3")
        em("dve", lambda e: e.tensor_single_scalar(mih, mih, 2.0, ALU.mult),
           reads=[mkey, ("rw",)], writes=[mkey, ("rw",)], tag="ms4")
        em("dve", lambda e: e.tensor_tensor(mrh, trh, tih, ALU.subtract),
           reads=[("mt",), mkey, ("rw",)], writes=[mkey, ("rw",)], tag="ms5")

    def dbl_step(R, I, gw, c, w, mkey, vkey, reg, off=0, cast=True):
        if cast:
            em("dve", lambda e: e.tensor_copy(mrb16[:, 0:gw], mr[:, 0:gw]),
               reads=[mkey, ("rw",)], writes=[("mb",), ("rw",)], tag="dc1")
            em("dve", lambda e: e.tensor_copy(mib16[:, 0:gw], mi[:, 0:gw]),
               reads=[mkey, ("rw",)], writes=[("mb",), ("rw",)], tag="dc2")
        mrb = mrb16[:, 0:gw].unsqueeze(2).broadcast_to((128, gw, w))
        mib = mib16[:, 0:gw].unsqueeze(2).broadcast_to((128, gw, w))
        a, b_ = R[:, :, off:off + w], I[:, :, off:off + w]
        o_r, o_i = R[:, :, c + off:c + off + w], I[:, :, c + off:c + off + w]
        t1 = t1f.bitcast(BF16)[:, 0:gw * w].rearrange(
            "p (g w) -> p g w", w=w)
        t2 = t2f.bitcast(BF16)[:, 0:gw * w].rearrange(
            "p (g w) -> p g w", w=w)
        em("dve", lambda e: e.tensor_tensor(t1, a, mrb, ALU.mult),
           reads=[vkey, ("mb",), ("rw",)], writes=[("t1",), ("rw",)], tag="d1")
        em("dve", lambda e: e.tensor_tensor(t2, b_, mib, ALU.mult),
           reads=[vkey, ("mb",), ("rw",)], writes=[("t2",), ("rw",)], tag="d2")
        em("dve", lambda e: e.tensor_tensor(o_r, t1, t2, ALU.subtract),
           reads=[("t1",), ("t2",), ("rw",)], writes=[vkey, reg], tag="d3")
        em("dve", lambda e: e.tensor_tensor(t1, a, mib, ALU.mult),
           reads=[vkey, ("mb",), ("rw",)], writes=[("t1",), ("rw",)], tag="d4")
        em("dve", lambda e: e.tensor_tensor(t2, b_, mrb, ALU.mult),
           reads=[vkey, ("mb",), ("rw",)], writes=[("t2",), ("rw",)], tag="d5")
        em("dve", lambda e: e.tensor_tensor(o_i, t1, t2, ALU.add),
           reads=[("t1",), ("t2",), ("rw",)], writes=[vkey, reg], tag="d6")

    vsecs = []
    for ch in range(4):
        _sec = []
        vsecs.append(_sec)
        g0 = ch * 64
        vsl = slice(g0 % 128, g0 % 128 + 64)
        vkey = ("vb", i, ch, "v")
        R = vre[:, vsl, :]
        I = vim[:, vsl, :]
        em("dve", lambda e, R=R: e.memset(R[:, :, 0:1], 1.0),
           writes=[vkey, ("B1",)], tag="v0r")
        em("dve", lambda e, I=I: e.memset(I[:, :, 0:1], 0.0),
           writes=[vkey, ("B1",)], tag="v0i")
        em("dve", lambda e, R=R, g0=g0: e.tensor_copy(
            R[:, :, 1:2], S(1)[:, g0:g0 + 64]),
            reads=[kk(1)], writes=[vkey, ("B1",)], tag="v1r")
        em("dve", lambda e, I=I, g0=g0: e.tensor_copy(
            I[:, :, 1:2], S(2)[:, g0:g0 + 64]),
            reads=[kk(2)], writes=[vkey, ("B1",)], tag="v1i")
        mkey = ("vm", i, ch)
        mul_chain_ops(64, S(1)[:, g0:g0 + 64], S(2)[:, g0:g0 + 64],
                      [kk(1), kk(2)], mkey)
        c = 2
        while c < 64:
            w = min(c, 64 - c)
            msq(64, mkey)
            for off in range(0, w, 16):
                dbl_step(R, I, 64, c, min(16, w - off), mkey, vkey,
                         ("B1",), off=off, cast=(off == 0))
            c += w
    udves, umps = [], []
    _sec = []
    em("dve", lambda e: e.memset(lhs[0], 0.0),
       writes=[("lhsb", 0), ("B2",)], tag="msA")
    em("dve", lambda e: e.memset(lhs[1], 0.0),
       writes=[("lhsb", 1), ("B2",)], tag="msB")
    pre_u = _sec
    l3r = lhs[0].rearrange("p (gl c) -> p gl c", c=128)
    l3i = lhs[1].rearrange("p (gl c) -> p gl c", c=128)
    for b in range(8):
        _sec = []
        udves.append(_sec)
        g0 = b * 32
        usl = slice(g0 % 128, g0 % 128 + 32)
        vsl0 = g0 % 128
        vkey = ("vb", i, g0 // 64, "v")
        ukey = ("ub", i, b)
        Ru = ure[:, usl, :]
        Iu = uim[:, usl, :]
        em("dve", lambda e, Ru=Ru, g0=g0: e.tensor_single_scalar(
            Ru[:, :, 0:1], S(7)[:, g0:g0 + 32], 2.0, ALU.mult),
            reads=[kk(7)], writes=[ukey, ("B2",)], tag="u0r")
        em("dve", lambda e, Iu=Iu, g0=g0: e.tensor_single_scalar(
            Iu[:, :, 0:1], S(8)[:, g0:g0 + 32], 2.0, ALU.mult),
            reads=[kk(8)], writes=[ukey, ("B2",)], tag="u0i")
        mkey = ("um", i, b)
        mul_chain_ops(32, S(5)[:, g0:g0 + 32], S(6)[:, g0:g0 + 32],
                      [kk(5), kk(6)], mkey)
        c = 1
        while c < 32:
            w = min(c, 32 - c)
            if c > 1:
                msq(32, mkey)
            dbl_step(Ru, Iu, 32, c, w, mkey, ukey, ("B2",))
            c += w
        del c
        for c4 in range(4):
            rows = slice(c4 * 32, (c4 + 1) * 32)
            csl = slice(c4 * 32, c4 * 32 + 32)
            em("dve", lambda e, rows=rows, usl=usl, csl=csl:
               e.tensor_copy(l3r[rows, :, csl], ure[rows, usl, :]),
               reads=[ukey, ("B2",)], writes=[("lhsb", 0), ("B2",)],
               tag="lscR")
            em("dve", lambda e, rows=rows, usl=usl, csl=csl:
               e.tensor_single_scalar(l3i[rows, :, csl],
                                      uim[rows, usl, :], -1.0, ALU.mult),
               reads=[ukey, ("B2",)], writes=[("lhsb", 1), ("B2",)],
               tag="lscI")
        _sec = []
        umps.append(_sec)
        for j in range(8):
            bank = 4 + j % 4
            pk = ("ps", bank)
            for k in range(4):
                gl = 4 * j + k
                csl_ = slice(k * 64, (k + 1) * 64)
                em("pe", lambda e, gl=gl, vsl0=vsl0, bank=bank, csl_=csl_:
                   e.matmul(cx.ps[bank][:, csl_], l3r[:, gl, :],
                            vre[:, vsl0 + gl, :], start=True, stop=False),
                   reads=[("lhsb", 0), vkey, ("B1",), ("B2",)],
                   writes=[pk], tag="kmmR")
                em("pe", lambda e, gl=gl, vsl0=vsl0, bank=bank, csl_=csl_:
                   e.matmul(cx.ps[bank][:, csl_], l3i[:, gl, :],
                            vim[:, vsl0 + gl, :], start=False, stop=True),
                   reads=[("lhsb", 1), vkey, ("B1",), ("B2",)],
                   writes=[pk], tag="kmmI")
            em("act", lambda e, j=j, bank=bank: e.activation(
                stg.rearrange("p (q gl) -> p q gl", gl=32)[
                    :, :, 4 * j:4 * j + 4],
                cx.ps[bank][:, 0:256].rearrange("p (g q) -> p q g", g=4),
                AF.Copy),
                reads=[pk], writes=[("stg",), ("B2",)], tag="kev")
        gsg = slice(768 + b * 32, 768 + (b + 1) * 32)
        em("dve", lambda e, gsg=gsg: e.tensor_tensor(
            stg[:, 0:32], stg[:, 0:32], cx.sm[:, gsg], ALU.add),
            reads=[("stg",), ("dskip", i), ("sm",)], writes=[("stg",)],
            tag="kdsk")
        for c4 in range(4):
            src_ = stg[c4 * 32:(c4 + 1) * 32, :].rearrange(
                "r (q gl) -> r q gl", gl=32)
            dst_ = cx.k_scr.rearrange(
                "(r q) (c4 bb gl) -> r q c4 bb gl", q=64, c4=4, bb=8)[
                0:32, :, c4, b, :]
            em("sp", lambda e, src_=src_, dst_=dst_: e.dma_start(
                out=dst_, in_=src_),
                reads=[("stg",)], writes=[("kscr", b, c4)], dma=True,
                tag="kdump")
    _sec = []
    em("dve", lambda e: e.engine_nop(),
       reads=[("kscr", b, c4) for b in range(8) for c4 in range(4)],
       writes=[("vb", i, 0, "v"), ("vb", i, 1, "v"), ("vb", i, 2, "v"),
               ("vb", i, 3, "v")] +
              [("ub", i, b) for b in range(8)] +
              [("lhsb", 0), ("lhsb", 1), ("stg",), ("Bfree", i)], tag="kbar")
    if defer is None and ("kscr%d" % i) in _DEBUG_TAPS:
        _tap(cx, "kscr%d" % i, cx.k_scr, (T, D), BF16,
             [("kscr", b2, c2) for b2 in range(8) for c2 in range(4)])
    kT = cx.av(65536, 34816, BF16)
    for t in range(16):
        em("act", lambda e, t=t: e.dma_start(
            out=kT[:, t * 1024:(t + 1) * 1024],
            in_=cx.k_scr[t * 128:(t + 1) * 128, :]),
            reads=[("Bfree", i)] +
                  [("kscr", b2, c2) for b2 in range(8) for c2 in range(4)],
            writes=[("kT", t), ("B1",)], dma=True, tag="kTld")
    tail = _sec
    groups = [vsecs[0] + vsecs[1],
              pre_u + udves[0],
              umps[0] + udves[1],
              umps[1] + udves[2],
              vsecs[2] + umps[2] + udves[3],
              umps[3] + udves[4],
              vsecs[3] + umps[4] + udves[5],
              umps[5] + udves[6],
              umps[6] + udves[7],
              umps[7] + tail]
    if defer is None:
        for sec in groups:
            for a, k in sec:
                pb.emit(*a, **k)
        return None
    cuts = []
    for sec in groups:
        defer.extend(sec)
        cuts.append(len(defer))
    return cuts


def _l0_conv(cx, i, nxt=None, kq=None, vcut=0):
    pb, ps = cx.pb, cx.ps
    zT = cx.av(0, 16384, BF16)
    kT = cx.av(65536, 34816, BF16)
    A2 = cx.av(32768, 16384, BF16)
    KREG = cx.av(65536, 34816, BF16)

    def p_re(ft):
        if ft < 16:
            return A2[:, ft * 1024:(ft + 1) * 1024]
        return cx.wst[:, 0:1024]

    def p_im(ft):
        if ft < 16:
            return KREG[:, 16384 + ft * 1024:16384 + (ft + 1) * 1024]
        return KREG[:, 32768:33792]

    for ft in range(FT):
        half = ft % 2
        cosb = cx.dst[:, half * 4352: half * 4352 + 2048]
        sinb = cx.dst[:, half * 4352 + 2048: half * 4352 + 4096]
        ck = ("dstream", half)
        pb.emit("sp", lambda e, ft=ft, cosb=cosb: e.dma_start(
            out=cosb, in_=cx.dr["dftc"][ft].rearrange("p a c -> p (a c)")),
            writes=[ck], dma=True, tag="dftc")
        pb.emit("sp", lambda e, ft=ft, sinb=sinb: e.dma_start(
            out=sinb, in_=cx.dr["dfts"][ft].rearrange("p a c -> p (a c)")),
            writes=[ck], dma=True, tag="dfts")
        for lb in range(16):
            st, sp_ = (lb == 0), (lb == 15)
            for comp, buf in ((0, cosb), (1, sinb)):
                for s in range(2):
                    base = 4 * ((2 * ft + s) % 2)
                    for kind in range(2):
                        bank = base + comp + 2 * kind
                        rhs_t = zT if kind == 0 else kT
                        rkey = ("zT", lb) if kind == 0 else ("kT", lb)
                        extra = [("A1",)] if kind == 0 else [("B1",)]
                        pb.emit("pe", lambda e, lb=lb, s=s, buf=buf,
                                bank=bank, rhs_t=rhs_t, st=st, sp_=sp_:
                                e.matmul(ps[bank][:, :],
                                         buf[:, lb * 128:(lb + 1) * 128],
                                         rhs_t[:, lb * 1024 + s * 512:
                                               lb * 1024 + (s + 1) * 512],
                                         start=st, stop=sp_),
                                reads=[ck, rkey] + extra,
                                writes=[("ps", bank)], tag="cfmm")
        for s in range(2):
            base = 4 * ((2 * ft + s) % 2)
            zr = cx.w0b[:, 0:512]
            zi = cx.w0b[:, 512:1024]
            kr = cx.w0b[:, 1024:1536]
            ki = cx.w0b[:, 1536:2048]
            for j, dstv in ((base + 0, zr), (base + 1, zi), (base + 2, kr),
                            (base + 3, ki)):
                pb.emit("dve", lambda e, j=j, dstv=dstv: e.tensor_copy(
                    dstv, ps[j][:, :]),
                    reads=[("ps", j)], writes=[("pw", i), ("w0",)],
                    tag="spev")
            t1 = cx.w1b[:, 0:512]
            t2 = cx.w1b[:, 512:1024]
            pr = p_re(ft)[:, s * 512:(s + 1) * 512]
            pi = p_im(ft)[:, s * 512:(s + 1) * 512]
            pb.emit("dve", lambda e, t1=t1, zr=zr, kr=kr: e.tensor_tensor(
                t1, zr, kr, ALU.mult), reads=[("pw", i), ("w0",)],
                writes=[("pwt", i), ("w1",)], tag="pw1")
            pb.emit("dve", lambda e, t2=t2, zi=zi, ki=ki: e.tensor_tensor(
                t2, zi, ki, ALU.mult), reads=[("pw", i), ("w0",)],
                writes=[("pwt", i), ("w1",)], tag="pw2")
            pb.emit("dve", lambda e, pr=pr, t1=t1, t2=t2: e.tensor_tensor(
                pr, t1, t2, ALU.subtract), reads=[("pwt", i), ("w1",)],
                writes=[("P", ft, 0)] + ([("A2",)] if ft < 16 else [("wst",)]),
                tag="pw3")
            pb.emit("dve", lambda e, t1=t1, zr=zr, ki=ki: e.tensor_tensor(
                t1, zr, ki, ALU.mult), reads=[("pw", i), ("w0",)],
                writes=[("pwt", i), ("w1",)], tag="pw4")
            pb.emit("dve", lambda e, t2=t2, zi=zi, kr=kr: e.tensor_tensor(
                t2, zi, kr, ALU.mult), reads=[("pw", i), ("w0",)],
                writes=[("pwt", i), ("w1",)], tag="pw5")
            pb.emit("dve", lambda e, pi=pi, t1=t1, t2=t2: e.tensor_tensor(
                pi, t1, t2, ALU.add), reads=[("pwt", i), ("w1",)],
                writes=[("P", ft, 1), ("B2",)], tag="pw6")
    pb.emit("dve", lambda e: e.engine_nop(),
            reads=[("P", ft, c) for ft in range(FT) for c in range(2)],
            writes=[("zT", lb) for lb in range(16)] +
                   [("kT", t) for t in range(16)] + [("convfwd", i)],
            tag="fbar")
    if nxt is not None:
        _l0_kbuild_lambda(cx, nxt)
    vflushed = [0]

    def vflush(j):
        if not kq:
            return
        hi = min(vcut, (vcut * j) // 10)
        while vflushed[0] < hi:
            a, kw = kq[vflushed[0]]
            pb.emit(*a, **kw)
            vflushed[0] += 1
    ygT = cx.av(0, 16384, BF16)
    for lt in range(16):
        half = lt % 2
        icb = cx.dst[:, half * 4352: half * 4352 + 2176]
        isb = cx.dst[:, half * 4352 + 2176: half * 4352 + 4352]
        ik = ("dstream", half)
        pb.emit("sp", lambda e, lt=lt, icb=icb: e.dma_start(
            out=icb, in_=cx.dr["idftc"][lt].rearrange("p a c -> p (a c)")),
            writes=[ik], dma=True, tag="idftc")
        pb.emit("sp", lambda e, lt=lt, isb=isb: e.dma_start(
            out=isb, in_=cx.dr["idfts"][lt].rearrange("p a c -> p (a c)")),
            writes=[ik], dma=True, tag="idfts")
        bank0 = (lt % 2) * 2
        n = 0
        for ft in range(FT):
            for comp in range(2):
                buf = icb if comp == 0 else isb
                pv = p_re(ft) if comp == 0 else p_im(ft)
                st = (n == 0)
                sp_ = (n == 2 * FT - 1)
                for s in range(2):
                    pb.emit("pe", lambda e, ft=ft, buf=buf, pv=pv, s=s,
                            st=st, sp_=sp_, bank0=bank0: e.matmul(
                                ps[bank0 + s][:, :],
                                buf[:, ft * 128:(ft + 1) * 128],
                                pv[:, s * 512:(s + 1) * 512], start=st,
                                stop=sp_),
                            reads=[ik, ("P", ft, comp), ("convfwd", i),
                                   ("A2",), ("B2",), ("wst",)],
                            writes=[("ps", bank0 + s)], tag="idft")
                n += 1
        for s in range(2):
            pb.emit("act", lambda e, lt=lt, s=s, bank0=bank0: e.activation(
                ygT[:, lt * 1024 + s * 512: lt * 1024 + (s + 1) * 512],
                ps[bank0 + s][:, :], AF.Gelu_apprx_tanh),
                reads=[("ps", bank0 + s)], writes=[("ygT", lt), ("A1",)],
                tag="gelu")
        vflush(lt + 1)
    pb.emit("dve", lambda e: e.engine_nop(),
            reads=[("ygT", lt) for lt in range(16)],
            writes=[("P", ft, c) for ft in range(FT) for c in range(2)] +
                   [("idone", i)], tag="ibar")


def _l0_wout(cx, i, kq=None, cuts=None):
    pb, ps, dr = cx.pb, cx.ps, cx.dr
    kflushed = [cuts[0] if cuts else 0]

    def kflush(j):
        if not kq:
            return
        hi = cuts[min(j, len(cuts) - 1)]
        while kflushed[0] < hi:
            a, kw = kq[kflushed[0]]
            pb.emit(*a, **kw)
            kflushed[0] += 1
    ygT = cx.av(0, 16384, BF16)
    ygfm = cx.av(32768, 16384, BF16)
    for c in range(8):
        for lt in range(16):
            pk = lt % 2
            pst = ps[pk][:, 0:64].bitcast(BF16)
            pb.emit("pe", lambda e, c=c, lt=lt, pst=pst: e.transpose(
                pst, ygT[:, lt * 1024 + c * 128:lt * 1024 + (c + 1) * 128],
                cx.ident_bf),
                reads=[("ygT", lt), ("idone", i), ("A1",), ("c", "idb")],
                writes=[("ps", pk)], tag="ytr")
            pb.emit("dve", lambda e, c=c, lt=lt, pst=pst: e.tensor_copy(
                ygfm[:, c * 2048 + lt * 128:c * 2048 + (lt + 1) * 128], pst),
                reads=[("ps", pk)], writes=[("ygfm", c), ("A2",)],
                tag="ytrev")
    pb.emit("sp", lambda e: e.dma_start(
        out=cx.cols[:, 16:32],
        in_=dr["b_out"][i]),
        writes=[("bocol", i), ("cols",)], dma=True, tag="bout")
    for lo in range(8):
        wslots = {}
        for part, ot in ((0, lo), (1, lo + 8)):
            wslot = cx.wst[:, (2 * lo + part) % 4 * 1024:
                           ((2 * lo + part) % 4 + 1) * 1024]
            wkey = ("wostream", (2 * lo + part) % 4)
            wslots[part] = (wslot, wkey)
            pb.emit("pool", lambda e, ot=ot, wslot=wslot: e.dma_start(
                out=wslot,
                in_=dr["w_out_t"][i, ot].rearrange("p a c -> p (a c)")),
                reads=[("idone", i)], writes=[wkey, ("wst",)], dma=True,
                tag="wold")
        for r in range(2):
            for part in range(2):
                wslot, wkey = wslots[part]
                for ha in range(8):
                    for si in range(2):
                        s = 2 * r + si
                        bank = part * 2 + si
                        pb.emit("pe", lambda e, s=s, ha=ha, bank=bank,
                                wslot=wslot: e.matmul(
                                    ps[bank][:, :],
                                    wslot[:, ha * 128:(ha + 1) * 128],
                                    ygfm[:, ha * 2048 + s * 512:
                                         ha * 2048 + (s + 1) * 512],
                                    start=(ha == 0), stop=(ha == 7)),
                                reads=[wkey, ("ygfm", ha), ("A2",)],
                                writes=[("ps", bank)], tag="womm")
            for si in range(2):
                s = 2 * r + si
                sig = cx.w0b[:, s * 512:(s + 1) * 512]
                pb.emit("act", lambda e, si=si, lo=lo, sig=sig: e.activation(
                    sig, ps[2 + si][:, :], AF.Sigmoid,
                    bias=cx.cols[:, 16 + lo + 8:17 + lo + 8]),
                    reads=[("ps", 2 + si), ("bocol", i), ("cols",)],
                    writes=[("glu", "s"), ("w0",)], tag="sig")
                lin = cx.w1b[:, s * 512:(s + 1) * 512]
                pb.emit("act", lambda e, si=si, lo=lo, lin=lin: e.activation(
                    lin, ps[si][:, :], AF.Identity,
                    bias=cx.cols[:, 16 + lo:17 + lo]),
                    reads=[("ps", si), ("bocol", i), ("cols",)],
                    writes=[("glu", "l"), ("w1",)], tag="lin")
        zo = cx.dst[:, 0:2048]
        pb.emit("dve", lambda e, zo=zo: e.tensor_tensor(
            zo, cx.w1b[:, 0:2048], cx.w0b[:, 0:2048], ALU.mult),
            reads=[("glu", "s"), ("glu", "l"), ("w0",), ("w1",)],
            writes=[("zo", lo), ("dstream", 0)], tag="glu")
        pb.emit("sp", lambda e, lo=lo: e.dma_start(
            out=cx.w2, in_=cx.h_dram[lo * 128:(lo + 1) * 128, :]),
            reads=[("hdram", lo)], writes=[("hres", lo), ("w2",)], dma=True,
            tag="hld")
        pb.emit("dve", lambda e, lo=lo, zo=zo: e.tensor_tensor(
            cx.w2, cx.w2, zo, ALU.add),
            reads=[("hres", lo), ("zo", lo), ("dstream", 0)],
            writes=[("hres", lo), ("w2",)], tag="hadd")
        pb.emit("pool", lambda e, lo=lo: e.dma_start(
            out=cx.h_dram[lo * 128:(lo + 1) * 128, :], in_=cx.w2),
            reads=[("hres", lo), ("w2",)], writes=[("hdram", lo)], dma=True,
            tag="hst")
        kflush(lo + 2)
    kflush(9)
    assert kflushed[0] == (len(kq) if kq else 0)
    pb.emit("dve", lambda e: e.engine_nop(),
            reads=[("hdram", t) for t in range(8)],
            writes=[("ygT", lt) for lt in range(16)] +
                   [("ygfm", c) for c in range(8)] + [("ldone", i)],
            tag="wbar")


def _phase_l0_attn(cx):
    pb, ps = cx.pb, cx.ps
    _l0_module_ln(cx)
    _tap(cx, "h0", cx.h_dram, (D, T), F32, [("hdram", t) for t in range(8)])
    layers = list(_LAYER_LIST if _LAYER_LIST is not None
                  else range(_NLAYERS))
    for li, i in enumerate(layers):
        _l0_zT(cx, i)
        if "zT%d" % i in _DEBUG_TAPS:
            _tap(cx, "zT%d" % i, cx.av(0, 16384, BF16), (128, 16384), BF16,
                 [("zT", lb) for lb in range(16)])
        if li == 0:
            _l0_kbuild_lambda(cx, i)
            _l0_kbuild_rest(cx, i)
        if "kT%d" % i in _DEBUG_TAPS:
            _tap(cx, "kT%d" % i, cx.av(65536, 16384, BF16), (128, 16384), BF16,
                 [("kT", t) for t in range(16)])
        nxt = layers[li + 1] if li + 1 < len(layers) else None
        kq, cuts = None, None
        if nxt is not None:
            kq = []
            cuts = _l0_kbuild_rest(cx, nxt, defer=kq)
        _l0_conv(cx, i, nxt=nxt, kq=kq, vcut=(cuts[0] if cuts else 0))
        if "ygT%d" % i in _DEBUG_TAPS:
            _tap(cx, "ygT%d" % i, cx.av(0, 16384, BF16), (128, 16384), BF16,
                 [("ygT", lt) for lt in range(16)])
        _l0_wout(cx, i, kq=kq, cuts=cuts)
        _tap(cx, "h%d" % (i + 1), cx.h_dram, (D, T), F32,
             [("hdram", t) for t in range(8)])
    sq = cx.av(178176, 2048, F32)
    for t in range(8):
        buf = cx.w0 if t % 2 == 0 else cx.w1
        rkey = ("w0",) if t % 2 == 0 else ("w1",)
        pb.emit("sp", lambda e, t=t, buf=buf: e.dma_start(
            out=buf, in_=cx.x_scr[t * 128:(t + 1) * 128, :]),
            reads=[("xscr", t)], writes=[("xres", t)] + [rkey], dma=True,
            tag="xld")
        pb.emit("pool", lambda e, t=t: e.dma_start(
            out=cx.w2, in_=cx.h_dram[t * 128:(t + 1) * 128, :]),
            reads=[("hdram", t)], writes=[("xres2", t), ("w2",)], dma=True,
            tag="xld2")
        pb.emit("dve", lambda e, t=t, buf=buf: e.tensor_tensor(
            buf, buf, cx.w2, ALU.add),
            reads=[("xres", t), ("xres2", t), ("w2",)] + [rkey],
            writes=[("xres", t)] + [rkey], tag="xadd")
        pb.emit("act", lambda e, buf=buf: e.activation(sq, buf, AF.Square),
                reads=[("xres", t)] + [rkey],
                writes=[("lnsq", "ffn"), ("dstream", 0)], tag="xsq")
        for s in range(4):
            pb.emit("pe", lambda e, t=t, s=s, buf=buf: e.matmul(
                ps[s][0:1, 0:512], cx.ones_col_f32,
                buf[:, s * 512:(s + 1) * 512], start=(t == 0), stop=(t == 7)),
                reads=[("xres", t), ("c", "ocf")] + [rkey],
                writes=[("ps", s)], tag="xlns")
            pb.emit("pe", lambda e, t=t, s=s: e.matmul(
                ps[4 + s][0:1, 0:512], cx.ones_col_f32,
                sq[:, s * 512:(s + 1) * 512], start=(t == 0), stop=(t == 7)),
                reads=[("lnsq", "ffn"), ("dstream", 0), ("c", "ocf")],
                writes=[("ps", 4 + s)], tag="xlns2")
        pb.emit("pool", lambda e, t=t, buf=buf: e.dma_start(
            out=cx.x_scr[t * 128:(t + 1) * 128, :], in_=buf),
            reads=[("xres", t)] + [rkey], writes=[("xscr", t)], dma=True,
            tag="xst")


def _phase_l0_ffn(cx):
    pb, ps, dr = cx.pb, cx.ps, cx.dr
    mb = cx.sm[:, 0:2048]
    rb = cx.w2
    _ln_finish(cx, "ffn", mb, rb, [("sm",)], [("w2",)])
    xq = [cx.av(t * 2048, 2048, FP8) for t in range(8)]
    for t in range(8):
        buf = cx.w0 if t % 2 == 0 else cx.w1
        rkey = ("w0",) if t % 2 == 0 else ("w1",)
        bkey = ("fxb", t % 2)
        pb.emit("sp", lambda e, t=t, buf=buf: e.dma_start(
            out=buf, in_=cx.x_scr[t * 128:(t + 1) * 128, :]),
            reads=[("xscr", t)], writes=[bkey, rkey], dma=True, tag="fxld")
        eng = "dve" if t % 2 == 0 else "pool"
        pb.emit(eng, lambda e, buf=buf: e.tensor_tensor(
            buf, buf, mb, ALU.subtract),
            reads=[bkey, ("mb", "ffn"), ("sm",)], writes=[bkey, rkey],
            tag="fxsub")
        pb.emit(eng, lambda e, t=t, buf=buf, rkey=rkey: e.tensor_tensor(
            xq[t], buf, rb, ALU.mult),
            reads=[bkey, rkey, ("rb", "ffn"), ("w2",)],
            writes=[("xq", t), ("A1",)], tag="fxq")
    qc = cx.av(61440, 256, FP8)
    pb.emit("sp", lambda e: e.dma_start(
        out=cx.cols[:, 32:64], in_=dr["b1"]),
        writes=[("bcol", 1), ("cols",)], dma=True, tag="b1ld")
    pb.emit("sp", lambda e: e.dma_start(
        out=cx.cols[:, 64:72], in_=dr["b2"]),
        writes=[("bcol", 2), ("cols",)], dma=True, tag="b2ld")
    pb.emit("dve", lambda e: e.tensor_copy(qc[:, 0:40], cx.cols[:, 32:72]),
            reads=[("bcol", 1), ("bcol", 2), ("cols",)],
            writes=[("qc",), ("A2",)], tag="bq1")
    pb.emit("dve", lambda e: e.tensor_copy(cx.cols[:, 32:72], qc[:, 0:40]),
            reads=[("qc",), ("A2",)], writes=[("bcol", 1), ("bcol", 2),
                                              ("cols",)], tag="bq2")

    h1q = [cx.av(16384 + nt * 1024, 1024, FP8) for nt in range(32)]
    wq1 = [cx.av(49152 + j * 1024, 1024, FP8) for j in range(2)]
    wq2 = [cx.av(51200 + j * 4096, 4096, FP8) for j in range(2)]
    msk = [cx.av(59392 + j * 1024, 1024, FP8) for j in range(2)]
    for half in range(2):
        hc = slice(half * 1024, (half + 1) * 1024)
        for nt in range(32):
            wslot = wq1[nt % 2]
            wkey = ("wq1", nt % 2)
            fkey = ("w1",)
            pb.emit("sp", lambda e, nt=nt: e.dma_start(
                out=cx.w1[:, 0:1024],
                in_=dr["w1_t"][nt].rearrange("p a c -> p (a c)")),
                writes=[("wf1",), fkey], dma=True, tag="w1ld")
            pb.emit("dve", lambda e, wslot=wslot: e.tensor_copy(
                wslot, cx.w1b.bitcast(F32)[:, 0:1024] if False else
                cx.w1[:, 0:1024]),
                reads=[("wf1",), fkey], writes=[wkey, ("A2",)], tag="w1q")
            for a in range(8):
                for s in range(2):
                    bank = (nt % 4) * 2 + s
                    pb.emit("pe", lambda e, nt=nt, s=s, a=a, bank=bank,
                            wslot=wslot, hc=hc: e.matmul(
                                ps[bank][:, :], wslot[:, a * 128:(a + 1) * 128],
                                xq[a][:, hc][:, s * 512:(s + 1) * 512],
                                start=(a == 0), stop=(a == 7)),
                            reads=[wkey, ("xq", a), ("A1",), ("A2",)],
                            writes=[("ps", bank)], tag="h1mm")
            for s in range(2):
                bank = (nt % 4) * 2 + s
                pb.emit("dve", lambda e, nt=nt, s=s, bank=bank: e.tensor_scalar(
                    cx.w0b[:, s * 512:(s + 1) * 512], ps[bank][:, :],
                    cx.cols[:, 32 + nt:33 + nt], 0.0, ALU.add, ALU.max),
                    reads=[("ps", bank), ("bcol", 1), ("cols",)],
                    writes=[("h1bf", nt % 2), ("w0",)], tag="h1ev")
            pb.emit("dve", lambda e, nt=nt: e.tensor_copy(
                h1q[nt], cx.w0b[:, 0:1024]),
                reads=[("h1bf", nt % 2), ("w0",)],
                writes=[("h1q", nt), ("A1",), ("A2",)], tag="h1q")
        for dt_ in range(8):
            wslot = wq2[dt_ % 2]
            wkey = ("wq2", dt_ % 2)
            for j in range(2):
                pb.emit("sp" if j == 0 else "pool",
                        lambda e, dt_=dt_, j=j: e.dma_start(
                            out=cx.w1[:, 0:2048] if j == 0
                            else cx.w0[:, 0:2048],
                            in_=dr["w2_t"][dt_, j].rearrange(
                                "p a c -> p (a c)")),
                        writes=[("wf2", j), ("w1",) if j == 0 else ("w0",)],
                        dma=True, tag="w2ld")
                pb.emit("dve" if j == 0 else "pool",
                        lambda e, wslot=wslot, j=j: e.tensor_copy(
                            wslot[:, j * 2048:(j + 1) * 2048],
                            cx.w1[:, 0:2048] if j == 0
                            else cx.w0[:, 0:2048]),
                        reads=[("wf2", j), ("w1",) if j == 0 else ("w0",)],
                        writes=[wkey, ("A2",)], tag="w2q")
            mslot = msk[dt_ % 2]
            pb.emit("pool", lambda e, dt_=dt_, mslot=mslot, hc=hc:
                    e.dma_start(out=mslot,
                                in_=dr["dropmask_fm"][
                                    dt_ * 128:(dt_ + 1) * 128, hc]),
                    writes=[("msk", dt_ % 2), ("A2",)], dma=True, tag="mskld")
            pb.emit("sp", lambda e, dt_=dt_, hc=hc: e.dma_start(
                out=cx.w2[:, 0:1024],
                in_=cx.x_scr[dt_ * 128:(dt_ + 1) * 128, hc]),
                reads=[("xscr", dt_)], writes=[("xr", dt_ % 2), ("w2",)],
                dma=True, tag="fxrld")
            for nt in range(32):
                for s in range(2):
                    bank = (dt_ % 2) * 4 + s
                    pb.emit("pe", lambda e, s=s, nt=nt, bank=bank,
                            wslot=wslot: e.matmul(
                                ps[bank][:, :],
                                wslot[:, nt * 128:(nt + 1) * 128],
                                h1q[nt][:, s * 512:(s + 1) * 512],
                                start=(nt == 0), stop=(nt == 31)),
                            reads=[wkey, ("h1q", nt), ("A1",), ("A2",)],
                            writes=[("ps", bank)], tag="h2mm")
            smb = cx.av(163840, 1024, BF16)
            for s in range(2):
                bank = (dt_ % 2) * 4 + s
                pb.emit("dve", lambda e, dt_=dt_, s=s, bank=bank, smb=smb:
                        e.tensor_scalar(
                            smb[:, s * 512:(s + 1) * 512], ps[bank][:, :],
                            cx.cols[:, 64 + dt_:65 + dt_], None, ALU.add),
                        reads=[("ps", bank), ("bcol", 2), ("cols",)],
                        writes=[("h2bf", dt_ % 2), ("sm",)], tag="h2ev")
                pb.emit("dve", lambda e, s=s, mslot=mslot, dt_=dt_, smb=smb:
                        e.tensor_tensor(
                            cx.sm[:, s * 512:(s + 1) * 512],
                            smb[:, s * 512:(s + 1) * 512],
                            mslot[:, s * 512:(s + 1) * 512], ALU.mult),
                        reads=[("h2bf", dt_ % 2), ("msk", dt_ % 2), ("sm",),
                               ("A2",)],
                        writes=[("h2f", dt_ % 2), ("sm",)], tag="h2msk")
                pb.emit("dve", lambda e, s=s: e.tensor_tensor(
                    cx.w2[:, s * 512:(s + 1) * 512],
                    cx.w2[:, s * 512:(s + 1) * 512],
                    cx.sm[:, s * 512:(s + 1) * 512], ALU.add),
                    reads=[("h2f", dt_ % 2), ("xr", dt_ % 2), ("sm",),
                           ("w2",)],
                    writes=[("xr", dt_ % 2), ("w2",)], tag="fxadd")
            pb.emit("pool", lambda e, dt_=dt_, hc=hc: e.dma_start(
                out=cx.x_scr[dt_ * 128:(dt_ + 1) * 128, hc],
                in_=cx.w2[:, 0:1024]),
                reads=[("xr", dt_ % 2), ("w2",)], writes=[("xscr", dt_)],
                dma=True, tag="fxst")


def _tokmaj_proj(cx, w2name, half, xln, evac, tag):
    pb, ps, dr = cx.pb, cx.ps, cx.dr
    wstb = cx.wst
    for chunk in range(2):
        buf = cx.w0 if chunk == 0 else cx.w1
        fkey = ("w0",) if chunk == 0 else ("w1",)
        pb.emit("sp" if chunk == 0 else "pool",
                lambda e, chunk=chunk, buf=buf: e.dma_start(
                    out=buf[:, 0:2048],
                    in_=dr[w2name][half, :, chunk * 4:(chunk + 1) * 4, :]
                    .rearrange("p a c -> p (a c)")),
                writes=[("tmw", tag, chunk), fkey], dma=True,
                tag="tmld" + tag)
        pb.emit("dve", lambda e, chunk=chunk, buf=buf: e.tensor_copy(
            wstb[:, chunk * 2048:(chunk + 1) * 2048], buf[:, 0:2048]),
            reads=[("tmw", tag, chunk), fkey],
            writes=[("tmq", tag, chunk), ("wst",)], tag="tmq" + tag)
    for t in range(16):
        bank = t % 2
        for a in range(8):
            pb.emit("pe", lambda e, t=t, a=a, bank=bank: e.matmul(
                ps[bank][:, :], xln[a][:, t * 128:(t + 1) * 128],
                wstb[:, a * 512:(a + 1) * 512],
                start=(a == 0), stop=(a == 7)),
                reads=[("tmq", tag, a // 4), ("wst",), ("xln", a), ("A1",)],
                writes=[("ps", bank)], tag="tmmm" + tag)
        evac(t, bank)


def _phase_l1_attn(cx):
    pb, ps, dr = cx.pb, cx.ps, cx.dr
    mb = cx.sm[:, 0:2048]
    rb = cx.w2
    _ln_full(cx, cx.x_scr, lambda t: ("xscr", t), "la", mb, rb,
             [("sm",)], [("w2",)])
    xln = [cx.av(t * 4096, 2048, BF16) for t in range(8)]
    for t in range(8):
        buf = cx.w0 if t % 2 == 0 else cx.w1
        rkey = ("w0",) if t % 2 == 0 else ("w1",)
        bkey = ("lab", t % 2)
        pb.emit("sp", lambda e, t=t, buf=buf: e.dma_start(
            out=buf, in_=cx.x_scr[t * 128:(t + 1) * 128, :]),
            reads=[("xscr", t)], writes=[bkey, rkey], dma=True, tag="lald")
        eng = "dve" if t % 2 == 0 else "pool"
        pb.emit(eng, lambda e, buf=buf: e.tensor_tensor(
            buf, buf, mb, ALU.subtract),
            reads=[bkey, ("mb", "la"), ("sm",)], writes=[bkey, rkey],
            tag="lasub")
        pb.emit(eng, lambda e, t=t, buf=buf, rkey=rkey: e.tensor_tensor(
            xln[t], buf, rb, ALU.mult),
            reads=[bkey, rkey, ("rb", "la"), ("w2",)],
            writes=[("xln", t), ("A1",)], tag="laxq")
    pb.emit("sp", lambda e: e.dma_start(
        out=cx.dst[:, 0:2048],
        in_=dr["spanmask"].rearrange("p a c -> p (a c)")),
        writes=[("spm",), ("dstream", 0)], dma=True, tag="spmld")
    spm = lambda v: cx.dst[:, v * 512:(v + 1) * 512]
    pb.emit("sp", lambda e: e.dma_start(
        out=cx.cols[:, 32:40], in_=dr["l1_attn_bo"]),
        writes=[("bocol", "la"), ("cols",)], dma=True, tag="bola")

    qT = [cx.av(32768 + j * 4096, 2048, BF16) for j in range(4)]
    kT = [cx.av(49152 + j * 4096, 2048, BF16) for j in range(4)]
    ktok = [cx.av(65536 + t * 1024, 512, BF16) for t in range(16)]
    vaug = [cx.av(81920 + t * 1040, 520, BF16) for t in range(16)]
    Ssb = [cx.av(98560 + t * 1024, 512, BF16) for t in range(4)]
    oTs = [cx.av(114944 + j * 4096, 2048, BF16) for j in range(4)]
    ssb_pair = [cx.av(131328, 65, BF16, npart=64, p0=0),
                cx.av(131328, 65, BF16, npart=64, p0=64)]
    ssb32_pair = [cx.av(103680, 65, F32, npart=64, p0=0),
                  cx.av(103680, 65, F32, npart=64, p0=64)]
    drow = cx.av(132368, 512, F32, npart=1)
    osc = cx.av(102656, 512, BF16, npart=64)

    for half in range(2):
        dh0 = half * 512
        for pi, (wname, kind) in enumerate((("l1_attn_wq", "q"),
                                            ("l1_attn_wk", "k"))):
            for j in range(4):
                slot = cx.wst[:, (j % 4) * 1024:(j % 4) * 1024 + 1024]
                wkey = ("wsl", j % 4)
                buf = cx.w1 if j % 2 == 0 else cx.w0
                fkey = ("w1",) if j % 2 == 0 else ("w0",)
                pb.emit("sp" if j % 2 == 0 else "pool",
                        lambda e, wname=wname, j=j, dh0=dh0, buf=buf:
                        e.dma_start(out=buf[:, 0:1024],
                                    in_=dr[wname][dh0 // 128 + j]
                                    .rearrange("p a c -> p (a c)")),
                        writes=[("wf", j % 4), fkey], dma=True, tag="lwld")
                pb.emit("dve", lambda e, slot=slot, buf=buf: e.tensor_copy(
                    slot, buf[:, 0:1024]),
                    reads=[("wf", j % 4), fkey], writes=[wkey, ("wst",)],
                    tag="lwq")
                if kind in ("q", "k"):
                    tiles, tkey = (qT, "qT") if kind == "q" else (kT, "kTl1")
                    for a in range(8):
                        for s in range(4):
                            bank = (j % 2) * 4 + s
                            pb.emit("pe", lambda e, s=s, a=a, slot=slot,
                                    bank=bank:
                                    e.matmul(ps[bank][:, :],
                                             slot[:, a * 128:(a + 1) * 128],
                                             xln[a][:, s * 512:(s + 1) * 512],
                                             start=(a == 0), stop=(a == 7)),
                                    reads=[wkey, ("wst",), ("xln", a),
                                           ("A1",)],
                                    writes=[("ps", bank)], tag="prmm")
                    for s in range(4):
                        bank = (j % 2) * 4 + s
                        pb.emit("dve", lambda e, j=j, s=s, tiles=tiles,
                                bank=bank:
                                e.tensor_scalar(
                                    tiles[j][:, s * 512:(s + 1) * 512],
                                    ps[bank][:, :], 0.0, 0.001, ALU.max,
                                    ALU.add),
                                reads=[("ps", bank)],
                                writes=[(tkey, j), ("A2",)], tag="prev")
        def vevac(t, bank):
            pb.emit("dve", lambda e, t=t, bank=bank: e.tensor_copy(
                vaug[t].rearrange("p (h c) -> p h c", c=65)[:, :, 0:64],
                ps[bank][:, :].rearrange("p (h c) -> p h c", c=64)),
                reads=[("ps", bank)],
                writes=[("va", t), ("B1",), ("B2",)], tag="vev")
        _tokmaj_proj(cx, "l1_attn_wv2", half, xln, vevac, "av%d" % half)

        def ktevac(t, bank):
            pb.emit("dve", lambda e, t=t, bank=bank: e.tensor_scalar(
                ktok[t], ps[bank][:, :], 0.0, 0.001, ALU.max, ALU.add),
                reads=[("ps", bank)], writes=[("kk", t), ("B1",)],
                tag="kkev")
        _tokmaj_proj(cx, "l1_attn_wk2", half, xln, ktevac, "ak%d" % half)
        for t in range(16):
            pb.emit("dve", lambda e, t=t: e.memset(
                vaug[t].rearrange("p (h c) -> p h c", c=65)[:, :, 64:65], 1.0),
                writes=[("va", t), ("B1",), ("B2",)], tag="vone")
        for hl in range(8):
            hrow = (hl % 2) * 64
            jt = hl // 2
            ssb = ssb_pair[hl % 2]
            ssb32 = ssb32_pair[hl % 2]
            spsum = ps[6][hrow:hrow + 64, 0:65]
            for s in range(4):
                tisl = slice(s * 512, (s + 1) * 512)
                for v in range(4):
                    tj = 4 * s + v
                    bank = v % 3
                    pb.emit("pe", lambda e, tj=tj, v=v, bank=bank, jt=jt,
                            hrow=hrow, tisl=tisl: e.matmul(
                                ps[bank][:, :],
                                kT[jt][hrow:hrow + 64,
                                       tj * 128:(tj + 1) * 128],
                                qT[jt][hrow:hrow + 64, tisl],
                                start=True, stop=True),
                            reads=[("kTl1", jt), ("qT", jt), ("A2",)],
                            writes=[("ps", bank)], tag="smm")
                    pb.emit("dve", lambda e, v=v, bank=bank: e.tensor_tensor(
                        Ssb[v], ps[bank][:, :], spm(v), ALU.mult),
                        reads=[("ps", bank), ("spm",), ("dstream", 0)],
                        writes=[("Ssb", v), ("B2",)], tag="sevm")
                first = True
                if s > 0:
                    pb.emit("pe", lambda e, jt=jt, hrow=hrow, tisl=tisl,
                            ssb=ssb: e.matmul(
                                ps[4][0:65, :], ssb,
                                qT[jt][hrow:hrow + 64, tisl],
                                start=True, stop=False),
                            reads=[("ssb",), ("qT", jt), ("A2",), ("B2",)],
                            writes=[("ps", 4)], tag="ocross")
                    first = False
                for v in range(4):
                    tj = 4 * s + v
                    pb.emit("pe", lambda e, tj=tj, v=v, hl=hl,
                            st=(first and v == 0), sp_=(v == 3): e.matmul(
                                ps[4][0:65, :],
                                vaug[tj][:, hl * 65:(hl + 1) * 65],
                                Ssb[v], start=st, stop=sp_),
                            reads=[("va", tj), ("Ssb", v), ("B2",)],
                            writes=[("ps", 4)], tag="omm")
                pb.emit("dve", lambda e: e.tensor_copy(drow, ps[4][64:65, :]),
                        reads=[("ps", 4)], writes=[("drow",), ("B2",)],
                        tag="dev")
                pb.emit("act", lambda e: e.activation(drow, drow, AF.Ln),
                        reads=[("drow",), ("B2",)],
                        writes=[("drow",), ("B2",)], tag="dln")
                pb.emit("act", lambda e: e.activation(drow, drow, AF.Exp,
                                                      scale=-1.0),
                        reads=[("drow",), ("B2",)],
                        writes=[("drow",), ("B2",)], tag="dexp")
                pb.emit("pe", lambda e: e.matmul(
                    ps[5][:, :], cx.ones_row_f32, drow, start=True,
                    stop=True),
                    reads=[("drow",), ("B2",), ("c", "orf")],
                    writes=[("ps", 5)], tag="dbc")
                pb.emit("dve", lambda e: e.tensor_copy(
                    cx.w0b[0:64, 0:512], ps[4][0:64, :]),
                    reads=[("ps", 4)], writes=[("oev",), ("w0",)], tag="oev")
                if hl % 2 == 0:
                    pb.emit("dve", lambda e, jt=jt, tisl=tisl:
                            e.tensor_tensor(oTs[jt][0:64, tisl],
                                            cx.w0b[0:64, 0:512],
                                            ps[5][0:64, :], ALU.mult),
                            reads=[("oev",), ("w0",), ("ps", 5)],
                            writes=[("oTs", jt), ("B2",)], tag="oscl")
                else:
                    pb.emit("dve", lambda e: e.tensor_tensor(
                        osc, cx.w0b[0:64, 0:512], ps[5][0:64, :], ALU.mult),
                        reads=[("oev",), ("w0",), ("ps", 5)],
                        writes=[("osc",), ("B2",)], tag="oscl2")
                    pb.emit("pe", lambda e: e.matmul(
                        ps[3][:, :], cx.shift64_bf, osc, start=True,
                        stop=True),
                        reads=[("osc",), ("B2",), ("c", "sh64")],
                        writes=[("ps", 3)], tag="oshift")
                    pb.emit("dve", lambda e, jt=jt, tisl=tisl: e.tensor_copy(
                        oTs[jt][64:128, tisl], ps[3][64:128, :]),
                        reads=[("ps", 3)], writes=[("oTs", jt), ("B2",)],
                        tag="osev")
                for c in range(4):
                    tj = 4 * s + c
                    pb.emit("pe", lambda e, tj=tj, hl=hl, st=(c == 0),
                            sp_=(c == 3), spsum=spsum: e.matmul(
                                spsum,
                                ktok[tj][:, hl * 64:(hl + 1) * 64],
                                vaug[tj][:, hl * 65:(hl + 1) * 65],
                                start=st, stop=sp_),
                            reads=[("kk", tj), ("va", tj), ("B1",), ("B2",)],
                            writes=[("ps", 6)], tag="stup")
                if s < 3:
                    if s == 0:
                        pb.emit("dve", lambda e, ssb32=ssb32, spsum=spsum:
                                e.tensor_copy(ssb32, spsum),
                                reads=[("ps", 6)],
                                writes=[("ssb32",), ("B2",)], tag="st32i")
                    else:
                        pb.emit("dve", lambda e, ssb32=ssb32, spsum=spsum:
                                e.tensor_tensor(ssb32, ssb32, spsum, ALU.add),
                                reads=[("ps", 6), ("ssb32",)],
                                writes=[("ssb32",), ("B2",)], tag="st32a")
                    pb.emit("dve", lambda e, ssb=ssb, ssb32=ssb32:
                            e.tensor_copy(ssb, ssb32),
                            reads=[("ssb32",), ("B2",)],
                            writes=[("ssb",), ("B2",)], tag="stev")
        for dm in range(8):
            slot = cx.wst[:, 2048:2048 + 512]
            wkey = ("wslo",)
            pb.emit("sp", lambda e, dm=dm, dh0=dh0: e.dma_start(
                out=cx.w1[:, 0:512],
                in_=dr["l1_attn_wo"][dh0 // 512, dm].rearrange(
                    "p a c -> p (a c)")),
                writes=[("wfo",), ("w1",)], dma=True, tag="wold2")
            pb.emit("dve", lambda e, slot=slot: e.tensor_copy(
                slot, cx.w1[:, 0:512]),
                reads=[("wfo",), ("w1",)], writes=[wkey, ("wst",)], tag="woq2")
            pb.emit("sp", lambda e, dm=dm: e.dma_start(
                out=cx.w0, in_=cx.x_scr[dm * 128:(dm + 1) * 128, :]),
                reads=[("xscr", dm)], writes=[("lax",), ("w0",)], dma=True,
                tag="laxld")
            for j in range(4):
                for s in range(4):
                    pb.emit("pe", lambda e, s=s, j=j, slot=slot: e.matmul(
                        ps[s][:, :], slot[:, j * 128:(j + 1) * 128],
                        oTs[j][:, s * 512:(s + 1) * 512],
                        start=(j == 0), stop=(j == 3)),
                        reads=[wkey, ("wst",), ("oTs", j), ("B2",)],
                        writes=[("ps", s)], tag="wo2mm")
            for s in range(4):
                pb.emit("dve", lambda e, s=s: e.tensor_tensor(
                    cx.w0[:, s * 512:(s + 1) * 512],
                    cx.w0[:, s * 512:(s + 1) * 512], ps[s][:, :], ALU.add),
                    reads=[("ps", s), ("lax",)], writes=[("lax",), ("w0",)],
                    tag="laacc")
            if half == 0:
                pb.emit("dve", lambda e, dm=dm: e.tensor_scalar(
                    cx.w0, cx.w0, cx.cols[:, 32 + dm:33 + dm], None, ALU.add),
                    reads=[("lax",), ("bocol", "la"), ("cols",), ("w0",)],
                    writes=[("lax",), ("w0",)], tag="laboadd")
            pb.emit("pool", lambda e, dm=dm: e.dma_start(
                out=cx.x_scr[dm * 128:(dm + 1) * 128, :], in_=cx.w0),
                reads=[("lax",), ("w0",)], writes=[("xscr", dm)], dma=True,
                tag="laxst")


def _phase_l1_ffn(cx):
    pb, ps, dr = cx.pb, cx.ps, cx.dr
    mb = cx.sm[:, 0:2048]
    rb = cx.w2
    _ln_full(cx, cx.x_scr, lambda t: ("xscr", t), "lf", mb, rb,
             [("sm",)], [("w2",)])
    xln = [cx.av(t * 4096, 2048, BF16) for t in range(8)]
    for t in range(8):
        buf = cx.w0 if t % 2 == 0 else cx.w1
        rkey = ("w0",) if t % 2 == 0 else ("w1",)
        bkey = ("lfb", t % 2)
        pb.emit("sp", lambda e, t=t, buf=buf: e.dma_start(
            out=buf, in_=cx.x_scr[t * 128:(t + 1) * 128, :]),
            reads=[("xscr", t)], writes=[bkey, rkey], dma=True, tag="lfld")
        eng = "dve" if t % 2 == 0 else "pool"
        pb.emit(eng, lambda e, buf=buf: e.tensor_tensor(
            buf, buf, mb, ALU.subtract),
            reads=[bkey, ("mb", "lf"), ("sm",)], writes=[bkey, rkey],
            tag="lfsub")
        pb.emit(eng, lambda e, t=t, buf=buf, rkey=rkey: e.tensor_tensor(
            xln[t], buf, rb, ALU.mult),
            reads=[bkey, rkey, ("rb", "lf"), ("w2",)],
            writes=[("xln", t), ("A1",)], tag="lfxq")
    rw = cx.rw
    gate = rw[0:1, :]
    gm = cx.av(99584, 2048, F32, npart=1)
    trow = cx.cf32[0:1, 260:276]
    pb.emit("sp", lambda e: e.dma_start(
        out=cx.cols[:, 40:48], in_=dr["l1_ffn_wg"]),
        writes=[("wg",), ("cols",)], dma=True, tag="wgld")
    wgb = cx.cbf[:, 388:396]
    pb.emit("dve", lambda e: e.tensor_copy(wgb, cx.cols[:, 40:48]),
            reads=[("wg",), ("cols",)], writes=[("wgb",)], tag="wgq")
    for s in range(4):
        for a in range(8):
            pb.emit("pe", lambda e, s=s, a=a: e.matmul(
                ps[7][0:1, 0:512], wgb[:, a:a + 1],
                xln[a][:, s * 512:(s + 1) * 512], start=(a == 0),
                stop=(a == 7)),
                reads=[("wgb",), ("xln", a), ("A1",)],
                writes=[("ps", 7)], tag="gmm")
        pb.emit("dve", lambda e, s=s: e.tensor_copy(
            gate[:, s * 512:(s + 1) * 512], ps[7][0:1, 0:512]),
            reads=[("ps", 7)], writes=[("gate",), ("rw",)], tag="gev")
    pb.emit("act", lambda e: e.activation(gate, gate, AF.Sigmoid),
            reads=[("gate",), ("rw",)], writes=[("gate",), ("rw",)],
            tag="gsig")
    gcols = cx.cols[:, 72:88]
    for t in range(16):
        pb.emit("pe", lambda e, t=t: e.transpose(
            ps[6][:, 0:1], gate[:, t * 128:(t + 1) * 128],
            cx.ones_col_f32[0:1, 0:1]),
            reads=[("gate",), ("rw",), ("c", "ocf")], writes=[("ps", 6)],
            tag="gtr")
        pb.emit("dve", lambda e, t=t: e.tensor_copy(
            gcols[:, t:t + 1], ps[6][:, 0:1]),
            reads=[("ps", 6)], writes=[("gcols",), ("cols",)], tag="gcev")
    tbc = cx.cols[:, 88:89]
    cntc = cx.cols[:, 89:90]
    cmpt = cx.w1[:, 0:16]
    pb.emit("dve", lambda e: e.memset(trow[:, 0:1], 0.0),
            writes=[("bis",), ("cf32",)], tag="bi0")
    pb.emit("dve", lambda e: e.memset(trow[:, 1:2], 1.0),
            writes=[("bis",), ("cf32",)], tag="bi1")
    for it in range(24):
        pb.emit("dve", lambda e: e.tensor_tensor(
            trow[:, 2:3], trow[:, 0:1], trow[:, 1:2], ALU.add),
            reads=[("bis",)], writes=[("bis",), ("cf32",)], tag="bt1")
        pb.emit("dve", lambda e: e.tensor_single_scalar(
            trow[:, 2:3], trow[:, 2:3], 0.5, ALU.mult),
            reads=[("bis",)], writes=[("bis",), ("cf32",)], tag="bt2")
        pb.emit("pe", lambda e: e.matmul(
            ps[7][:, 0:1], cx.ones_row_f32, trow[:, 2:3], start=True,
            stop=True),
            reads=[("bis",), ("cf32",), ("c", "orf")], writes=[("ps", 7)],
            tag="btb")
        pb.emit("dve", lambda e: e.tensor_copy(tbc, ps[7][:, 0:1]),
                reads=[("ps", 7)], writes=[("tbc",), ("cols",)], tag="btc")
        pb.emit("dve", lambda e: e.tensor_single_scalar(
            cmpt, gcols, tbc, ALU.is_gt),
            reads=[("gcols",), ("tbc",), ("cols",)],
            writes=[("cmpt",), ("w1",)], tag="bt3")
        pb.emit("dve", lambda e: e.reduce_sum(
            cntc, cmpt, axis=mybir.AxisListType.X),
            reads=[("cmpt",), ("w1",)], writes=[("cntc",), ("cols",)],
            tag="bt4")
        pb.emit("pe", lambda e: e.matmul(
            ps[7][0:1, 0:1], cx.ones_col_f32, cntc, start=True, stop=True),
            reads=[("cntc",), ("cols",), ("c", "ocf")], writes=[("ps", 7)],
            tag="btm")
        pb.emit("dve", lambda e: e.tensor_copy(trow[:, 3:4], ps[7][0:1, 0:1]),
                reads=[("ps", 7)], writes=[("bis",), ("cf32",)], tag="btcnt")
        pb.emit("dve", lambda e: e.tensor_single_scalar(
            trow[:, 4:5], trow[:, 3:4], 1024.0, ALU.is_ge),
            reads=[("bis",)], writes=[("bis",), ("cf32",)], tag="bt5")
        pb.emit("dve", lambda e: e.tensor_tensor(
            trow[:, 5:6], trow[:, 2:3], trow[:, 0:1], ALU.subtract),
            reads=[("bis",)], writes=[("bis",), ("cf32",)], tag="bt6")
        pb.emit("dve", lambda e: e.tensor_tensor(
            trow[:, 5:6], trow[:, 5:6], trow[:, 4:5], ALU.mult),
            reads=[("bis",)], writes=[("bis",), ("cf32",)], tag="bt7")
        pb.emit("dve", lambda e: e.tensor_tensor(
            trow[:, 0:1], trow[:, 0:1], trow[:, 5:6], ALU.add),
            reads=[("bis",)], writes=[("bis",), ("cf32",)], tag="bt8")
        pb.emit("dve", lambda e: e.tensor_tensor(
            trow[:, 5:6], trow[:, 1:2], trow[:, 2:3], ALU.subtract),
            reads=[("bis",)], writes=[("bis",), ("cf32",)], tag="bt9")
        pb.emit("dve", lambda e: e.tensor_single_scalar(
            trow[:, 6:7], trow[:, 4:5], -1.0, ALU.mult),
            reads=[("bis",)], writes=[("bis",), ("cf32",)], tag="bt10")
        pb.emit("dve", lambda e: e.tensor_single_scalar(
            trow[:, 6:7], trow[:, 6:7], 1.0, ALU.add),
            reads=[("bis",)], writes=[("bis",), ("cf32",)], tag="bt11")
        pb.emit("dve", lambda e: e.tensor_tensor(
            trow[:, 5:6], trow[:, 5:6], trow[:, 6:7], ALU.mult),
            reads=[("bis",)], writes=[("bis",), ("cf32",)], tag="bt12")
        pb.emit("dve", lambda e: e.tensor_tensor(
            trow[:, 1:2], trow[:, 1:2], trow[:, 5:6], ALU.subtract),
            reads=[("bis",)], writes=[("bis",), ("cf32",)], tag="bt13")
    tmp = cx.av(107776, 2048, F32, npart=1)
    pb.emit("dve", lambda e: e.tensor_single_scalar(
        tmp, gate, trow[:, 0:1], ALU.is_gt),
        reads=[("gate",), ("bis",)], writes=[("mrow",), ("rw",)], tag="mrow")
    pb.emit("dve", lambda e: e.tensor_tensor(gm, gate, tmp, ALU.mult),
            reads=[("gate",), ("mrow",)], writes=[("gm",), ("rw",)],
            tag="gmrow")
    _tap(cx, "gate", gate, (1, 2048), F32, [("gate",), ("rw",)])
    _tap(cx, "maskrow", tmp, (1, 2048), F32, [("mrow",)])
    _tap(cx, "gmrow", gm, (1, 2048), F32, [("gm",)])
    for t in range(16):
        pb.emit("pe", lambda e, t=t: e.transpose(
            ps[6][:, 0:1].bitcast(F32) if False else ps[6][:, 0:1],
            tmp[:, t * 128:(t + 1) * 128], cx.ones_col_f32[0:1, 0:1]),
            reads=[("mrow",), ("rw",), ("c", "ocf")], writes=[("ps", 6)],
            tag="mtr")
        pb.emit("dve", lambda e, t=t: e.tensor_copy(
            cx.cols[:, 48 + t:49 + t], ps[6][:, 0:1]),
            reads=[("ps", 6)], writes=[("mcol", t), ("cols",)], tag="mcev")

    _tap(cx, "mcols", cx.cols[:, 48:64], (128, 16), F32,
         [("mcol", t) for t in range(16)] + [("cols",)])
    qT = [cx.av(32768 + j * 4096, 2048, BF16) for j in range(4)]
    kT = [cx.av(49152 + j * 4096, 2048, BF16) for j in range(4)]
    vaug = [cx.av(65536 + t * 1040, 520, BF16) for t in range(16)]
    ST = [cx.av(82176 + t * 1024, 512, BF16) for t in range(16)]
    oTs = [cx.av(115968 + j * 4096, 2048, BF16) for j in range(4)]
    drow = cx.av(132352, 512, F32, npart=1)
    osc = cx.av(98560, 512, BF16, npart=64)

    for half in range(2):
        dh0 = half * 512
        for pi, (wname, kind) in enumerate((("l1_ffn_wq", "q"),
                                            ("l1_ffn_wk", "k"))):
            for j in range(4):
                slot = cx.wst[:, (j % 4) * 1024:(j % 4) * 1024 + 1024]
                wkey = ("wslf", j % 4)
                buf = cx.w1 if j % 2 == 0 else cx.w0
                fkey = ("w1",) if j % 2 == 0 else ("w0",)
                pb.emit("sp" if j % 2 == 0 else "pool",
                        lambda e, wname=wname, j=j, dh0=dh0, buf=buf:
                        e.dma_start(out=buf[:, 0:1024],
                                    in_=dr[wname][dh0 // 128 + j]
                                    .rearrange("p a c -> p (a c)")),
                        writes=[("wff", j % 4), fkey], dma=True, tag="fwld")
                pb.emit("dve", lambda e, slot=slot, buf=buf: e.tensor_copy(
                    slot, buf[:, 0:1024]),
                    reads=[("wff", j % 4), fkey], writes=[wkey, ("wst",)],
                    tag="fwq")
                if kind in ("q", "k"):
                    tiles, tkey = (qT, "qTf") if kind == "q" else (kT, "kTf")
                    scl = 1.0 if kind == "q" else 0.125
                    for a in range(8):
                        for s in range(4):
                            bank = (j % 2) * 4 + s
                            pb.emit("pe", lambda e, s=s, a=a, slot=slot,
                                    bank=bank:
                                    e.matmul(ps[bank][:, :],
                                             slot[:, a * 128:(a + 1) * 128],
                                             xln[a][:, s * 512:(s + 1) * 512],
                                             start=(a == 0), stop=(a == 7)),
                                    reads=[wkey, ("wst",), ("xln", a),
                                           ("A1",)],
                                    writes=[("ps", bank)], tag="fprmm")
                    for s in range(4):
                        bank = (j % 2) * 4 + s
                        pb.emit("dve", lambda e, j=j, s=s, tiles=tiles,
                                scl=scl, bank=bank: e.tensor_single_scalar(
                                    tiles[j][:, s * 512:(s + 1) * 512],
                                    ps[bank][:, :], scl, ALU.mult),
                                reads=[("ps", bank)],
                                writes=[(tkey, j), ("A2",)], tag="fprev")
        def fvevac(t, bank):
            pb.emit("dve", lambda e, t=t, bank=bank: e.tensor_scalar(
                vaug[t].rearrange("p (h c) -> p h c", c=65)[:, :, 0:64],
                ps[bank][:, :].rearrange("p (h c) -> p h c", c=64),
                cx.cols[:, 48 + t:49 + t], None, ALU.mult),
                reads=[("ps", bank), ("mcol", t), ("cols",)],
                writes=[("vaf", t), ("B1",), ("B2",)], tag="fvev")
        _tokmaj_proj(cx, "l1_ffn_wv2", half, xln, fvevac, "fv%d" % half)
        for t in range(16):
            pb.emit("dve", lambda e, t=t: e.tensor_copy(
                vaug[t].rearrange("p (h c) -> p h c", c=65)[:, :, 64:65],
                cx.cols[:, 48 + t:49 + t].unsqueeze(2).broadcast_to(
                    (128, 1, 8)).rearrange("p one h -> p h one")
                if False else
                cx.cols[:, 48 + t:49 + t].unsqueeze(1).broadcast_to(
                    (128, 8, 1))),
                reads=[("mcol", t), ("cols",)],
                writes=[("vaf", t), ("B1",), ("B2",)], tag="fvone")
        iters = []
        for hl in range(8):
            hrow = (hl % 2) * 64
            jt = hl // 2
            for s in range(4):
                tisl = slice(s * 512, (s + 1) * 512)
                A, B, C = [], [], []
                emA = lambda *a, **k: A.append((a, k))
                emB = lambda *a, **k: B.append((a, k))
                emC = lambda *a, **k: C.append((a, k))
                for tj in range(16):
                    bank = tj % 3
                    emA("pe", lambda e, tj=tj, bank=bank, jt=jt,
                        hrow=hrow, tisl=tisl: e.matmul(
                            ps[bank][:, :],
                            kT[jt][hrow:hrow + 64,
                                   tj * 128:(tj + 1) * 128],
                            qT[jt][hrow:hrow + 64, tisl],
                            start=True, stop=True),
                        reads=[("kTf", jt), ("qTf", jt), ("A2",)],
                        writes=[("ps", bank)], tag="fsmm")
                    emA("act", lambda e, tj=tj, bank=bank: e.activation(
                        ST[tj], ps[bank][:, :], AF.Exp),
                        reads=[("ps", bank)],
                        writes=[("ST", tj), ("B1",), ("B2",)], tag="fexp")
                for tj in range(16):
                    emB("pe", lambda e, tj=tj, hl=hl: e.matmul(
                        ps[4][0:65, :], vaug[tj][:, hl * 65:(hl + 1) * 65],
                        ST[tj], start=(tj == 0), stop=(tj == 15)),
                        reads=[("vaf", tj), ("ST", tj), ("B1",), ("B2",)],
                        writes=[("ps", 4)], tag="fomm")
                emC("dve", lambda e: e.tensor_copy(drow, ps[4][64:65, :]),
                    reads=[("ps", 4)], writes=[("drow",), ("B2",)],
                    tag="fdev")
                emC("act", lambda e: e.activation(drow, drow, AF.Ln),
                    reads=[("drow",), ("B2",)],
                    writes=[("drow",), ("B2",)], tag="fdln")
                emC("act", lambda e: e.activation(drow, drow, AF.Exp,
                                                  scale=-1.0),
                    reads=[("drow",), ("B2",)],
                    writes=[("drow",), ("B2",)], tag="fdexp")
                emC("dve", lambda e, tisl=tisl: e.tensor_tensor(
                    drow, drow, gm[:, tisl], ALU.mult),
                    reads=[("drow",), ("gm",), ("rw",), ("B2",)],
                    writes=[("drow",), ("B2",)], tag="fdgm")
                emC("pe", lambda e: e.matmul(
                    ps[5][:, :], cx.ones_row_f32, drow, start=True,
                    stop=True),
                    reads=[("drow",), ("B2",), ("c", "orf")],
                    writes=[("ps", 5)], tag="fdbc")
                emC("dve", lambda e: e.tensor_copy(
                    cx.w0b[0:64, 0:512], ps[4][0:64, :]),
                    reads=[("ps", 4)], writes=[("foev",), ("w0",)],
                    tag="foev")
                if hl % 2 == 0:
                    emC("dve", lambda e, jt=jt, tisl=tisl:
                        e.tensor_tensor(oTs[jt][0:64, tisl],
                                        cx.w0b[0:64, 0:512],
                                        ps[5][0:64, :], ALU.mult),
                        reads=[("foev",), ("w0",), ("ps", 5)],
                        writes=[("oTf", jt), ("B2",)], tag="foscl")
                else:
                    emC("dve", lambda e: e.tensor_tensor(
                        osc, cx.w0b[0:64, 0:512], ps[5][0:64, :], ALU.mult),
                        reads=[("foev",), ("w0",), ("ps", 5)],
                        writes=[("oscf",), ("B2",)], tag="foscl2")
                    emC("pe", lambda e: e.matmul(
                        ps[3][:, :], cx.shift64_bf, osc, start=True,
                        stop=True),
                        reads=[("oscf",), ("B2",), ("c", "sh64")],
                        writes=[("ps", 3)], tag="foshift")
                    emC("dve", lambda e, jt=jt, tisl=tisl: e.tensor_copy(
                        oTs[jt][64:128, tisl], ps[3][64:128, :]),
                        reads=[("ps", 3)], writes=[("oTf", jt), ("B2",)],
                        tag="fosev")
                iters.append((A, B, C))
        prevC = None
        for A, B, C in iters:
            for a, k in A:
                pb.emit(*a, **k)
            if prevC is not None:
                for a, k in prevC:
                    pb.emit(*a, **k)
            for a, k in B:
                pb.emit(*a, **k)
            prevC = C
        for a, k in prevC:
            pb.emit(*a, **k)
        for dm in range(8):
            slot = cx.wst[:, 2048:2048 + 512]
            wkey = ("wslof",)
            pb.emit("sp", lambda e, dm=dm, dh0=dh0: e.dma_start(
                out=cx.w1[:, 0:512],
                in_=dr["l1_ffn_wo"][dh0 // 512, dm].rearrange(
                    "p a c -> p (a c)")),
                writes=[("wfof",), ("w1",)], dma=True, tag="fwold")
            pb.emit("dve", lambda e, slot=slot: e.tensor_copy(
                slot, cx.w1[:, 0:512]),
                reads=[("wfof",), ("w1",)], writes=[wkey, ("wst",)],
                tag="fwoq")
            pb.emit("sp", lambda e, dm=dm: e.dma_start(
                out=cx.w0, in_=cx.x_scr[dm * 128:(dm + 1) * 128, :]),
                reads=[("xscr", dm)], writes=[("lfx",), ("w0",)], dma=True,
                tag="fxrld2")
            for j in range(4):
                for s in range(4):
                    pb.emit("pe", lambda e, s=s, j=j, slot=slot: e.matmul(
                        ps[s][:, :], slot[:, j * 128:(j + 1) * 128],
                        oTs[j][:, s * 512:(s + 1) * 512],
                        start=(j == 0), stop=(j == 3)),
                        reads=[wkey, ("wst",), ("oTf", j), ("B2",)],
                        writes=[("ps", s)], tag="fwomm")
            for s in range(4):
                pb.emit("dve", lambda e, s=s: e.tensor_tensor(
                    cx.w0[:, s * 512:(s + 1) * 512],
                    cx.w0[:, s * 512:(s + 1) * 512], ps[s][:, :], ALU.add),
                    reads=[("ps", s), ("lfx",)], writes=[("lfx",), ("w0",)],
                    tag="flacc")
            pb.emit("pool", lambda e, dm=dm: e.dma_start(
                out=cx.x_scr[dm * 128:(dm + 1) * 128, :], in_=cx.w0),
                reads=[("lfx",), ("w0",)], writes=[("xscr", dm)], dma=True,
                tag="fxst2")



def kernel(**inputs):
    per_core = _host_prep(inputs)
    nc, cx = build_program()
    res = run_bass_kernel_spmd(nc, per_core, core_ids=list(range(NCORES)))
    outs = [np.asarray(r["out_fm"]).T for r in res.results]
    return np.stack(outs, axis=0).astype(np.float32)
